# version 09
# baseline (speedup 1.0000x reference)
import contextlib
import math
import numpy as np
import concourse.bass as bass
import concourse.mybir as mybir
from concourse.bass_utils import run_bass_kernel_spmd

F32 = mybir.dt.float32
BF16 = mybir.dt.bfloat16
AF = mybir.ActivationFunctionType
ALU = mybir.AluOpType

B, S, D = 4, 4096, 1024
NCORES = 8
RPC = S // 2
NCH = D // 128
TT = 512
NTT = RPC // TT

_progs = {}


class PB:

    def __init__(self, name):
        self.name = name
        self.nc = bass.Bass("TRN2", target_bir_lowering=False)
        self.ctx = contextlib.ExitStack()
        self.steps = []
        self._uid = 0
        self.io_map = None
        self._stage_ctx = None
        self._stage_pref = ""

    def uid(self):
        self._uid += 1
        return str(self._uid)

    def constcol(self, value, p=128):
        if not hasattr(self, "_consts") or self._consts is None:
            self._consts = {}
        if value not in self._consts:
            t = self.sb(f"const_{len(self._consts)}", [128, 1], F32)
            self.memset(t[:], float(value))
            self._consts[value] = t
        return self._consts[value][0:p, 0:1]

    def inp(self, name, shape, dt=F32):
        if self.io_map is not None:
            ap = self.io_map[name]
            assert tuple(ap.shape) == tuple(shape), (name, ap.shape, shape)
            return ap
        return self.nc.dram_tensor(name, list(shape), dt, kind="ExternalInput").ap()

    def out(self, name, shape, dt=F32):
        if self.io_map is not None:
            ap = self.io_map[name]
            assert tuple(ap.shape) == tuple(shape), (name, ap.shape, shape)
            return ap
        return self.nc.dram_tensor(name, list(shape), dt, kind="ExternalOutput").ap()

    @contextlib.contextmanager
    def stage(self, pref, io_map):
        old_ctx, old_map, old_pref = self._stage_ctx, self.io_map, self._stage_pref
        self._stage_ctx = contextlib.ExitStack()
        self.io_map = io_map
        self._stage_pref = pref + "_"
        self._consts = {}
        try:
            yield self
        finally:
            self._stage_ctx.close()
            self._stage_ctx, self.io_map, self._stage_pref = old_ctx, old_map, old_pref
            self._consts = {}

    def dram(self, name, shape, dt=F32):
        return self.nc.dram_tensor(name, list(shape), dt).ap()

    def sb(self, name, shape, dt=F32):
        ctx = self._stage_ctx if self._stage_ctx is not None else self.ctx
        return ctx.enter_context(
            self.nc.sbuf_tensor(self._stage_pref + name, list(shape), dt))

    def ps(self, name, shape, dt=F32):
        ctx = self._stage_ctx if self._stage_ctx is not None else self.ctx
        return ctx.enter_context(
            self.nc.psum_tensor(self._stage_pref + name, list(shape), dt))

    def add(self, eng, fn, dma=False):
        self.steps.append((eng, fn, 16 if dma else 1))

    def dma(self, out, in_, eng="sync", allow_slow=False):
        self.add(eng, lambda e, o=out, i=in_, asl=allow_slow:
                 e.dma_start(out=o, in_=i, allow_slow_non_contiguous=asl), dma=True)

    def cast_dma(self, out, in_):
        self.add("gpsimd", lambda e, o=out, i=in_: e.dma_start(out=o, in_=i), dma=True)

    def mm(self, out, lhsT, rhs, start=True, stop=True, tile_position=None):
        self.add("tensor", lambda e, o=out, l=lhsT, r=rhs, s1=start, s2=stop,
                 tp=tile_position:
                 e.matmul(o, l, r, start=s1, stop=s2, tile_position=tp))

    def act(self, out, in_, func, bias=None, scale=None, accum_out=None, eng="scalar"):
        kw = {}
        if bias is not None:
            kw["bias"] = bias
        if scale is not None:
            kw["scale"] = scale
        if accum_out is not None:
            kw["accum_out"] = accum_out
        self.add(eng, lambda e, o=out, i=in_, f=func, kw=kw:
                 e.activation(out=o, in_=i, func=f, **kw))

    def tt(self, out, a, b, op, eng="vector"):
        self.add(eng, lambda e, o=out, x=a, y=b, p=op:
                 e.tensor_tensor(out=o, in0=x, in1=y, op=p))

    def ts(self, out, in_, s1, op0, s2=None, op1=None, accum_out=None, eng="vector"):
        kw = {}
        if accum_out is not None:
            kw["accum_out"] = accum_out
            if op1 is None:
                s2, op1 = 0.0, ALU.add
        self.add(eng, lambda e, o=out, i=in_, a=s1, b=s2, p0=op0, p1=op1, kw=kw:
                 e.tensor_scalar(out=o, in0=i, scalar1=a, scalar2=b, op0=p0,
                                 **({"op1": p1} if p1 is not None else {}), **kw))

    def stt(self, out, in0, scalar, in1, op0, op1, eng="vector"):
        self.add(eng, lambda e, o=out, i0=in0, s=scalar, i1=in1, p0=op0, p1=op1:
                 e.scalar_tensor_tensor(out=o, in0=i0, scalar=s, in1=i1, op0=p0, op1=p1))

    def copy(self, out, in_, eng="vector"):
        self.add(eng, lambda e, o=out, i=in_: e.tensor_copy(out=o, in_=i))

    def scan(self, out, data0, data1, initial, op0=ALU.mult, op1=ALU.add):
        self.add("vector", lambda e, o=out, d0=data0, d1=data1, ini=initial, p0=op0, p1=op1:
                 e.tensor_tensor_scan(out=o, data0=d0, data1=d1, initial=ini, op0=p0, op1=p1))

    def memset(self, ap, v, eng="gpsimd"):
        self.add(eng, lambda e, a=ap, vv=v: e.memset(a, vv))

    def reduce(self, out, in_, op, eng="vector"):
        self.add(eng, lambda e, o=out, i=in_, p=op:
                 e.tensor_reduce(out=o, in_=i, axis=mybir.AxisListType.X, op=p))

    def recip(self, out, in_):
        self.add("vector", lambda e, o=out, i=in_: e.reciprocal(out=o, in_=i))

    def transpose(self, out_psum, in_, ident):
        self.add("tensor", lambda e, o=out_psum, i=in_, d=ident: e.transpose(o, i, d))

    def affine_select(self, out, in_, pattern, compare_op, fill, channel_multiplier=1,
                      base=None):
        self.add("gpsimd", lambda e, o=out, i=in_, p=pattern, c=compare_op, f=fill,
                 m=channel_multiplier, b=base:
                 e.affine_select(out=o, in_=i, pattern=p, compare_op=c, fill=f,
                                 channel_multiplier=m, base=b))

    def iota(self, out, pattern, base=0, channel_multiplier=0):
        self.add("gpsimd", lambda e, o=out, p=pattern, b=base, m=channel_multiplier:
                 e.iota(out=o, pattern=p, base=b, channel_multiplier=m,
                        allow_small_or_imprecise_dtypes=True))

    def collective(self, kind, op, groups, ins, outs):
        self.steps.append(("gpsimd",
                           lambda e, k=kind, o=op, g=groups, i=ins, u=outs:
                           e.collective_compute(k, o, replica_groups=g, ins=i, outs=u),
                           1))

    def finish(self):
        nc = self.nc
        sem_c = self.ctx.enter_context(nc.semaphore("semc"))
        sem_d = self.ctx.enter_context(nc.semaphore("semd"))
        block = self.ctx.enter_context(nc.Block())
        steps = self.steps
        for eng_name in ("sync", "scalar", "vector", "tensor", "gpsimd"):
            def make(eng_name):
                def body(eng):
                    c_before = d_before = 0
                    for (e, fn, inc) in steps:
                        if e == eng_name:
                            if c_before:
                                eng.wait_ge(sem_c, c_before)
                            if d_before:
                                eng.wait_ge(sem_d, d_before)
                            fn(eng).then_inc(sem_d if inc == 16 else sem_c, inc)
                        if inc == 16:
                            d_before += 16
                        else:
                            c_before += 1
                return body
            getattr(block, eng_name)(make(eng_name))
        cfin = sum(1 for (_, _, i) in steps if i == 1)
        dfin = sum(16 for (_, _, i) in steps if i == 16)

        @block.vector
        def _(v):
            if cfin:
                v.wait_ge(sem_c, cfin)
            if dfin:
                v.wait_ge(sem_d, dfin)
        self.ctx.close()
        return nc


def run_prog(key, build_fn, in_maps):
    if key not in _progs:
        pb = PB(key)
        build_fn(pb)
        _progs[key] = pb.finish()
    res = run_bass_kernel_spmd(_progs[key], in_maps, core_ids=list(range(NCORES)))
    return res.results


def emit_ones(pb):
    onc = pb.sb("ones_col", [128, 1], F32)
    onr = pb.sb("ones_row", [1, 128], F32)
    pb.memset(onc[:], 1.0)
    pb.memset(onr[:], 1.0)
    return onc, onr


def load_w_fm(pb, hbm_ap, kdim, ndim, name):
    kch = kdim // 128
    wsb = pb.sb(f"w_{name}", [128, kch * ndim], BF16)
    pb.cast_dma(wsb[:].rearrange("p (kb n) -> p kb n", kb=kch),
                hbm_ap.rearrange("(kb p) n -> p kb n", p=128))
    return lambda kb, j0, j1: wsb[:, kb * ndim + j0: kb * ndim + j1]


def load_vec_cols(pb, hbm_ap, n, name):
    nch = n // 128
    vsb = pb.sb(f"v_{name}", [128, nch], F32)
    pb.dma(vsb[:], hbm_ap)
    return vsb


def vec_cols_host(v):
    v = np.asarray(v, np.float32)
    return np.ascontiguousarray(v.reshape(-1, 128).T)


def make_ln_scratch(pb, pref):
    return {
        "ps_s": pb.ps(f"{pref}_lnps", [1, 2 * TT], F32),
        "ps_b": pb.ps(f"{pref}_lnpb", [128, 2 * TT], F32),
        "sq": pb.sb(f"{pref}_lnsq", [128, TT], F32),
        "srow": pb.sb(f"{pref}_lnrow", [1, 4 * TT], F32),
    }


def emit_ln_tile(pb, scr, x_slices, y_out_fn, onc, onr, w, eps=1e-5):
    ps_s, ps_b, sq, srow = scr["ps_s"], scr["ps_b"], scr["sq"], scr["srow"]
    M = srow[0:1, 0:w]
    RS = srow[0:1, TT:TT + w]
    T1 = srow[0:1, 2 * TT:2 * TT + w]
    for ch in range(NCH):
        pb.mm(ps_s[0:1, 0:w], onc[:], x_slices[ch],
              start=(ch == 0), stop=(ch == NCH - 1))
    for ch in range(NCH):
        pb.act(sq[:, 0:w], x_slices[ch], AF.Square)
        pb.mm(ps_s[0:1, TT:TT + w], onc[:], sq[:, 0:w],
              start=(ch == 0), stop=(ch == NCH - 1))
    pb.ts(M, ps_s[0:1, 0:w], 1.0 / D, ALU.mult)
    pb.ts(T1, ps_s[0:1, TT:TT + w], 1.0 / D, ALU.mult)
    pb.tt(RS, M, M, ALU.mult)
    pb.tt(T1, T1, RS, ALU.subtract)
    pb.act(T1, T1, AF.Sqrt, bias=pb.constcol(eps, 1))
    pb.recip(RS, T1)
    pb.mm(ps_b[:, 0:w], onr[:], M)
    pb.mm(ps_b[:, TT:TT + w], onr[:], RS)
    mb = ps_b[:, 0:w]
    rb = ps_b[:, TT:TT + w]
    for ch in range(NCH):
        pb.tt(sq[:, 0:w], x_slices[ch], mb, ALU.subtract)
        pb.tt(y_out_fn(ch), sq[:, 0:w], rb, ALU.mult)
    return mb, rb


def build_l1_ffn(pb):
    WID = 4096
    WCH = WID // 128
    xr = pb.inp("xr", [D, RPC])
    w1 = pb.inp("w1", [D, WID])
    b1 = pb.inp("b1", [128, WID // 128])
    w2 = pb.inp("w2", [WID, D])
    b2 = pb.inp("b2", [128, D // 128])
    xo = pb.out("xo", [D, RPC])

    onc, onr = emit_ones(pb)
    w1b = load_w_fm(pb, w1, D, WID, "w1")
    w2b = load_w_fm(pb, w2, WID, D, "w2")
    b1c = load_vec_cols(pb, b1, WID, "b1")
    b2c = load_vec_cols(pb, b2, D, "b2")

    xbuf = pb.sb("xbuf", [128, NCH * TT], F32)
    ybuf = pb.sb("ybuf", [128, NCH * TT], BF16)
    hbuf = pb.sb("hbuf", [128, WCH * TT], BF16)
    obuf = pb.sb("obuf", [128, TT], F32)
    ps_h = pb.ps("ps_h", [128, TT], F32)
    ps_o = pb.ps("ps_o", [128, 2 * TT], F32)

    lnscr = make_ln_scratch(pb, "f")
    for t0 in range(0, RPC, TT):
        for ch in range(NCH):
            pb.dma(xbuf[:, ch * TT:(ch + 1) * TT],
                   xr[ch * 128:(ch + 1) * 128, t0:t0 + TT])
        xs = [xbuf[:, ch * TT:(ch + 1) * TT] for ch in range(NCH)]
        emit_ln_tile(pb, lnscr, xs, lambda ch: ybuf[:, ch * TT:(ch + 1) * TT],
                     onc, onr, TT)
        for j in range(WCH):
            for dch in range(NCH):
                pb.mm(ps_h[:], w1b(dch, j * 128, (j + 1) * 128),
                      ybuf[:, dch * TT:(dch + 1) * TT],
                      start=(dch == 0), stop=(dch == NCH - 1))
            pb.act(hbuf[:, j * TT:(j + 1) * TT], ps_h[:], AF.Relu,
                   bias=b1c[:, j:j + 1])
        for half in range(4):
            for oc in range(2):
                och = half * 2 + oc
                for j in range(WCH):
                    pb.mm(ps_o[:, oc * TT:(oc + 1) * TT],
                          w2b(j, och * 128, (och + 1) * 128),
                          hbuf[:, j * TT:(j + 1) * TT],
                          start=(j == 0), stop=(j == WCH - 1))
            for oc in range(2):
                och = half * 2 + oc
                pb.ts(obuf[:], ps_o[:, oc * TT:(oc + 1) * TT], b2c[:, och:och + 1],
                      ALU.add)
                pb.tt(obuf[:], obuf[:], xbuf[:, och * TT:(och + 1) * TT], ALU.add)
                pb.dma(xo[och * 128:(och + 1) * 128, t0:t0 + TT], obuf[:])
    return pb


def sub_l1_ffn(X, inputs):
    in_maps = []
    for c in range(NCORES):
        b, h = c // 2, c % 2
        xr = np.ascontiguousarray(X[b, h * RPC:(h + 1) * RPC, :].T)
        in_maps.append({
            "xr": xr,
            "w1": inputs["L1_ffn_w1"], "b1": vec_cols_host(inputs["L1_ffn_b1"]),
            "w2": inputs["L1_ffn_w2"], "b2": vec_cols_host(inputs["L1_ffn_b2"]),
        })
    res = run_prog("l1_ffn", build_l1_ffn, in_maps)
    Xn = np.empty_like(X)
    for c in range(NCORES):
        b, h = c // 2, c % 2
        Xn[b, h * RPC:(h + 1) * RPC, :] = res[c]["xo"].T
    return Xn


def build_l0_ffn(pb):
    xr_in = pb.inp("xr", [D, RPC])
    xprev = pb.inp("xprev", [2, D])
    rolec = pb.inp("rolec", [128, 1])
    wrT = pb.inp("wrT", [D, D])
    wkT = pb.inp("wkT", [D, D])
    wvT = pb.inp("wvT", [D, D])
    mur = pb.inp("mur", [128, NCH])
    muk = pb.inp("muk", [128, NCH])
    xo = pb.out("xo", [D, RPC])

    onc, onr = emit_ones(pb)
    wrb = load_w_fm(pb, wrT, D, D, "wr")
    wkb = load_w_fm(pb, wkT, D, D, "wk")
    wvb = load_w_fm(pb, wvT, D, D, "wv")
    murc = load_vec_cols(pb, mur, D, "mur")
    mukc = load_vec_cols(pb, muk, D, "muk")

    YW = RPC + 1
    ybuf = pb.sb("ybuf", [128, NCH * YW], BF16)
    xbuf = pb.sb("xbuf", [128, NCH * TT], F32)
    xpbuf = pb.sb("xpbuf", [128, NCH], F32)
    rbuf = pb.sb("rbuf", [128, NCH * TT], F32)
    kbuf = pb.sb("kbuf", [128, NCH * TT], BF16)
    mixr = pb.sb("mixr", [128, NCH * TT], BF16)
    mixk = pb.sb("mixk", [128, NCH * TT], BF16)
    tmp = pb.sb("tmpf", [128, TT], F32)
    obuf = pb.sb("obuf", [128, TT], F32)
    ps_m = pb.ps("ps_m", [128, 2 * TT], F32)
    lnscr = make_ln_scratch(pb, "f0")

    def ycol(ch, a, b):
        return ybuf[:, ch * YW + a: ch * YW + b]

    rcol = pb.sb("rcol", [128, 1], F32)
    pb.dma(rcol[:], rolec)
    pb.dma(xpbuf[:], xprev[0:1, :].rearrange("one (p c) -> p (one c)", p=128))
    ytmp = pb.sb("ybnd", [128, NCH], F32)
    emit_ln_tile(pb, lnscr, [xpbuf[:, ch:ch + 1] for ch in range(NCH)],
                 lambda ch: ytmp[:, ch:ch + 1], onc, onr, 1)
    for ch in range(NCH):
        pb.ts(ytmp[:, ch:ch + 1], ytmp[:, ch:ch + 1], rcol[:, 0:1], ALU.mult)
        pb.copy(ycol(ch, 0, 1), ytmp[:, ch:ch + 1])
    for t0 in range(0, RPC, TT):
        for ch in range(NCH):
            pb.dma(xbuf[:, ch * TT:(ch + 1) * TT],
                   xr_in[ch * 128:(ch + 1) * 128, t0:t0 + TT])
        emit_ln_tile(pb, lnscr, [xbuf[:, ch * TT:(ch + 1) * TT] for ch in range(NCH)],
                     lambda ch: ycol(ch, 1 + t0, 1 + t0 + TT), onc, onr, TT)

    for t0 in range(0, RPC, TT):
        for ch in range(NCH):
            y_cur = ycol(ch, 1 + t0, 1 + t0 + TT)
            y_prv = ycol(ch, t0, t0 + TT)
            for (mc, dst) in ((murc, mixr), (mukc, mixk)):
                pb.ts(tmp[:], y_cur, mc[:, ch:ch + 1], ALU.mult)
                pb.ts(obuf[:], y_prv, mc[:, ch:ch + 1], ALU.mult)
                pb.tt(obuf[:], y_prv, obuf[:], ALU.subtract)
                pb.tt(dst[:, ch * TT:(ch + 1) * TT], tmp[:], obuf[:], ALU.add)
        for och in range(NCH):
            for dch in range(NCH):
                pb.mm(ps_m[:, 0:TT], wrb(dch, och * 128, (och + 1) * 128),
                      mixr[:, dch * TT:(dch + 1) * TT],
                      start=(dch == 0), stop=(dch == NCH - 1))
            pb.act(rbuf[:, och * TT:(och + 1) * TT], ps_m[:, 0:TT], AF.Sigmoid)
            for dch in range(NCH):
                pb.mm(ps_m[:, TT:2 * TT], wkb(dch, och * 128, (och + 1) * 128),
                      mixk[:, dch * TT:(dch + 1) * TT],
                      start=(dch == 0), stop=(dch == NCH - 1))
            pb.act(tmp[:], ps_m[:, TT:2 * TT], AF.Relu)
            pb.act(kbuf[:, och * TT:(och + 1) * TT], tmp[:], AF.Square)
        for och in range(NCH):
            for dch in range(NCH):
                pb.mm(ps_m[:, 0:TT], wvb(dch, och * 128, (och + 1) * 128),
                      kbuf[:, dch * TT:(dch + 1) * TT],
                      start=(dch == 0), stop=(dch == NCH - 1))
            pb.tt(obuf[:], ps_m[:, 0:TT], rbuf[:, och * TT:(och + 1) * TT], ALU.mult)
            pb.dma(xbuf[:, 0:TT], xr_in[och * 128:(och + 1) * 128, t0:t0 + TT])
            pb.tt(obuf[:], obuf[:], xbuf[:, 0:TT], ALU.add)
            pb.dma(xo[och * 128:(och + 1) * 128, t0:t0 + TT], obuf[:])
    return pb


def sub_l0_ffn(X, inputs):
    in_maps = []
    wrT = np.ascontiguousarray(np.asarray(inputs["L0_ffn_W_r"]).T)
    wkT = np.ascontiguousarray(np.asarray(inputs["L0_ffn_W_k"]).T)
    wvT = np.ascontiguousarray(np.asarray(inputs["L0_ffn_W_v"]).T)
    mur = vec_cols_host(inputs["L0_ffn_mu_r"])
    muk = vec_cols_host(inputs["L0_ffn_mu_k"])
    for c in range(NCORES):
        b, h = c // 2, c % 2
        xr = np.ascontiguousarray(X[b, h * RPC:(h + 1) * RPC, :].T)
        xp = np.zeros((2, D), np.float32)
        xp[0] = X[b, RPC - 1].reshape(NCH, 128).T.flatten()
        role = np.full((128, 1), float(h), np.float32)
        in_maps.append({"xr": xr, "xprev": xp, "rolec": role, "wrT": wrT,
                        "wkT": wkT, "wvT": wvT, "mur": mur, "muk": muk})
    res = run_prog("l0_ffn", build_l0_ffn, in_maps)
    Xn = np.empty_like(X)
    for c in range(NCORES):
        b, h = c // 2, c % 2
        Xn[b, h * RPC:(h + 1) * RPC, :] = res[c]["xo"].T
    return Xn


def build_attn_a_qkv(pb):
    xr = pb.inp("xr", [D, RPC])
    wqT = pb.inp("wqT", [D, D]); wkT = pb.inp("wkT", [D, D]); wvT = pb.inp("wvT", [D, D])
    bq = pb.inp("bq", [128, NCH]); bk = pb.inp("bk", [128, NCH]); bv = pb.inp("bv", [128, NCH])
    Qb = pb.out("Qb", [D, RPC], BF16)
    Kb = pb.out("Kb", [D, RPC], BF16)
    Vb = pb.out("Vb", [D, RPC], BF16)
    rq = pb.out("rq", [1, RPC])
    rk = pb.out("rk", [1, RPC])

    onc, onr = emit_ones(pb)
    wq = load_w_fm(pb, wqT, D, D, "wq")
    wk = load_w_fm(pb, wkT, D, D, "wk")
    wv = load_w_fm(pb, wvT, D, D, "wv")
    bqc = load_vec_cols(pb, bq, D, "bq")
    bkc = load_vec_cols(pb, bk, D, "bk")
    bvc = load_vec_cols(pb, bv, D, "bv")

    xbuf = pb.sb("xbuf", [128, NCH * TT], F32)
    ybuf = pb.sb("ybuf", [128, NCH * TT], BF16)
    qf = pb.sb("qf", [128, TT], F32)
    sq = pb.sb("sqn", [128, TT], F32)
    stage = pb.sb("stage", [128, TT], BF16)
    nrow = pb.sb("nrow", [1, TT], F32)
    ps_p = pb.ps("ps_p", [128, TT], F32)
    ps_n = pb.ps("ps_n", [1, TT], F32)
    lnscr = make_ln_scratch(pb, "aa")

    for t0 in range(0, RPC, TT):
        for ch in range(NCH):
            pb.dma(xbuf[:, ch * TT:(ch + 1) * TT],
                   xr[ch * 128:(ch + 1) * 128, t0:t0 + TT])
        emit_ln_tile(pb, lnscr, [xbuf[:, ch * TT:(ch + 1) * TT] for ch in range(NCH)],
                     lambda ch: ybuf[:, ch * TT:(ch + 1) * TT], onc, onr, TT)
        for (wb, bc, OUT, rr) in ((wq, bqc, Qb, rq), (wk, bkc, Kb, rk)):
            for och in range(NCH):
                for dch in range(NCH):
                    pb.mm(ps_p[:], wb(dch, och * 128, (och + 1) * 128),
                          ybuf[:, dch * TT:(dch + 1) * TT],
                          start=(dch == 0), stop=(dch == NCH - 1))
                pb.ts(qf[:], ps_p[:], bc[:, och:och + 1], ALU.add)
                pb.copy(stage[:], qf[:])
                pb.dma(OUT[och * 128:(och + 1) * 128, t0:t0 + TT], stage[:])
                pb.act(sq[:], qf[:], AF.Square)
                pb.mm(ps_n[0:1, :], onc[:], sq[:],
                      start=(och == 0), stop=(och == NCH - 1))
            pb.act(nrow[0:1, :], ps_n[0:1, :], AF.Sqrt)
            pb.ts(nrow[0:1, :], nrow[0:1, :], 1e-8, ALU.add)
            pb.recip(nrow[0:1, :], nrow[0:1, :])
            pb.dma(rr[0:1, t0:t0 + TT], nrow[0:1, :])
        for och in range(NCH):
            for dch in range(NCH):
                pb.mm(ps_p[:], wv(dch, och * 128, (och + 1) * 128),
                      ybuf[:, dch * TT:(dch + 1) * TT],
                      start=(dch == 0), stop=(dch == NCH - 1))
            pb.ts(qf[:], ps_p[:], bvc[:, och:och + 1], ALU.add)
            pb.copy(stage[:], qf[:])
            pb.dma(Vb[och * 128:(och + 1) * 128, t0:t0 + TT], stage[:])
    return pb


def build_attn_a_av(pb):
    NKT = S // 128
    xr = pb.inp("xr", [D, RPC])
    Qb = pb.inp("Qb", [D, RPC], BF16)
    Kt = pb.inp("Kt", [NKT, 128, NCH * 128], BF16)
    Vt = pb.inp("Vt", [S, D], BF16)
    rqv = pb.inp("rq", [1, RPC])
    rkc = pb.inp("rkc", [128, NKT])
    woT = pb.inp("woT", [D, D])
    bo = pb.inp("bo", [128, NCH])
    xo = pb.out("xo", [D, RPC])

    onc, onr = emit_ones(pb)
    oncb = pb.sb("oncb", [128, 1], BF16)
    pb.memset(oncb[:], 1.0)
    wo = load_w_fm(pb, woT, D, D, "wo")
    boc = load_vec_cols(pb, bo, D, "bo")

    qbuf = pb.sb("qbuf", [128, NCH * RPC], BF16)
    vbuf = pb.sb("vbuf", [128, NKT * D], BF16)
    kbuf = pb.sb("kbuf", [128, NCH * 128], BF16)
    rkcol = pb.sb("rkcol", [128, NKT], F32)
    rqb = pb.sb("rqb", [128, TT], F32)
    ebuf = pb.sb("ebuf", [128, NKT * TT], BF16)
    zbuf = pb.sb("zbuf", [128, TT], F32)
    sbuf2 = pb.sb("sbuf2", [128, TT], F32)
    att = pb.sb("att", [128, NCH * TT], BF16)
    dben = pb.sb("dben", [128, TT], F32)
    drow = pb.sb("drow", [1, TT], F32)
    obuf = pb.sb("obuf", [128, TT], F32)
    xbuf = pb.sb("xbuf", [128, TT], F32)
    ps_s = pb.ps("ps_s", [128, TT], F32)
    ps_d = pb.ps("ps_d", [1, TT], F32)
    ps_b = pb.ps("ps_b", [128, TT], F32)
    ps_av = pb.ps("ps_av", [128, TT], F32)
    ps_o = pb.ps("ps_o", [128, TT], F32)

    pb.dma(qbuf[:].rearrange("p (dc t) -> p dc t", dc=NCH),
           Qb.rearrange("(dc p) t -> p dc t", p=128))
    pb.dma(vbuf[:].rearrange("p (kt d) -> p kt d", kt=NKT),
           Vt.rearrange("(kt p) d -> p kt d", p=128))
    pb.dma(rkcol[:], rkc)
    rqrow = pb.sb("rqrow", [1, RPC], F32)
    pb.dma(rqrow[:], rqv)

    mpi = -math.pi / 2.0
    for t0 in range(0, RPC, TT):
        pb.mm(ps_b[:], onr[:], rqrow[0:1, t0:t0 + TT])
        pb.copy(rqb[:], ps_b[:])
        for kt in range(NKT):
            pb.dma(kbuf[:], Kt[kt])
            for dch in range(NCH):
                pb.mm(ps_s[:], kbuf[:, dch * 128:(dch + 1) * 128],
                      qbuf[:, dch * RPC + t0: dch * RPC + t0 + TT],
                      start=(dch == 0), stop=(dch == NCH - 1))
            pb.ts(zbuf[:], ps_s[:], rkcol[:, kt:kt + 1], ALU.mult)
            pb.tt(zbuf[:], zbuf[:], rqb[:], ALU.mult)
            pb.act(zbuf[:], zbuf[:], AF.Abs)
            pb.act(sbuf2[:], zbuf[:], AF.Sin, bias=pb.constcol(mpi), scale=math.pi)
            pb.tt(zbuf[:], sbuf2[:], ps_s[:], ALU.mult)
            pb.act(ebuf[:, kt * TT:(kt + 1) * TT], zbuf[:], AF.Exp,
                   scale=-1.0 / 32.0)
            pb.mm(ps_d[0:1, :], oncb[:], ebuf[:, kt * TT:(kt + 1) * TT],
                  start=(kt == 0), stop=(kt == NKT - 1))
        pb.recip(drow[0:1, :], ps_d[0:1, :])
        pb.mm(ps_b[:], onr[:], drow[0:1, :])
        pb.copy(dben[:], ps_b[:])
        for dv in range(NCH):
            for kt in range(NKT):
                pb.mm(ps_av[:], vbuf[:, kt * D + dv * 128: kt * D + (dv + 1) * 128],
                      ebuf[:, kt * TT:(kt + 1) * TT],
                      start=(kt == 0), stop=(kt == NKT - 1))
            pb.tt(att[:, dv * TT:(dv + 1) * TT], ps_av[:], dben[:], ALU.mult)
        for och in range(NCH):
            for dv in range(NCH):
                pb.mm(ps_o[:], wo(dv, och * 128, (och + 1) * 128),
                      att[:, dv * TT:(dv + 1) * TT],
                      start=(dv == 0), stop=(dv == NCH - 1))
            pb.ts(obuf[:], ps_o[:], boc[:, och:och + 1], ALU.add)
            pb.dma(xbuf[:], xr[och * 128:(och + 1) * 128, t0:t0 + TT])
            pb.tt(obuf[:], obuf[:], xbuf[:], ALU.add)
            pb.dma(xo[och * 128:(och + 1) * 128, t0:t0 + TT], obuf[:])
    return pb


def _tile_k_host(Kfm):
    NKT = Kfm.shape[1] // 128
    a = Kfm.reshape(NCH, 128, NKT, 128)
    a = np.transpose(a, (2, 1, 0, 3))
    return np.ascontiguousarray(a.reshape(NKT, 128, NCH * 128))


def sub_l0_attn_a(X, inputs):
    ii = {k: np.asarray(v) for k, v in inputs.items()}
    wqT = np.ascontiguousarray(ii["L0_attn_a_Wq"].T)
    wkT = np.ascontiguousarray(ii["L0_attn_a_Wk"].T)
    wvT = np.ascontiguousarray(ii["L0_attn_a_Wv"].T)
    woT = np.ascontiguousarray(ii["L0_attn_a_Wo"].T)
    in_maps = []
    for c in range(NCORES):
        b, h = c // 2, c % 2
        xr = np.ascontiguousarray(X[b, h * RPC:(h + 1) * RPC, :].T)
        in_maps.append({
            "xr": xr, "wqT": wqT, "wkT": wkT, "wvT": wvT,
            "bq": vec_cols_host(ii["L0_attn_a_bq"]),
            "bk": vec_cols_host(ii["L0_attn_a_bk"]),
            "bv": vec_cols_host(ii["L0_attn_a_bv"]),
        })
    r1 = run_prog("attn_a_qkv", build_attn_a_qkv, in_maps)

    in_maps2 = []
    for c in range(NCORES):
        b, h = c // 2, c % 2
        pair0, pair1 = 2 * b, 2 * b + 1
        Kfull = np.concatenate([r1[pair0]["Kb"], r1[pair1]["Kb"]], axis=1)
        Vfull = np.concatenate([r1[pair0]["Vb"], r1[pair1]["Vb"]], axis=1)
        rkfull = np.concatenate([r1[pair0]["rk"], r1[pair1]["rk"]], axis=1)
        xr = np.ascontiguousarray(X[b, h * RPC:(h + 1) * RPC, :].T)
        in_maps2.append({
            "xr": xr,
            "Qb": r1[c]["Qb"],
            "Kt": _tile_k_host(Kfull),
            "Vt": np.ascontiguousarray(Vfull.T),
            "rq": r1[c]["rq"],
            "rkc": np.ascontiguousarray(rkfull.reshape(S // 128, 128).T),
            "woT": woT,
            "bo": vec_cols_host(ii["L0_attn_a_bo"]),
        })
    r2 = run_prog("attn_a_av", build_attn_a_av, in_maps2)
    Xn = np.empty_like(X)
    for c in range(NCORES):
        b, h = c // 2, c % 2
        Xn[b, h * RPC:(h + 1) * RPC, :] = r2[c]["xo"].T
    return Xn


def build_attn_b1_qkv(pb):
    xr = pb.inp("xr", [D, RPC])
    wq = pb.inp("wq", [D, D]); wk = pb.inp("wk", [D, D]); wv = pb.inp("wv", [D, D])
    wg1 = pb.inp("wg1", [D, 64]); wg2 = pb.inp("wg2", [64, D])
    poscol = pb.inp("poscol", [128, RPC // 128])
    qr = pb.out("qr", [RPC, D], BF16)
    kr = pb.out("kr", [RPC, D], BF16)
    vt = pb.out("vt", [RPC, D], BF16)
    gate = pb.out("gate", [D, RPC])

    onc, onr = emit_ones(pb)
    wqb = load_w_fm(pb, wq, D, D, "wq")
    wkb = load_w_fm(pb, wk, D, D, "wk")
    wvb = load_w_fm(pb, wv, D, D, "wv")
    wg1b = load_w_fm(pb, wg1, D, 64, "wg1")
    wg2s = pb.sb("w_wg2", [64, D], BF16)
    pb.cast_dma(wg2s[:], wg2)

    xbuf = pb.sb("xbuf", [128, NCH * TT], F32)
    ybuf = pb.sb("ybuf", [128, NCH * RPC], BF16)
    pcol = pb.sb("pcol", [128, RPC // 128], F32)
    fbc = pb.sb("fbc", [128, 32], F32)
    cosb = pb.sb("cosb", [128, 32], F32)
    sinb = pb.sb("sinb", [128, 32], F32)
    angb = pb.sb("angb", [128, 32], F32)
    angm = pb.sb("angm", [128, 32], F32)
    qs = pb.sb("qs", [128, TT], F32)
    rt1 = pb.sb("rt1", [128, 256], F32)
    rt2 = pb.sb("rt2", [128, 256], F32)
    rop = pb.sb("rop", [128, TT], BF16)
    g1buf = pb.sb("g1buf", [64, TT], BF16)
    gf = pb.sb("gf", [128, TT], F32)
    ps_q = pb.ps("ps_q", [128, TT], F32)
    ps_g = pb.ps("ps_g", [64, TT], F32)
    ps_r = pb.ps("ps_r", [128, 32], F32)
    lnscr = make_ln_scratch(pb, "b1")

    pb.dma(pcol[:], poscol)
    pb.iota(fbc[0:1, :], [[1, 32]])
    pb.act(fbc[0:1, :], fbc[0:1, :], AF.Exp, scale=-math.log(10000.0) / 32.0)
    pb.ts(fbc[0:1, :], fbc[0:1, :], 1.0 / (2.0 * math.pi), ALU.mult)
    pb.mm(ps_r[:], onr[:], fbc[0:1, :])
    pb.copy(fbc[:], ps_r[:])
    angi = pb.sb("angi", [128, 32], mybir.dt.int32)

    for t0 in range(0, RPC, TT):
        for ch in range(NCH):
            pb.dma(xbuf[:, ch * TT:(ch + 1) * TT],
                   xr[ch * 128:(ch + 1) * 128, t0:t0 + TT])
        emit_ln_tile(pb, lnscr,
                     [xbuf[:, ch * TT:(ch + 1) * TT] for ch in range(NCH)],
                     lambda ch: ybuf[:, ch * RPC + t0: ch * RPC + t0 + TT],
                     onc, onr, TT)

    PI = math.pi
    for tr in range(RPC // 128):
        t0 = tr * 128
        pb.ts(angb[:], fbc[:], pcol[:, tr:tr + 1], ALU.mult)
        pb.copy(angi[:], angb[:])
        pb.copy(angm[:], angi[:])
        pb.tt(angm[:], angb[:], angm[:], ALU.subtract)
        pb.act(sinb[:], angm[:], AF.Sin, scale=2 * PI)
        pb.ts(angb[:], angb[:], 0.25, ALU.add)
        pb.copy(angi[:], angb[:])
        pb.copy(angm[:], angi[:])
        pb.tt(angm[:], angb[:], angm[:], ALU.subtract)
        pb.act(cosb[:], angm[:], AF.Sin, scale=2 * PI)
        for (wb, OUT, do_rope) in ((wqb, qr, True), (wkb, kr, True), (wvb, vt, False)):
            for js in range(2):
                for dch in range(NCH):
                    pb.mm(ps_q[:], ybuf[:, dch * RPC + t0: dch * RPC + t0 + 128],
                          wb(dch, js * TT, (js + 1) * TT),
                          start=(dch == 0), stop=(dch == NCH - 1))
                if not do_rope:
                    pb.copy(rop[:], ps_q[:])
                    pb.dma(OUT[t0:t0 + 128, js * TT:(js + 1) * TT], rop[:])
                    continue
                pb.act(qs[:], ps_q[:], AF.Silu)
                v3 = qs[:].rearrange("p (h two j) -> p h two j", two=2, j=32)
                o3 = rop[:].rearrange("p (h two j) -> p h two j", two=2, j=32)
                cb = cosb[:].unsqueeze(1).broadcast_to((128, 8, 32))
                sb_ = sinb[:].unsqueeze(1).broadcast_to((128, 8, 32))
                x1 = v3[:, :, 0, :]
                x2 = v3[:, :, 1, :]
                t1 = rt1[:].rearrange("p (h j) -> p h j", j=32)
                t2 = rt2[:].rearrange("p (h j) -> p h j", j=32)
                pb.tt(t1, x1, cb, ALU.mult)
                pb.tt(t2, x2, sb_, ALU.mult)
                pb.tt(o3[:, :, 0, :], t1, t2, ALU.subtract)
                pb.tt(t1, x1, sb_, ALU.mult)
                pb.tt(t2, x2, cb, ALU.mult)
                pb.tt(o3[:, :, 1, :], t1, t2, ALU.add)
                pb.dma(OUT[t0:t0 + 128, js * TT:(js + 1) * TT], rop[:])
    for t0 in range(0, RPC, TT):
        for dch in range(NCH):
            pb.mm(ps_g[:], wg1b(dch, 0, 64),
                  ybuf[:, dch * RPC + t0: dch * RPC + t0 + TT],
                  start=(dch == 0), stop=(dch == NCH - 1))
        pb.copy(g1buf[:], ps_g[:])
        for och in range(NCH):
            pb.mm(ps_q[:], wg2s[:, och * 128:(och + 1) * 128], g1buf[:])
            pb.act(gf[:], ps_q[:], AF.Sigmoid)
            pb.dma(gate[och * 128:(och + 1) * 128, t0:t0 + TT], gf[:])
    return pb


def build_attn_b1_scan(pb):
    NH = 8
    C = 64
    NC_ = S // C
    F = NH * 64
    qfm = pb.inp("qfm", [4, 128, S], BF16)
    kfm = pb.inp("kfm", [4, 128, S], BF16)
    ktm = pb.inp("ktm", [S, F], BF16)
    vtm = pb.inp("vtm", [S, F], BF16)
    ot = pb.out("ot", [S, F])

    qsb = pb.sb("qsb", [128, 4 * S], BF16)
    ksb = pb.sb("ksb", [128, 4 * S], BF16)
    ktsb = pb.sb("ktsb", [64, NC_ * F], BF16)
    vtsb = pb.sb("vtsb", [64, NC_ * F], BF16)
    Sf = pb.sb("Sf", [64, F], F32)
    Sb = pb.sb("Sb", [128, 4 * 64], BF16)
    ebuf = pb.sb("ebuf", [64, F], BF16)
    mask = pb.sb("maskt", [64, F], BF16)
    ostage = pb.sb("ostage", [64, F], F32)
    pstage = pb.sb("pstage", [64, F], F32)
    ps_e = pb.ps("ps_e", [64, F], F32)
    ps_o = pb.ps("ps_o", [64, F], F32)
    ps_s = pb.ps("ps_s", [64, F], F32)

    for hp in range(4):
        pb.dma(qsb[:, hp * S:(hp + 1) * S], qfm[hp])
        pb.dma(ksb[:, hp * S:(hp + 1) * S], kfm[hp])
    pb.dma(ktsb[:].rearrange("p (c f) -> p c f", f=F),
           ktm.rearrange("(c p) f -> p c f", p=C))
    pb.dma(vtsb[:].rearrange("p (c f) -> p c f", f=F),
           vtm.rearrange("(c p) f -> p c f", p=C))
    pb.memset(mask[:], 1.0)
    pb.affine_select(mask[:], mask[:], [[0, NH], [1, C]], ALU.is_ge, 0.0,
                     channel_multiplier=-1)
    pb.memset(Sf[:], 0.0, eng="vector")
    for h in range(NH):
        hp, par = divmod(h, 2)
        pb.copy(Sb[64 * par:64 * par + 64, hp * 64:(hp + 1) * 64],
                Sf[:, h * 64:(h + 1) * 64])

    for c in range(NC_):
        t0 = c * C
        for h in range(NH):
            hp, par = divmod(h, 2)
            pb.mm(ps_e[:, h * 64:(h + 1) * 64],
                  ksb[64 * par:64 * par + 64, hp * S + t0: hp * S + t0 + C],
                  qsb[64 * par:64 * par + 64, hp * S + t0: hp * S + t0 + C])
        pb.tt(ebuf[:], ps_e[:], mask[:], ALU.mult)
        for h in range(NH):
            hp, par = divmod(h, 2)
            pb.mm(ps_o[:, h * 64:(h + 1) * 64], ebuf[:, h * 64:(h + 1) * 64],
                  vtsb[0:64, c * F + h * 64: c * F + (h + 1) * 64],
                  start=True, stop=False)
            pb.mm(ps_o[:, h * 64:(h + 1) * 64],
                  qsb[64 * par:64 * par + 64, hp * S + t0: hp * S + t0 + C],
                  Sb[64 * par:64 * par + 64, hp * 64:(hp + 1) * 64],
                  start=False, stop=True)
        pb.copy(ostage[:], ps_o[:])
        pb.dma(ot[t0:t0 + C, :], ostage[:])
        for h in range(NH):
            pb.mm(ps_s[:, h * 64:(h + 1) * 64],
                  ktsb[0:64, c * F + h * 64: c * F + (h + 1) * 64],
                  vtsb[0:64, c * F + h * 64: c * F + (h + 1) * 64])
        pb.tt(Sf[:], Sf[:], ps_s[:], ALU.add)
        for h in range(NH):
            hp, par = divmod(h, 2)
            pb.copy(Sb[64 * par:64 * par + 64, hp * 64:(hp + 1) * 64],
                    Sf[:, h * 64:(h + 1) * 64])
    return pb


def build_attn_b1_out(pb):
    xr = pb.inp("xr", [D, RPC])
    ofm = pb.inp("ofm", [D, RPC])
    gfm = pb.inp("gfm", [D, RPC])
    wo = pb.inp("wo", [D, D])
    gam = pb.inp("gam", [128, NCH])
    bet = pb.inp("bet", [128, NCH])
    xo = pb.out("xo", [D, RPC])

    onc, onr = emit_ones(pb)
    wob = load_w_fm(pb, wo, D, D, "wo")
    gamc = load_vec_cols(pb, gam, D, "gam")
    betc = load_vec_cols(pb, bet, D, "bet")

    abuf = pb.sb("abuf", [128, NCH * TT], F32)
    ybuf = pb.sb("ybuf", [128, NCH * TT], BF16)
    tmp = pb.sb("tmpo", [128, TT], F32)
    obuf = pb.sb("obuf", [128, TT], F32)
    ps_p = pb.ps("ps_p", [128, TT], F32)
    lnscr = make_ln_scratch(pb, "b1o")

    for t0 in range(0, RPC, TT):
        for ch in range(NCH):
            pb.dma(abuf[:, ch * TT:(ch + 1) * TT],
                   ofm[ch * 128:(ch + 1) * 128, t0:t0 + TT])
            pb.dma(tmp[:], gfm[ch * 128:(ch + 1) * 128, t0:t0 + TT])
            pb.tt(abuf[:, ch * TT:(ch + 1) * TT],
                  abuf[:, ch * TT:(ch + 1) * TT], tmp[:], ALU.mult)
        xs = [abuf[:, ch * TT:(ch + 1) * TT] for ch in range(NCH)]

        def yout(ch):
            return ybuf[:, ch * TT:(ch + 1) * TT]
        emit_ln_tile(pb, lnscr, xs, yout, onc, onr, TT)
        for ch in range(NCH):
            pb.ts(ybuf[:, ch * TT:(ch + 1) * TT], ybuf[:, ch * TT:(ch + 1) * TT],
                  gamc[:, ch:ch + 1], ALU.mult, betc[:, ch:ch + 1], ALU.add)
        for och in range(NCH):
            for dch in range(NCH):
                pb.mm(ps_p[:], wob(dch, och * 128, (och + 1) * 128),
                      ybuf[:, dch * TT:(dch + 1) * TT],
                      start=(dch == 0), stop=(dch == NCH - 1))
            pb.dma(tmp[:], xr[och * 128:(och + 1) * 128, t0:t0 + TT])
            pb.tt(obuf[:], ps_p[:], tmp[:], ALU.add)
            pb.dma(xo[och * 128:(och + 1) * 128, t0:t0 + TT], obuf[:])
    return pb


def sub_l1_attn_b(X, inputs):
    ii = {k: np.asarray(v) for k, v in inputs.items()}
    in_maps = []
    for c in range(NCORES):
        b, h = c // 2, c % 2
        xr = np.ascontiguousarray(X[b, h * RPC:(h + 1) * RPC, :].T)
        qoff = h * RPC
        pos = (qoff + np.arange(RPC, dtype=np.float32)).reshape(RPC // 128, 128).T
        in_maps.append({
            "xr": xr, "wq": ii["L1_attn_b_Wq"], "wk": ii["L1_attn_b_Wk"],
            "wv": ii["L1_attn_b_Wv"], "wg1": ii["L1_attn_b_Wg1"],
            "wg2": ii["L1_attn_b_Wg2"],
            "poscol": np.ascontiguousarray(pos),
        })
    r1 = run_prog("attn_b1_qkv", build_attn_b1_qkv, in_maps)

    in_maps2 = []
    for c in range(NCORES):
        b, hh = c // 2, c % 2
        qr = np.concatenate([r1[2 * b]["qr"], r1[2 * b + 1]["qr"]], axis=0)
        kr = np.concatenate([r1[2 * b]["kr"], r1[2 * b + 1]["kr"]], axis=0)
        vt = np.concatenate([r1[2 * b]["vt"], r1[2 * b + 1]["vt"]], axis=0)
        def fmpack(a):
            af = np.ascontiguousarray(a.T)
            af = af.reshape(16, 64, S)[hh * 8:(hh + 1) * 8]
            return np.ascontiguousarray(af.reshape(4, 128, S))
        cols = slice(hh * 512, (hh + 1) * 512)
        in_maps2.append({
            "qfm": fmpack(qr), "kfm": fmpack(kr),
            "ktm": np.ascontiguousarray(kr[:, cols]),
            "vtm": np.ascontiguousarray(vt[:, cols]),
        })
    r2 = run_prog("attn_b1_scan", build_attn_b1_scan, in_maps2)

    in_maps3 = []
    for c in range(NCORES):
        b, h = c // 2, c % 2
        o_full = np.concatenate([r2[2 * b]["ot"], r2[2 * b + 1]["ot"]], axis=1)
        ofm = np.ascontiguousarray(o_full[h * RPC:(h + 1) * RPC, :].T)
        xr = np.ascontiguousarray(X[b, h * RPC:(h + 1) * RPC, :].T)
        in_maps3.append({
            "xr": xr, "ofm": ofm, "gfm": r1[c]["gate"],
            "wo": ii["L1_attn_b_Wo"],
            "gam": vec_cols_host(ii["L1_attn_b_gamma"]),
            "bet": vec_cols_host(ii["L1_attn_b_beta"]),
        })
    r3 = run_prog("attn_b1_out", build_attn_b1_out, in_maps3)
    Xn = np.empty_like(X)
    for c in range(NCORES):
        b, h = c // 2, c % 2
        Xn[b, h * RPC:(h + 1) * RPC, :] = r3[c]["xo"].T
    return Xn


def build_attn_a1_qkv(pb):
    xr = pb.inp("xr", [D, RPC])
    wq = pb.inp("wq", [D, D]); wk = pb.inp("wk", [D, D]); wv = pb.inp("wv", [D, D])
    qt = pb.out("qt", [RPC, D], BF16)
    kt = pb.out("kt", [RPC, D], BF16)
    vt = pb.out("vt", [RPC, D], BF16)

    onc, onr = emit_ones(pb)
    wqb = load_w_fm(pb, wq, D, D, "wq")
    wkb = load_w_fm(pb, wk, D, D, "wk")
    wvb = load_w_fm(pb, wv, D, D, "wv")

    xbuf = pb.sb("xbuf", [128, NCH * TT], F32)
    ybuf = pb.sb("ybuf", [128, NCH * RPC], BF16)
    kf = pb.sb("kf", [128, TT], F32)
    sq = pb.sb("sq2", [128, TT], F32)
    rn = pb.sb("rn", [128, 16], F32)
    rop = pb.sb("rop2", [128, TT], BF16)
    ps_q = pb.ps("ps_q", [128, TT], F32)
    lnscr = make_ln_scratch(pb, "a1")

    for t0 in range(0, RPC, TT):
        for ch in range(NCH):
            pb.dma(xbuf[:, ch * TT:(ch + 1) * TT],
                   xr[ch * 128:(ch + 1) * 128, t0:t0 + TT])
        emit_ln_tile(pb, lnscr,
                     [xbuf[:, ch * TT:(ch + 1) * TT] for ch in range(NCH)],
                     lambda ch: ybuf[:, ch * RPC + t0: ch * RPC + t0 + TT],
                     onc, onr, TT)
    for tr in range(RPC // 128):
        t0 = tr * 128
        for (wb, OUT, norm) in ((wqb, qt, False), (wkb, kt, True), (wvb, vt, False)):
            for js in range(2):
                pb_slice = slice(js * TT, (js + 1) * TT)
                for dch in range(NCH):
                    pb.mm(ps_q[:], ybuf[:, dch * RPC + t0: dch * RPC + t0 + 128],
                          wb(dch, js * TT, (js + 1) * TT),
                          start=(dch == 0), stop=(dch == NCH - 1))
                if not norm:
                    pb.copy(rop[:], ps_q[:])
                    pb.dma(OUT[t0:t0 + 128, js * TT:(js + 1) * TT], rop[:])
                    continue
                pb.copy(kf[:], ps_q[:])
                pb.act(sq[:], kf[:], AF.Square)
                pb.reduce(rn[:, 0:8], sq[:].rearrange("p (h d) -> p h d", d=64),
                          ALU.add)
                pb.act(rn[:, 0:8], rn[:, 0:8], AF.Sqrt)
                pb.ts(rn[:, 0:8], rn[:, 0:8], 1e-12, ALU.max)
                pb.recip(rn[:, 0:8], rn[:, 0:8])
                pb.tt(rop[:].rearrange("p (h d) -> p h d", d=64),
                      kf[:].rearrange("p (h d) -> p h d", d=64),
                      rn[:, 0:8].unsqueeze(2).broadcast_to((128, 8, 64)), ALU.mult)
                pb.dma(OUT[t0:t0 + 128, js * TT:(js + 1) * TT], rop[:])
    return pb


def build_attn_a1_scan(pb):
    NH, C = 8, 64
    NC_ = S // C
    F = NH * 64
    qfm = pb.inp("qfm", [4, 128, S], BF16)
    kfm = pb.inp("kfm", [4, 128, S], BF16)
    ktm = pb.inp("ktm", [S, F], BF16)
    vtm = pb.inp("vtm", [S, F], BF16)
    ab_raw = pb.inp("ab_raw", [1, 16])
    ot = pb.out("ot", [S, F])

    qsb = pb.sb("qsb", [128, 4 * S], BF16)
    ksb = pb.sb("ksb", [128, 4 * S], BF16)
    ktsb = pb.sb("ktsb", [128, (NC_ // 2) * F], BF16)
    vtsb = pb.sb("vtsb", [128, (NC_ // 2) * F], BF16)
    kst = pb.sb("kst", [64, F], BF16)
    vst = pb.sb("vst", [64, F], BF16)
    abr = pb.sb("abr", [1, 16], F32)
    maskA_SL = pb.sb("maskA_SL", [64, F], F32)
    maskA_SU = pb.sb("maskA_SU", [64, F], F32)
    maskSU = pb.sb("maskSU", [64, F], BF16)
    Ib = pb.sb("Ib", [64, F], F32)
    b3sb = pb.sb("b3sb", [1, 16], F32)
    Mb = pb.sb("Mb", [64, F], BF16)
    MTb = pb.sb("MTb", [64, F], BF16)
    POW = [pb.sb(f"POW{l}", [64, F], BF16) for l in range(4)]
    POWT = [pb.sb(f"POWT{l}", [64, F], BF16) for l in range(4)]
    P32 = pb.sb("P32", [64, F], BF16)
    gT = pb.sb("gT", [64, F], BF16)
    Xf = pb.sb("Xf", [64, F], F32)
    Xb = pb.sb("Xb", [64, F], BF16)
    Rall = pb.sb("Rall", [64, NH * 128], BF16)
    Ub = pb.sb("Ub", [64, NH * 128], BF16)
    ghT = pb.sb("ghT", [128, 4 * 64], BF16)
    Rtb = pb.sb("Rtb", [64, F], BF16)
    Sf = pb.sb("Sf", [64, F], F32)
    Sb0 = pb.sb("Sb0", [64, F], BF16)
    Sbp = pb.sb("Sbp", [128, 4 * 64], BF16)
    ostage = pb.sb("ostage", [64, F], F32)
    pstage = pb.sb("pstage", [64, F], F32)
    ps_a = pb.ps("ps_a", [64, F], F32)
    ps_b2 = pb.ps("ps_b2", [64, F], F32)
    ps_u = pb.ps("ps_u", [64, NH * 128], F32)
    ps_y = pb.ps("ps_y", [64, F], F32)
    ps_s = pb.ps("ps_s", [64, F], F32)

    for hp in range(4):
        pb.dma(qsb[:, hp * S:(hp + 1) * S], qfm[hp])
        pb.dma(ksb[:, hp * S:(hp + 1) * S], kfm[hp])
    pb.dma(ktsb[:].rearrange("p (c f) -> p c f", f=F),
           ktm.rearrange("(c p) f -> p c f", p=128))
    pb.dma(vtsb[:].rearrange("p (c f) -> p c f", f=F),
           vtm.rearrange("(c p) f -> p c f", p=128))
    pb.dma(abr[:], ab_raw)
    pb.act(abr[:], abr[:], AF.Sigmoid)
    pb.ts(abr[:], abr[:], 0.5, ALU.mult)
    onr64 = pb.sb("onr64", [1, 64], F32)
    pb.memset(onr64[:], 1.0)
    abbc = pb.sb("abbc", [64, 16], F32)
    ps_ab = pb.ps("ps_ab", [64, 16], F32)
    pb.mm(ps_ab[:], onr64[:], abr[:])
    pb.copy(abbc[:], ps_ab[:])
    pb.memset(maskA_SL[:], 1.0)
    pb.affine_select(maskA_SL[:], maskA_SL[:], [[0, NH], [-1, C]], ALU.is_gt, 0.0,
                     channel_multiplier=1)
    pb.memset(maskSU[:], 1.0)
    pb.affine_select(maskSU[:], maskSU[:], [[0, NH], [1, C]], ALU.is_gt, 0.0,
                     channel_multiplier=-1)
    pb.copy(maskA_SU[:], maskSU[:])
    a3 = abbc[0:64, 0:8].unsqueeze(2).broadcast_to((64, 8, C))
    b3 = abbc[0:64, 8:16].unsqueeze(2).broadcast_to((64, 8, C))
    pb.tt(maskA_SL[:].rearrange("p (h i) -> p h i", i=C),
          maskA_SL[:].rearrange("p (h i) -> p h i", i=C), a3, ALU.mult)
    pb.tt(maskA_SU[:].rearrange("p (h i) -> p h i", i=C),
          maskA_SU[:].rearrange("p (h i) -> p h i", i=C), a3, ALU.mult)
    pb.memset(Ib[:], 1.0)
    pb.affine_select(Ib[:], Ib[:], [[0, NH], [-1, C]], ALU.is_equal, 0.0,
                     channel_multiplier=1)
    pb.memset(Sf[:], 0.0, eng="vector")
    pb.copy(Sb0[:], Sf[:])
    for h in range(NH):
        hp, par = divmod(h, 2)
        pb.copy(Sbp[64 * par:64 * par + 64, hp * 64:(hp + 1) * 64],
                Sf[:, h * 64:(h + 1) * 64])

    def qslice(h, t0, w=C):
        hp, par = divmod(h, 2)
        return qsb[64 * par:64 * par + 64, hp * S + t0: hp * S + t0 + w]

    def kslice(h, t0, w=C):
        hp, par = divmod(h, 2)
        return ksb[64 * par:64 * par + 64, hp * S + t0: hp * S + t0 + w]

    for c in range(NC_):
        t0 = c * C
        cr, cpar = divmod(c, 2)
        pb.copy(kst[:], ktsb[64 * cpar:64 * cpar + 64, cr * F:(cr + 1) * F])
        pb.copy(vst[:], vtsb[64 * cpar:64 * cpar + 64, cr * F:(cr + 1) * F])
        ktc = lambda h: kst[0:64, h * 64:(h + 1) * 64]
        vtc = lambda h: vst[0:64, h * 64:(h + 1) * 64]
        for h in range(NH):
            pb.mm(ps_a[:, h * 64:(h + 1) * 64], kslice(h, t0), kslice(h, t0))
        for h in range(NH):
            pb.mm(ps_b2[:, h * 64:(h + 1) * 64], kslice(h, t0), qslice(h, t0))
        pb.tt(Mb[:], ps_a[:], maskA_SL[:], ALU.mult)
        pb.tt(MTb[:], ps_a[:], maskA_SU[:], ALU.mult)
        pb.tt(gT[:], ps_b2[:], maskSU[:], ALU.mult)
        prev, prevT = Mb, MTb
        for l in range(4):
            for h in range(NH):
                sl = slice(h * 64, (h + 1) * 64)
                pb.mm(ps_a[:, sl], prevT[:, sl], prev[:, sl])
            pb.copy(POW[l][:], ps_a[:])
            for h in range(NH):
                sl = slice(h * 64, (h + 1) * 64)
                pb.mm(ps_a[:, sl], prev[:, sl], prevT[:, sl])
            pb.copy(POWT[l][:], ps_a[:])
            prev, prevT = POW[l], POWT[l]
        for h in range(NH):
            sl = slice(h * 64, (h + 1) * 64)
            pb.mm(ps_a[:, sl], POWT[3][:, sl], POW[3][:, sl])
        pb.copy(P32[:], ps_a[:])
        pb.tt(Xf[:], Ib[:], MTb[:], ALU.subtract)
        pb.copy(Xb[:], Xf[:])
        for pw in (POW[0], POW[1], POW[2], POW[3], P32):
            for h in range(NH):
                sl = slice(h * 64, (h + 1) * 64)
                pb.mm(ps_a[:, sl], pw[:, sl], Xb[:, sl])
            pb.tt(Xf[:], Xf[:], ps_a[:], ALU.add)
            pb.copy(Xb[:], Xf[:])
        pb.tt(Rall[:].rearrange("p (h z) -> p h z", z=128)[:, :, 0:64],
              vst[:].rearrange("p (h z) -> p h z", z=64), b3, ALU.mult)
        pb.tt(Rall[:].rearrange("p (h z) -> p h z", z=128)[:, :, 64:128],
              kst[:].rearrange("p (h z) -> p h z", z=64), a3, ALU.mult)
        for h in range(NH):
            pb.mm(ps_u[:, h * 128:(h + 1) * 128], Xb[:, h * 64:(h + 1) * 64],
                  Rall[:, h * 128:(h + 1) * 128])
        pb.copy(Ub[:], ps_u[:])
        U0 = lambda h: Ub[:, h * 128: h * 128 + 64]
        W0 = lambda h: Ub[:, h * 128 + 64: (h + 1) * 128]
        for h in range(NH):
            pb.mm(ps_b2[:, h * 64:(h + 1) * 64], W0(h), gT[:, h * 64:(h + 1) * 64])
        for par in range(2):
            src = ps_b2[:].rearrange("p (h z) -> p h z", z=64)
            pb.tt(ghT[64 * par:64 * par + 64, :].rearrange("p (hp z) -> p hp z", z=64),
                  qsb[64 * par:64 * par + 64, :].rearrange("p (hp s) -> p hp s", s=S)[:, :, t0:t0 + C],
                  ps_b2[0:64, :].rearrange("p (h z) -> p h z", z=64)[:, par::2, :],
                  ALU.subtract)
        for h in range(NH):
            pb.mm(ps_s[:, h * 64:(h + 1) * 64], ktc(h), U0(h))
        for h in range(NH):
            pb.mm(ps_b2[:, h * 64:(h + 1) * 64], W0(h), ktc(h))
        pb.tt(Rtb[:], Ib[:], ps_b2[:], ALU.subtract)
        for h in range(NH):
            hp, par = divmod(h, 2)
            pb.mm(ps_y[:, h * 64:(h + 1) * 64], gT[:, h * 64:(h + 1) * 64], U0(h),
                  start=True, stop=False)
            pb.mm(ps_y[:, h * 64:(h + 1) * 64],
                  ghT[64 * par:64 * par + 64, hp * 64:(hp + 1) * 64],
                  Sbp[64 * par:64 * par + 64, hp * 64:(hp + 1) * 64],
                  start=False, stop=True)
        pb.copy(ostage[:], ps_y[:])
        pb.dma(ot[t0:t0 + C, :], ostage[:])
        pb.copy(pstage[:], ps_s[:])
        for h in range(NH):
            pb.mm(ps_y[:, h * 64:(h + 1) * 64], Rtb[:, h * 64:(h + 1) * 64],
                  Sb0[:, h * 64:(h + 1) * 64])
        pb.tt(Sf[:], pstage[:], ps_y[:], ALU.add)
        pb.copy(Sb0[:], Sf[:])
        for h in range(NH):
            hp, par = divmod(h, 2)
            pb.copy(Sbp[64 * par:64 * par + 64, hp * 64:(hp + 1) * 64],
                    Sf[:, h * 64:(h + 1) * 64])
    return pb


def build_attn_a1_out(pb):
    xr = pb.inp("xr", [D, RPC])
    ofm = pb.inp("ofm", [D, RPC])
    wo = pb.inp("wo", [D, D])
    xo = pb.out("xo", [D, RPC])
    wob = load_w_fm(pb, wo, D, D, "wo")
    abuf = pb.sb("abuf", [128, TT], F32)
    ybuf = pb.sb("ybuf", [128, NCH * TT], BF16)
    obuf = pb.sb("obuf", [128, TT], F32)
    ps_p = pb.ps("ps_p", [128, TT], F32)
    for t0 in range(0, RPC, TT):
        for ch in range(NCH):
            pb.dma(abuf[:], ofm[ch * 128:(ch + 1) * 128, t0:t0 + TT])
            pb.copy(ybuf[:, ch * TT:(ch + 1) * TT], abuf[:])
        for och in range(NCH):
            for dch in range(NCH):
                pb.mm(ps_p[:], wob(dch, och * 128, (och + 1) * 128),
                      ybuf[:, dch * TT:(dch + 1) * TT],
                      start=(dch == 0), stop=(dch == NCH - 1))
            pb.dma(abuf[:], xr[och * 128:(och + 1) * 128, t0:t0 + TT])
            pb.tt(obuf[:], ps_p[:], abuf[:], ALU.add)
            pb.dma(xo[och * 128:(och + 1) * 128, t0:t0 + TT], obuf[:])
    return pb


def sub_l1_attn_a(X, inputs):
    ii = {k: np.asarray(v) for k, v in inputs.items()}
    in_maps = []
    for c in range(NCORES):
        b, h = c // 2, c % 2
        xr = np.ascontiguousarray(X[b, h * RPC:(h + 1) * RPC, :].T)
        in_maps.append({"xr": xr, "wq": ii["L1_attn_a_Wq"],
                        "wk": ii["L1_attn_a_Wk"], "wv": ii["L1_attn_a_Wv"]})
    r1 = run_prog("attn_a1_qkv", build_attn_a1_qkv, in_maps)

    alpha = ii["L1_attn_a_alpha_raw"].reshape(16)
    beta = ii["L1_attn_a_beta_raw"].reshape(16)
    in_maps2 = []
    for c in range(NCORES):
        b, hh = c // 2, c % 2
        qr = np.concatenate([r1[2 * b]["qt"], r1[2 * b + 1]["qt"]], axis=0)
        kr = np.concatenate([r1[2 * b]["kt"], r1[2 * b + 1]["kt"]], axis=0)
        vt = np.concatenate([r1[2 * b]["vt"], r1[2 * b + 1]["vt"]], axis=0)
        def fmpack(a):
            af = np.ascontiguousarray(a.T)
            af = af.reshape(16, 64, S)[hh * 8:(hh + 1) * 8]
            return np.ascontiguousarray(af.reshape(4, 128, S))
        cols = slice(hh * 512, (hh + 1) * 512)
        abrow = np.concatenate([alpha[hh * 8:(hh + 1) * 8],
                                beta[hh * 8:(hh + 1) * 8]]).reshape(1, 16).astype(np.float32)
        in_maps2.append({
            "qfm": fmpack(qr), "kfm": fmpack(kr),
            "ktm": np.ascontiguousarray(kr[:, cols]),
            "vtm": np.ascontiguousarray(vt[:, cols]),
            "ab_raw": abrow,
        })
    r2 = run_prog("attn_a1_scan", build_attn_a1_scan, in_maps2)

    in_maps3 = []
    for c in range(NCORES):
        b, h = c // 2, c % 2
        o_full = np.concatenate([r2[2 * b]["ot"], r2[2 * b + 1]["ot"]], axis=1)
        ofm = np.ascontiguousarray(o_full[h * RPC:(h + 1) * RPC, :].T)
        xr = np.ascontiguousarray(X[b, h * RPC:(h + 1) * RPC, :].T)
        in_maps3.append({"xr": xr, "ofm": ofm, "wo": ii["L1_attn_a_Wo"]})
    r3 = run_prog("attn_a1_out", build_attn_a1_out, in_maps3)
    Xn = np.empty_like(X)
    for c in range(NCORES):
        b, h = c // 2, c % 2
        Xn[b, h * RPC:(h + 1) * RPC, :] = r3[c]["xo"].T
    return Xn


CTRLW = 4128

def build_gdn_prep(pb):
    xr = pb.inp("xr", [D, RPC])
    xp3 = pb.inp("xp3", [2, NCH * 3 * 128])
    rolec = pb.inp("rolec", [128, 1])
    cw = pb.inp("cw", [D, CTRLW])
    cbm = pb.inp("cbm", [128, 32])
    cbt = pb.inp("cbt", [32, 1])
    convw = pb.inp("convw", [128, NCH * 12])
    y32 = pb.out("y32", [D, RPC])
    qc = pb.out("qc", [D, RPC], BF16)
    kc = pb.out("kc", [D, RPC], BF16)
    vc = pb.out("vc", [D, RPC], BF16)
    g16 = pb.out("g16", [16, RPC])
    lnb16 = pb.out("lnb16", [16, RPC])
    beta16 = pb.out("beta16", [16, RPC])
    gsig = pb.out("gsig", [D, RPC])

    onc, onr = emit_ones(pb)
    cwb = load_w_fm(pb, cw, D, CTRLW, "cw")
    cbms = pb.sb("cbms", [128, 32], F32)
    pb.dma(cbms[:], cbm)
    cbts = pb.sb("cbts", [32, 1], F32)
    pb.dma(cbts[:], cbt)
    cvs = pb.sb("cvs", [128, NCH * 12], F32)
    pb.dma(cvs[:], convw)
    rc = pb.sb("rc", [128, 1], F32)
    pb.dma(rc[:], rolec)

    xbuf = pb.sb("xbuf", [128, NCH * TT], F32)
    ybuf = pb.sb("ybuf", [128, NCH * RPC], BF16)
    y32b = pb.sb("y32b", [128, NCH * TT], F32)
    xb3 = pb.sb("xb3", [128, NCH * 3], F32)
    yb3 = pb.sb("yb3", [128, NCH * 3], BF16)
    bct = pb.sb("bct", [128, 24 * 3], BF16)
    hist = pb.sb("hist", [128, 24 * 3], BF16)
    cbuf = pb.sb("cbuf", [128, TT], F32)
    csrc = pb.sb("csrc", [128, TT + 3], F32)
    cacc = pb.sb("cacc", [128, TT], F32)
    ctmp = pb.sb("ctmp", [128, TT], F32)
    stage = pb.sb("stageP", [128, TT], BF16)
    tl32 = pb.sb("tl32", [32, TT], F32)
    ps_p = pb.ps("ps_pp", [128, TT], F32)
    ps_b3 = pb.ps("ps_b3", [128, 3], F32)
    lnscr = make_ln_scratch(pb, "gp")

    for t0 in range(0, RPC, TT):
        for ch in range(NCH):
            pb.dma(xbuf[:, ch * TT:(ch + 1) * TT],
                   xr[ch * 128:(ch + 1) * 128, t0:t0 + TT])
        xs = [xbuf[:, ch * TT:(ch + 1) * TT] for ch in range(NCH)]
        mb, rb = emit_ln_tile(pb, lnscr, xs,
                              lambda ch: y32b[:, ch * TT:(ch + 1) * TT],
                              onc, onr, TT)
        for ch in range(NCH):
            pb.copy(ybuf[:, ch * RPC + t0: ch * RPC + t0 + TT],
                    y32b[:, ch * TT:(ch + 1) * TT])
            pb.dma(y32[ch * 128:(ch + 1) * 128, t0:t0 + TT],
                   y32b[:, ch * TT:(ch + 1) * TT])
    pb.dma(xb3[:], xp3[0:1, :].rearrange("one (p c) -> p (one c)", p=128))
    emit_ln_tile(pb, lnscr, [xb3[:, ch * 3:(ch + 1) * 3] for ch in range(NCH)],
                 lambda ch: yb3[:, ch * 3:(ch + 1) * 3], onc, onr, 3)
    for och in range(24):
        for dch in range(NCH):
            pb.mm(ps_b3[:], cwb(dch, och * 128, (och + 1) * 128),
                  yb3[:, dch * 3:(dch + 1) * 3],
                  start=(dch == 0), stop=(dch == NCH - 1))
        pb.ts(csrc[:, 0:3], ps_b3[:], cbms[:, och:och + 1], ALU.add)
        pb.tt(bct[:, och * 3:(och + 1) * 3], csrc[:, 0:3], rc[:, 0:1].broadcast_to((128, 3)), ALU.mult)
    pb.copy(hist[:], bct[:])

    for t0 in range(0, RPC, TT):
        for och in range(24):
            for dch in range(NCH):
                pb.mm(ps_p[:], cwb(dch, och * 128, (och + 1) * 128),
                      ybuf[:, dch * RPC + t0: dch * RPC + t0 + TT],
                      start=(dch == 0), stop=(dch == NCH - 1))
            pb.copy(csrc[:, 0:3], hist[:, och * 3:(och + 1) * 3])
            pb.ts(csrc[:, 3:3 + TT], ps_p[:], cbms[:, och:och + 1], ALU.add)
            pb.copy(hist[:, och * 3:(och + 1) * 3], csrc[:, TT:TT + 3])
            which = och // 8
            fch = och % 8
            wc = lambda i: cvs[:, which * NCH * 4 + fch * 4 + i: which * NCH * 4 + fch * 4 + i + 1]
            pb.ts(cacc[:], csrc[:, 0:TT], wc(0), ALU.mult)
            for i in range(1, 4):
                pb.ts(ctmp[:], csrc[:, i:i + TT], wc(i), ALU.mult)
                pb.tt(cacc[:], cacc[:], ctmp[:], ALU.add)
            pb.act(stage[:], cacc[:], AF.Silu)
            OUT = (qc, kc, vc)[which]
            pb.dma(OUT[fch * 128:(fch + 1) * 128, t0:t0 + TT], stage[:])
        for och in range(24, 32):
            for dch in range(NCH):
                pb.mm(ps_p[:], cwb(dch, och * 128, (och + 1) * 128),
                      ybuf[:, dch * RPC + t0: dch * RPC + t0 + TT],
                      start=(dch == 0), stop=(dch == NCH - 1))
            pb.ts(cbuf[:], ps_p[:], cbms[:, och:och + 1], ALU.add)
            pb.act(cbuf[:], cbuf[:], AF.Sigmoid)
            pb.dma(gsig[(och - 24) * 128:(och - 23) * 128, t0:t0 + TT], cbuf[:])
        for dch in range(NCH):
            pb.mm(ps_p[0:32, 0:TT], cwb(dch, 4096, 4128),
                  ybuf[:, dch * RPC + t0: dch * RPC + t0 + TT],
                  start=(dch == 0), stop=(dch == NCH - 1))
        pb.ts(tl32[:], ps_p[0:32, 0:TT], cbts[:, 0:1], ALU.add)
        pb.act(cbuf[0:32, :], tl32[:], AF.Abs)
        pb.act(cbuf[0:32, :], cbuf[0:32, :], AF.Exp, scale=-1.0)
        pb.act(cbuf[0:32, :], cbuf[0:32, :], AF.Ln, bias=pb.constcol(1.0, 32))
        pb.ts(ctmp[0:32, :], tl32[:], 0.0, ALU.min)
        pb.tt(cbuf[0:32, :], ctmp[0:32, :], cbuf[0:32, :], ALU.subtract)
        pb.dma(g16[0:16, t0:t0 + TT], cbuf[0:16, :])
        pb.dma(lnb16[0:16, t0:t0 + TT], cbuf[16:32, :])
        pb.act(cbuf[0:32, :], tl32[:], AF.Sigmoid)
        pb.dma(beta16[0:16, t0:t0 + TT], cbuf[16:32, :])
    return pb


def build_p20_scan(pb):
    NCHN = 256
    ur = pb.inp("ur", [NCHN, S])
    ui = pb.inp("ui", [NCHN, S])
    dl = pb.inp("dl", [128, 2])
    th = pb.inp("th", [128, 2])
    sr_o = pb.out("sr", [NCHN, S])
    si_o = pb.out("si", [NCHN, S])

    urs = pb.sb("urs", [128, 2 * S], F32)
    uis = pb.sb("uis", [128, 2 * S], F32)
    cs = pb.sb("cs", [128, 2 * S], F32)
    sn = pb.sb("sn", [128, 2 * S], F32)
    lam = pb.sb("lam", [128, 2], F32)
    dls = pb.sb("dls", [128, 2], F32)
    ths = pb.sb("ths", [128, 2], F32)
    t1 = pb.sb("t1", [128, S], F32)
    t2 = pb.sb("t2", [128, S], F32)
    ti = pb.sb("ti", [128, 2], mybir.dt.int32)

    pb.dma(urs[:].rearrange("p (g t) -> p g t", g=2),
           ur.rearrange("(g p) t -> p g t", p=128))
    pb.dma(uis[:].rearrange("p (g t) -> p g t", g=2),
           ui.rearrange("(g p) t -> p g t", p=128))
    pb.dma(dls[:], dl)
    pb.dma(ths[:], th)
    pb.act(lam[:], dls[:], AF.Exp)
    pb.act(lam[:], lam[:], AF.Exp, scale=-1.0)
    pb.ts(t1[:, 0:2], ths[:], 1.0 / (2 * math.pi), ALU.mult)
    pb.copy(ti[:], t1[:, 0:2])
    pb.copy(t2[:, 0:2], ti[:])
    pb.tt(t1[:, 0:2], t1[:, 0:2], t2[:, 0:2], ALU.subtract)
    for g in range(2):
        pb.memset(cs[:, g * S: g * S + 1], 1.0, eng="vector")
        pb.memset(sn[:, g * S: g * S + 1], 0.0, eng="vector")
        pb.act(cs[:, g * S + 1: g * S + 2], t1[:, g:g + 1], AF.Sin,
               scale=2 * math.pi, bias=pb.constcol(math.pi / 2.0))
        pb.act(sn[:, g * S + 1: g * S + 2], t1[:, g:g + 1], AF.Sin, scale=2 * math.pi)
    for g in range(2):
        pb.act(t2[:, 0:1], t1[:, g:g + 1], AF.Sin, scale=math.pi)
        pb.act(t2[:, 1:2], t2[:, 0:1], AF.Square)
        pb.ts(t2[:, 1:2], t2[:, 1:2], -2.0, ALU.mult, 1.0, ALU.add)
        pb.copy(cs[:, g * S + 1: g * S + 2], t2[:, 1:2])
    twc = pb.sb("twc", [128, 1], F32)
    tws = pb.sb("tws", [128, 1], F32)
    tta = pb.sb("tta", [128, 1], F32)
    ttb = pb.sb("ttb", [128, 1], F32)
    n = 1
    while n < S:
        m = min(n, S - n)
        for g in range(2):
            base = g * S
            pb.tt(tta[:], cs[:, base + n - 1:base + n], cs[:, base + 1:base + 2], ALU.mult)
            pb.tt(ttb[:], sn[:, base + n - 1:base + n], sn[:, base + 1:base + 2], ALU.mult)
            pb.tt(twc[:], tta[:], ttb[:], ALU.subtract)
            pb.tt(tta[:], sn[:, base + n - 1:base + n], cs[:, base + 1:base + 2], ALU.mult)
            pb.tt(ttb[:], cs[:, base + n - 1:base + n], sn[:, base + 1:base + 2], ALU.mult)
            pb.tt(tws[:], tta[:], ttb[:], ALU.add)
            pb.ts(t1[:, 0:m], cs[:, base:base + m], twc[:, 0:1], ALU.mult)
            pb.ts(t2[:, 0:m], sn[:, base:base + m], tws[:, 0:1], ALU.mult)
            pb.tt(t1[:, 0:m], t1[:, 0:m], t2[:, 0:m], ALU.subtract)
            pb.ts(t2[:, 0:m], cs[:, base:base + m], tws[:, 0:1], ALU.mult)
            pb.ts(cs[:, base + n:base + n + m], sn[:, base:base + m],
                  twc[:, 0:1], ALU.mult)
            pb.tt(sn[:, base + n:base + n + m], t2[:, 0:m],
                  cs[:, base + n:base + n + m], ALU.add)
            pb.copy(cs[:, base + n:base + n + m], t1[:, 0:m])
        n *= 2
    for g in range(2):
        b0 = g * S
        U, V, Cc, Sn = urs[:, b0:b0 + S], uis[:, b0:b0 + S], cs[:, b0:b0 + S], sn[:, b0:b0 + S]
        pb.tt(t1[:, :], V, Sn, ALU.mult)
        pb.tt(t2[:, :], V, Cc, ALU.mult)
        pb.tt(V, U, Sn, ALU.mult)
        pb.tt(V, t2[:, :], V, ALU.subtract)
        pb.tt(U, U, Cc, ALU.mult)
        pb.tt(U, U, t1[:, :], ALU.add)
    lamt = pb.sb("lamt", [128, S], F32)
    for g in range(2):
        b0 = g * S
        pb.memset(lamt[:], 1.0)
        pb.ts(lamt[:], lamt[:], lam[:, g:g + 1], ALU.mult)
        pb.scan(urs[:, b0:b0 + S], lamt[:], urs[:, b0:b0 + S], 0.0)
        pb.scan(uis[:, b0:b0 + S], lamt[:], uis[:, b0:b0 + S], 0.0)
    for g in range(2):
        b0 = g * S
        Zr, Zi, Cc, Sn = urs[:, b0:b0 + S], uis[:, b0:b0 + S], cs[:, b0:b0 + S], sn[:, b0:b0 + S]
        pb.tt(t1[:, :], Zr, Cc, ALU.mult)
        pb.tt(t2[:, :], Zi, Sn, ALU.mult)
        pb.tt(t1[:, :], t1[:, :], t2[:, :], ALU.subtract)
        pb.dma(sr_o[g * 128:(g + 1) * 128, :], t1[:, :])
        pb.tt(t1[:, :], Zi, Cc, ALU.mult)
        pb.tt(t2[:, :], Zr, Sn, ALU.mult)
        pb.tt(t1[:, :], t1[:, :], t2[:, :], ALU.add)
        pb.dma(si_o[g * 128:(g + 1) * 128, :], t1[:, :])
    return pb


def build_gdn_mix(pb):
    sfm = pb.inp("sfm", [D, RPC])
    qcf = pb.inp("qcf", [D, RPC], BF16)
    kcf = pb.inp("kcf", [D, RPC], BF16)
    vcf = pb.inp("vcf", [D, RPC], BF16)
    p20cW = pb.inp("p20cW", [D, 3 * D])
    vnc = pb.inp("vnc", [128, NCH])
    rlc = pb.inp("rlc", [128, NCH])
    qm = pb.out("qm", [D, RPC], BF16)
    km = pb.out("km", [D, RPC], BF16)
    vm = pb.out("vm", [D, RPC], BF16)
    np20 = pb.out("np20", [D, RPC], BF16)

    onc, onr = emit_ones(pb)
    pwb = load_w_fm(pb, p20cW, D, 3 * D, "p20c")
    vncs = load_vec_cols(pb, vnc, D, "vnc")
    ramp = pb.sb("ramp", [128, NCH], F32)
    pb.dma(ramp[:], rlc)
    pb.act(ramp[:], ramp[:], AF.Sigmoid)

    sbu = pb.sb("sbu", [128, NCH * TT], F32)
    npb = pb.sb("npb", [128, NCH * TT], BF16)
    cin = pb.sb("cin", [128, TT], BF16)
    sq = pb.sb("sqm", [128, TT], F32)
    row = pb.sb("rowm", [1, TT], F32)
    stage = pb.sb("stageM", [128, TT], BF16)
    tmp = pb.sb("tmpM", [128, TT], F32)
    ps_r = pb.ps("ps_rm", [1, TT], F32)
    ps_b = pb.ps("ps_bm", [128, TT], F32)
    ps_p = pb.ps("ps_pm", [128, TT], F32)

    for t0 in range(0, RPC, TT):
        for ch in range(NCH):
            pb.dma(sbu[:, ch * TT:(ch + 1) * TT],
                   sfm[ch * 128:(ch + 1) * 128, t0:t0 + TT])
        for ch in range(NCH):
            pb.act(sq[:], sbu[:, ch * TT:(ch + 1) * TT], AF.Square)
            pb.mm(ps_r[0:1, :], onc[:], sq[:],
                  start=(ch == 0), stop=(ch == NCH - 1))
        pb.ts(row[0:1, :], ps_r[0:1, :], 1.0 / D, ALU.mult)
        pb.act(row[0:1, :], row[0:1, :], AF.Sqrt, bias=pb.constcol(1e-6, 1))
        pb.recip(row[0:1, :], row[0:1, :])
        pb.mm(ps_b[:], onr[:], row[0:1, :])
        for ch in range(NCH):
            pb.tt(tmp[:], sbu[:, ch * TT:(ch + 1) * TT], ps_b[:], ALU.mult)
            pb.ts(npb[:, ch * TT:(ch + 1) * TT], tmp[:], vncs[:, ch:ch + 1],
                  ALU.mult)
            pb.copy(stage[:], npb[:, ch * TT:(ch + 1) * TT])
            pb.dma(np20[ch * 128:(ch + 1) * 128, t0:t0 + TT], stage[:])
        for which, (OUT, CIN) in enumerate(((qm, qcf), (km, kcf), (vm, vcf))):
            for fch in range(NCH):
                och = which * NCH + fch
                for dch in range(NCH):
                    pb.mm(ps_p[:], pwb(dch, och * 128, (och + 1) * 128),
                          npb[:, dch * TT:(dch + 1) * TT],
                          start=(dch == 0), stop=(dch == NCH - 1))
                pb.ts(tmp[:], ps_p[:], ramp[:, fch:fch + 1], ALU.mult)
                pb.dma(cin[:], CIN[fch * 128:(fch + 1) * 128, t0:t0 + TT])
                pb.tt(stage[:], tmp[:], cin[:], ALU.add)
                pb.dma(OUT[fch * 128:(fch + 1) * 128, t0:t0 + TT], stage[:])
    return pb


def build_gdn_scan(pb):
    NH, C = 8, 64
    NC_ = S // C
    F = NH * 64
    qfm = pb.inp("qfm", [4, 128, S], BF16)
    kfm = pb.inp("kfm", [4, 128, S], BF16)
    ktm = pb.inp("ktm", [S, F], BF16)
    vtm = pb.inp("vtm", [S, F], BF16)
    gtb = pb.inp("gtb", [S, 24])
    mnt = pb.inp("mnt", [1, F])
    nr = pb.out("nr", [S, F])

    qsb = pb.sb("qsb", [128, 4 * S], BF16)
    ksb = pb.sb("ksb", [128, 4 * S], BF16)
    ktsb = pb.sb("ktsb", [128, (NC_ // 2) * F], BF16)
    vtsb = pb.sb("vtsb", [128, (NC_ // 2) * F], BF16)
    gts = pb.sb("gts", [128, (S // 128) * 24], F32)
    kst = pb.sb("kst", [64, F], BF16)
    vst = pb.sb("vst", [64, F], BF16)
    gst = pb.sb("gst", [64, 24], F32)
    bb = pb.sb("bb", [64, 16], F32)
    eb = pb.sb("ebx", [64, 8], F32)
    ed = pb.sb("edx", [64, 8], F32)
    ebC = pb.sb("ebC", [64, 8], F32)
    blnb = pb.sb("blnb", [64, 8], F32)
    bpre = pb.sb("bpre", [64, 8], F32)
    fdiag = pb.sb("fdiag", [64, F], F32)
    gmA = pb.sb("gmA", [64, F], F32)
    Mb = pb.sb("Mb2", [64, F], BF16)
    MTb = pb.sb("MTb2", [64, F], BF16)
    M1T = pb.sb("M1T", [64, F], BF16)
    POW = [pb.sb(f"gPOW{l}", [64, F], BF16) for l in range(4)]
    POWT = [pb.sb(f"gPOWT{l}", [64, F], BF16) for l in range(4)]
    P32 = pb.sb("gP32", [64, F], BF16)
    Xf = pb.sb("gXf", [64, F], F32)
    Xb = pb.sb("gXb", [64, F], BF16)
    Rall = pb.sb("gRall", [64, NH * 128], BF16)
    Ub = pb.sb("gUb", [64, NH * 128], BF16)
    ngb = pb.sb("gngb", [64, F], BF16)
    dk = pb.sb("gdk", [64, F], BF16)
    Rtb = pb.sb("gRtb", [64, F], BF16)
    Sf = pb.sb("gSf", [64, F], F32)
    Sb0 = pb.sb("gSb0", [64, F], BF16)
    Sbp = pb.sb("gSbp", [128, 4 * 64], BF16)
    ostage = pb.sb("gost", [64, F], F32)
    pstage = pb.sb("gpst", [64, F], F32)
    tmpF = pb.sb("gtmpF", [64, F], F32)
    sq = pb.sb("gsq", [128, 512], F32)
    rowq = pb.sb("growq", [1, 512], F32)
    red8 = pb.sb("gred8", [128, 8], F32)
    mntb = pb.sb("gmntb", [64, F], F32)
    mSLn = pb.sb("mSLn", [64, F], F32)
    mSUn = pb.sb("mSUn", [64, F], F32)
    mSUi = pb.sb("mSUi", [64, F], F32)
    Imask = pb.sb("Imask2", [64, F], F32)
    ones64f = pb.sb("ones64f", [64, 64], F32)
    triuI = pb.sb("triuI", [64, 64], F32)
    sLowS = pb.sb("sLowS", [64, 64], F32)
    onr1 = pb.sb("onr1", [1, 64], F32)
    mrow = pb.sb("mrow", [1, F], F32)
    ps1 = pb.ps("gps1", [64, F], F32)
    ps2 = pb.ps("gps2", [64, F], F32)
    ps3 = pb.ps("gps3", [64, F], F32)
    ps4 = pb.ps("gps4", [64, F], F32)
    psU = pb.ps("gpsU", [64, NH * 128], F32)
    psY = pb.ps("gpsY", [64, F], F32)
    psY2 = pb.ps("gpsY2", [64, F], F32)

    for hp in range(4):
        pb.dma(qsb[:, hp * S:(hp + 1) * S], qfm[hp])
        pb.dma(ksb[:, hp * S:(hp + 1) * S], kfm[hp])
    pb.dma(ktsb[:].rearrange("p (c f) -> p c f", f=F),
           ktm.rearrange("(c p) f -> p c f", p=128))
    pb.dma(vtsb[:].rearrange("p (c f) -> p c f", f=F),
           vtm.rearrange("(c p) f -> p c f", p=128))
    pb.dma(gts[:].rearrange("p (r f) -> p r f", f=24),
           gtb.rearrange("(r p) f -> p r f", p=128))
    pb.dma(mrow[:], mnt)

    pb.memset(mSLn[:], 0.0)
    pb.affine_select(mSLn[:], mSLn[:], [[0, NH], [-1, C]], ALU.is_gt, -1e30,
                     channel_multiplier=1)
    pb.memset(mSUn[:], 0.0)
    pb.affine_select(mSUn[:], mSUn[:], [[0, NH], [1, C]], ALU.is_gt, -1e30,
                     channel_multiplier=-1)
    pb.memset(mSUi[:], 0.0)
    pb.affine_select(mSUi[:], mSUi[:], [[0, NH], [1, C]], ALU.is_ge, -1e30,
                     channel_multiplier=-1)
    pb.memset(Imask[:], 1.0)
    pb.affine_select(Imask[:], Imask[:], [[0, NH], [-1, C]], ALU.is_equal, 0.0,
                     channel_multiplier=1)
    pb.memset(ones64f[:], 1.0)
    pb.memset(onr1[:], 1.0)
    pb.memset(triuI[:], 1.0)
    pb.affine_select(triuI[:], triuI[:], [[1, 64]], ALU.is_ge, 0.0,
                     channel_multiplier=-1)
    pb.memset(sLowS[:], 1.0)
    pb.affine_select(sLowS[:], sLowS[:], [[-1, 64]], ALU.is_gt, 0.0,
                     channel_multiplier=1)
    pb.mm(ps1[:], onr1[:], mrow[:])
    pb.copy(mntb[:], ps1[:])
    pb.memset(Sf[:], 0.0, eng="vector")
    pb.copy(Sb0[:], Sf[:])
    for h in range(NH):
        hp, par = divmod(h, 2)
        pb.copy(Sbp[64 * par:64 * par + 64, hp * 64:(hp + 1) * 64],
                Sf[:, h * 64:(h + 1) * 64])

    def qslice(h, t0, w=C):
        hp, par = divmod(h, 2)
        return qsb[64 * par:64 * par + 64, hp * S + t0: hp * S + t0 + w]

    def kslice(h, t0, w=C):
        hp, par = divmod(h, 2)
        return ksb[64 * par:64 * par + 64, hp * S + t0: hp * S + t0 + w]

    for (buf, nm) in ((qsb, "q"), (ksb, "k")):
        for h in range(NH):
            hp, par = divmod(h, 2)
            for t0 in range(0, S, 512):
                sl = buf[64 * par:64 * par + 64, hp * S + t0: hp * S + t0 + 512]
                pb.act(sq[0:64, :], sl, AF.Square)
                pb.mm(ps1[0:1, 0:512], pb.constcol(1.0)[64 * par:64 * par + 64, 0:1]
                      if False else ones64f[:, 0:1], sq[0:64, :])
                pb.act(rowq[0:1, :], ps1[0:1, 0:512], AF.Sqrt,
                       bias=pb.constcol(1e-6, 1))
                pb.recip(rowq[0:1, :], rowq[0:1, :])
                pb.mm(ps2[0:64, 0:512], onr1[:], rowq[0:1, :])
                pb.tt(sl, sl, ps2[0:64, 0:512], ALU.mult)
    for r in range(S // 128):
        sl = ktsb[:, r * F:(r + 1) * F] if False else None
    for r in range(NC_ // 2):
        sl = ktsb[:, r * F:(r + 1) * F]
        pb.act(sq[:, :], sl, AF.Square)
        pb.reduce(red8[:, 0:8], sq[:].rearrange("p (h d) -> p h d", d=64), ALU.add)
        pb.act(red8[:, 0:8], red8[:, 0:8], AF.Sqrt, bias=pb.constcol(1e-6))
        pb.recip(red8[:, 0:8], red8[:, 0:8])
        pb.tt(sl.rearrange("p (h d) -> p h d", d=64),
              sl.rearrange("p (h d) -> p h d", d=64),
              red8[:, 0:8].unsqueeze(2).broadcast_to((128, 8, 64)), ALU.mult)

    b8 = bb[:, 0:8]
    brev8 = bb[:, 8:16]
    for c in range(NC_):
        t0 = c * C
        cr, cpar = divmod(c, 2)
        pb.copy(kst[:], ktsb[64 * cpar:64 * cpar + 64, cr * F:(cr + 1) * F])
        pb.copy(vst[:], vtsb[64 * cpar:64 * cpar + 64, cr * F:(cr + 1) * F])
        gr, gpar = divmod(t0, 128)
        pb.copy(gst[:], gts[64 * (gpar // 64):64 * (gpar // 64) + 64,
                            gr * 24:(gr + 1) * 24])
        pb.mm(ps1[0:64, 0:8], triuI[:], gst[:, 0:8])
        pb.mm(ps1[0:64, 8:16], sLowS[:], gst[:, 0:8])
        pb.copy(bb[:], ps1[0:64, 0:16])
        pb.act(eb[:], b8, AF.Exp)
        pb.act(ed[:], brev8, AF.Exp)
        pb.tt(ebC[:], eb[:], ed[:], ALU.mult)
        pb.tt(blnb[:], b8, gst[:, 8:16], ALU.add)
        pb.tt(bpre[:], gst[:, 16:24], eb[:], ALU.mult)
        for h in range(NH):
            pb.mm(ps1[:, h * 64:(h + 1) * 64], kslice(h, t0), kslice(h, t0))
        for h in range(NH):
            pb.mm(ps2[:, h * 64:(h + 1) * 64], kslice(h, t0), qslice(h, t0))
        pb.tt(fdiag[:].rearrange("p (h i) -> p h i", i=C), Imask[:].rearrange("p (h i) -> p h i", i=C),
              b8.unsqueeze(2).broadcast_to((64, 8, C)), ALU.mult)
        pb.mm(ps3[:], ones64f[:], fdiag[:])
        pb.tt(fdiag[:].rearrange("p (h i) -> p h i", i=C), Imask[:].rearrange("p (h i) -> p h i", i=C),
              blnb[:].unsqueeze(2).broadcast_to((64, 8, C)), ALU.mult)
        pb.mm(ps4[:], ones64f[:], fdiag[:])
        pb.tt(gmA[:].rearrange("p (h i) -> p h i", i=C),
              blnb[:].unsqueeze(2).broadcast_to((64, 8, C)),
              ps3[:].rearrange("p (h i) -> p h i", i=C), ALU.subtract)
        pb.tt(gmA[:], gmA[:], mSLn[:], ALU.add)
        pb.act(gmA[:], gmA[:], AF.Exp)
        pb.tt(Mb[:], gmA[:], ps1[:], ALU.mult)
        pb.tt(gmA[:].rearrange("p (h i) -> p h i", i=C),
              ps4[:].rearrange("p (h i) -> p h i", i=C),
              b8.unsqueeze(2).broadcast_to((64, 8, C)), ALU.subtract)
        pb.tt(gmA[:], gmA[:], mSUn[:], ALU.add)
        pb.act(gmA[:], gmA[:], AF.Exp)
        pb.tt(MTb[:], gmA[:], ps1[:], ALU.mult)
        pb.tt(gmA[:].rearrange("p (h i) -> p h i", i=C),
              ps3[:].rearrange("p (h i) -> p h i", i=C),
              b8.unsqueeze(2).broadcast_to((64, 8, C)), ALU.subtract)
        pb.tt(gmA[:], gmA[:], mSUi[:], ALU.add)
        pb.act(gmA[:], gmA[:], AF.Exp)
        pb.tt(M1T[:], gmA[:], ps2[:], ALU.mult)
        prev, prevT = Mb, MTb
        for l in range(4):
            for h in range(NH):
                sl = slice(h * 64, (h + 1) * 64)
                pb.mm(ps1[:, sl], prevT[:, sl], prev[:, sl])
            pb.copy(POW[l][:], ps1[:])
            for h in range(NH):
                sl = slice(h * 64, (h + 1) * 64)
                pb.mm(ps1[:, sl], prev[:, sl], prevT[:, sl])
            pb.copy(POWT[l][:], ps1[:])
            prev, prevT = POW[l], POWT[l]
        for h in range(NH):
            sl = slice(h * 64, (h + 1) * 64)
            pb.mm(ps1[:, sl], POWT[3][:, sl], POW[3][:, sl])
        pb.copy(P32[:], ps1[:])
        pb.tt(Xf[:], Imask[:], MTb[:], ALU.subtract)
        pb.copy(Xb[:], Xf[:])
        for pw in (POW[0], POW[1], POW[2], POW[3], P32):
            for h in range(NH):
                sl = slice(h * 64, (h + 1) * 64)
                pb.mm(ps1[:, sl], pw[:, sl], Xb[:, sl])
            pb.tt(Xf[:], Xf[:], ps1[:], ALU.add)
            pb.copy(Xb[:], Xf[:])
        pb.tt(Rall[:].rearrange("p (h z) -> p h z", z=128)[:, :, 0:64],
              vst[:].rearrange("p (h z) -> p h z", z=64),
              gst[:, 16:24].unsqueeze(2).broadcast_to((64, 8, C)), ALU.mult)
        pb.tt(Rall[:].rearrange("p (h z) -> p h z", z=128)[:, :, 64:128],
              kst[:].rearrange("p (h z) -> p h z", z=64),
              bpre[:].unsqueeze(2).broadcast_to((64, 8, C)), ALU.mult)
        for h in range(NH):
            pb.mm(psU[:, h * 128:(h + 1) * 128], Xb[:, h * 64:(h + 1) * 64],
                  Rall[:, h * 128:(h + 1) * 128])
        pb.copy(Ub[:], psU[:])
        U0 = lambda h: Ub[:, h * 128: h * 128 + 64]
        W0 = lambda h: Ub[:, h * 128 + 64: (h + 1) * 128]
        for h in range(NH):
            pb.mm(ps2[:, h * 64:(h + 1) * 64], W0(h), M1T[:, h * 64:(h + 1) * 64])
        pb.ts(ngb[:], ps2[:], -1.0, ALU.mult)
        pb.tt(dk[:].rearrange("p (h z) -> p h z", z=64),
              kst[:].rearrange("p (h z) -> p h z", z=64),
              ed[:].unsqueeze(2).broadcast_to((64, 8, C)), ALU.mult)
        for h in range(NH):
            pb.mm(ps3[:, h * 64:(h + 1) * 64], dk[:, h * 64:(h + 1) * 64], U0(h))
        pb.copy(pstage[:], ps3[:])
        for h in range(NH):
            pb.mm(ps4[:, h * 64:(h + 1) * 64], W0(h), dk[:, h * 64:(h + 1) * 64])
        pb.tt(tmpF[:].rearrange("p (h i) -> p h i", i=C),
              Imask[:].rearrange("p (h i) -> p h i", i=C),
              ebC[:].unsqueeze(2).broadcast_to((64, 8, C)), ALU.mult)
        pb.tt(Rtb[:], tmpF[:], ps4[:], ALU.subtract)
        for h in range(NH):
            pb.mm(psY[:, h * 64:(h + 1) * 64], M1T[:, h * 64:(h + 1) * 64], U0(h),
                  start=True, stop=False)
            pb.mm(psY[:, h * 64:(h + 1) * 64], ngb[:, h * 64:(h + 1) * 64],
                  Sb0[:, h * 64:(h + 1) * 64], start=False, stop=True)
        for h in range(NH):
            hp, par = divmod(h, 2)
            pb.mm(psY2[:, h * 64:(h + 1) * 64], qslice(h, t0),
                  Sbp[64 * par:64 * par + 64, hp * 64:(hp + 1) * 64])
        for h in range(NH):
            sl = slice(h * 64, (h + 1) * 64)
            pb.ts(tmpF[:, sl], psY2[:, sl], eb[:, h:h + 1], ALU.mult)
            pb.tt(ostage[:, sl], psY[:, sl], tmpF[:, sl], ALU.add)
        pb.ts(ostage[:], ostage[:], 0.125, ALU.mult)
        pb.act(sq[0:64, :], ostage[:], AF.Square)
        pb.reduce(red8[0:64, 0:8], sq[0:64, :].rearrange("p (h d) -> p h d", d=64),
                  ALU.add)
        pb.ts(red8[0:64, 0:8], red8[0:64, 0:8], 1.0 / 64.0, ALU.mult)
        pb.act(red8[0:64, 0:8], red8[0:64, 0:8], AF.Sqrt, bias=pb.constcol(1e-6, 64))
        pb.recip(red8[0:64, 0:8], red8[0:64, 0:8])
        pb.tt(ostage[:].rearrange("p (h d) -> p h d", d=64),
              ostage[:].rearrange("p (h d) -> p h d", d=64),
              red8[0:64, 0:8].unsqueeze(2).broadcast_to((64, 8, 64)), ALU.mult)
        pb.tt(ostage[:], ostage[:], mntb[:], ALU.mult)
        pb.dma(nr[t0:t0 + C, :], ostage[:])
        for h in range(NH):
            pb.mm(psY2[:, h * 64:(h + 1) * 64], Rtb[:, h * 64:(h + 1) * 64],
                  Sb0[:, h * 64:(h + 1) * 64])
        pb.tt(Sf[:], pstage[:], psY2[:], ALU.add)
        pb.copy(Sb0[:], Sf[:])
        for h in range(NH):
            hp, par = divmod(h, 2)
            pb.copy(Sbp[64 * par:64 * par + 64, hp * 64:(hp + 1) * 64],
                    Sf[:, h * 64:(h + 1) * 64])
    return pb


def build_gdn_out(pb):
    nrf = pb.inp("nrf", [D, RPC])
    npf = pb.inp("npf", [D, RPC], BF16)
    gsig = pb.inp("gsig", [D, RPC])
    outw = pb.inp("outw", [2 * D, D])
    outb = pb.inp("outb", [128, NCH])
    xr = pb.inp("xr", [D, RPC])
    xo = pb.out("xo", [D, RPC])

    owb = load_w_fm(pb, outw, 2 * D, D, "outw")
    obc = load_vec_cols(pb, outb, D, "outb")
    nrb = pb.sb("nrb", [128, NCH * TT], BF16)
    npb = pb.sb("npb2", [128, NCH * TT], BF16)
    gb = pb.sb("gb2", [128, TT], F32)
    obuf = pb.sb("obuf2", [128, TT], F32)
    ps_p = pb.ps("ps_go", [128, TT], F32)

    for t0 in range(0, RPC, TT):
        for ch in range(NCH):
            pb.cast_dma(nrb[:, ch * TT:(ch + 1) * TT],
                        nrf[ch * 128:(ch + 1) * 128, t0:t0 + TT])
            pb.dma(npb[:, ch * TT:(ch + 1) * TT],
                   npf[ch * 128:(ch + 1) * 128, t0:t0 + TT])
        for och in range(NCH):
            for dch in range(NCH):
                pb.mm(ps_p[:], owb(dch, och * 128, (och + 1) * 128),
                      nrb[:, dch * TT:(dch + 1) * TT],
                      start=(dch == 0), stop=False)
            for dch in range(NCH):
                pb.mm(ps_p[:], owb(NCH + dch, och * 128, (och + 1) * 128),
                      npb[:, dch * TT:(dch + 1) * TT],
                      start=False, stop=(dch == NCH - 1))
            pb.ts(obuf[:], ps_p[:], obc[:, och:och + 1], ALU.add)
            pb.dma(gb[:], gsig[och * 128:(och + 1) * 128, t0:t0 + TT])
            pb.tt(obuf[:], obuf[:], gb[:], ALU.mult)
            pb.dma(gb[:], xr[och * 128:(och + 1) * 128, t0:t0 + TT])
            pb.tt(obuf[:], obuf[:], gb[:], ALU.add)
            pb.dma(xo[och * 128:(och + 1) * 128, t0:t0 + TT], obuf[:])
    return pb


def sub_l0_attn_b(X, inputs):
    ii = {k: np.asarray(v) for k, v in inputs.items()}
    cw = ii["L0_attn_b_ctrl_W"]
    cb = ii["L0_attn_b_ctrl_b"]
    perm = np.concatenate([np.arange(0, 3 * D), np.arange(3 * D + 32, 3 * D + 32 + D),
                           np.arange(3 * D, 3 * D + 32)])
    cw_r = np.ascontiguousarray(cw[:, perm])
    cb_r = np.ascontiguousarray(cb[perm])
    cbm = vec_cols_host(cb_r[:4096])
    cbt = np.ascontiguousarray(cb_r[4096:].reshape(32, 1))
    convw = np.zeros((128, NCH * 12), np.float32)
    for wi, wname in enumerate(["qconv_w", "kconv_w", "vconv_w"]):
        cwv = ii[f"L0_attn_b_{wname}"]
        for fch in range(NCH):
            for tap in range(4):
                convw[:, wi * NCH * 4 + fch * 4 + tap] = cwv[fch * 128:(fch + 1) * 128, tap]
    in_maps = []
    for c in range(NCORES):
        b, h = c // 2, c % 2
        xr = np.ascontiguousarray(X[b, h * RPC:(h + 1) * RPC, :].T)
        xp3 = np.zeros((2, NCH * 3 * 128), np.float32)
        xp3[0] = np.ascontiguousarray(
            X[b, RPC - 3:RPC, :].T.reshape(NCH, 128, 3).transpose(1, 0, 2)).flatten()
        role = np.full((128, 1), float(h), np.float32)
        in_maps.append({"xr": xr, "xp3": xp3, "rolec": role, "cw": cw_r,
                        "cbm": cbm, "cbt": cbt, "convw": convw})
    rA = run_prog("gdn_prep", build_gdn_prep, in_maps)

    dl = ii["L0_attn_b_p20_decay_log"]
    th = ii["L0_attn_b_p20_theta"]
    in_maps2 = []
    for c in range(NCORES):
        b, j = c // 2, c % 2
        yfull = np.concatenate([rA[2 * b]["y32"], rA[2 * b + 1]["y32"]], axis=1)
        blk = yfull[512 * j: 512 * (j + 1), :]
        ur = np.ascontiguousarray(blk[0::2, :])
        ui = np.ascontiguousarray(blk[1::2, :])
        dls = dl[256 * j: 256 * (j + 1)].reshape(2, 128).T
        ths = th[256 * j: 256 * (j + 1)].reshape(2, 128).T
        in_maps2.append({"ur": ur, "ui": ui,
                         "dl": np.ascontiguousarray(dls),
                         "th": np.ascontiguousarray(ths)})
    rB = run_prog("p20_scan", build_p20_scan, in_maps2)

    in_maps3 = []
    for c in range(NCORES):
        b, h = c // 2, c % 2
        sfull = np.zeros((D, S), np.float32)
        for j in range(2):
            cc = 2 * b + j
            sfull[512 * j: 512 * (j + 1):, :][0::2] = rB[cc]["sr"]
            sfull[512 * j: 512 * (j + 1):, :][1::2] = rB[cc]["si"]
        sfm = np.ascontiguousarray(sfull[:, h * RPC:(h + 1) * RPC])
        in_maps3.append({
            "sfm": sfm, "qcf": rA[c]["qc"], "kcf": rA[c]["kc"], "vcf": rA[c]["vc"],
            "p20cW": ii["L0_attn_b_p20c_W"],
            "vnc": vec_cols_host(ii["L0_attn_b_vnorm_scale"]),
            "rlc": vec_cols_host(ii["L0_attn_b_ramp_logit"]),
        })
    rC = run_prog("gdn_mix", build_gdn_mix, in_maps3)

    in_maps4 = []
    for c in range(NCORES):
        b, hh = c // 2, c % 2
        qm = np.concatenate([rC[2 * b]["qm"], rC[2 * b + 1]["qm"]], axis=1)
        km = np.concatenate([rC[2 * b]["km"], rC[2 * b + 1]["km"]], axis=1)
        vm = np.concatenate([rC[2 * b]["vm"], rC[2 * b + 1]["vm"]], axis=1)
        def fmpack(afm):
            af = afm.reshape(16, 64, S)[hh * 8:(hh + 1) * 8]
            return np.ascontiguousarray(af.reshape(4, 128, S))
        rows = slice(hh * 512, (hh + 1) * 512)
        g16 = np.concatenate([rA[2 * b]["g16"], rA[2 * b + 1]["g16"]], axis=1)
        lnb16 = np.concatenate([rA[2 * b]["lnb16"], rA[2 * b + 1]["lnb16"]], axis=1)
        be16 = np.concatenate([rA[2 * b]["beta16"], rA[2 * b + 1]["beta16"]], axis=1)
        hs = slice(hh * 8, (hh + 1) * 8)
        gtb = np.concatenate([g16[hs].T, lnb16[hs].T, be16[hs].T], axis=1)
        in_maps4.append({
            "qfm": fmpack(qm), "kfm": fmpack(km),
            "ktm": np.ascontiguousarray(km[rows].T),
            "vtm": np.ascontiguousarray(vm[rows].T),
            "gtb": np.ascontiguousarray(gtb.astype(np.float32)),
            "mnt": np.ascontiguousarray(
                np.tile(ii["L0_attn_b_mnorm_scale"], 8).reshape(1, 512).astype(np.float32)),
        })
    rD = run_prog("gdn_scan", build_gdn_scan, in_maps4)

    in_maps5 = []
    for c in range(NCORES):
        b, h = c // 2, c % 2
        nrfull = np.concatenate([rD[2 * b]["nr"], rD[2 * b + 1]["nr"]], axis=1)
        nrf = np.ascontiguousarray(nrfull[h * RPC:(h + 1) * RPC, :].T)
        xr = np.ascontiguousarray(X[b, h * RPC:(h + 1) * RPC, :].T)
        in_maps5.append({
            "nrf": nrf.astype(np.float32), "npf": rC[c]["np20"],
            "gsig": rA[c]["gsig"], "outw": ii["L0_attn_b_out_W"],
            "outb": vec_cols_host(ii["L0_attn_b_out_b"]), "xr": xr,
        })
    r5 = run_prog("gdn_out", build_gdn_out, in_maps5)
    Xn = np.empty_like(X)
    for c in range(NCORES):
        b, h = c // 2, c % 2
        Xn[b, h * RPC:(h + 1) * RPC, :] = r5[c]["xo"].T
    return Xn


PAIRS = [[0, 1], [2, 3], [4, 5], [6, 7]]


def emit_ag(pb, dst, src):
    pb.collective("AllGather", ALU.bypass, PAIRS, [src.opt()], [dst.opt()])


def emit_transpose_mm(pb, pref, dst, src, rows, cols, dt):
    BW = min(rows, 2048)
    with pb.stage(pref, {}):
        ident = pb.sb("ident", [128, 128], dt)
        pb.memset(ident[:], 1.0)
        pb.affine_select(ident[:], ident[:], [[-1, 128]], ALU.is_equal, 0.0,
                         channel_multiplier=1)
        srcb = pb.sb("srcb", [128, 128], dt)
        dstb = pb.sb("dstb", [128, BW], dt)
        ps = pb.ps("ps", [128, 128], F32)
        for c0 in range(0, cols, 128):
            for rr0 in range(0, rows, BW):
                for r0 in range(rr0, rr0 + BW, 128):
                    pb.dma(srcb[:], src[r0:r0 + 128, c0:c0 + 128])
                    pb.mm(ps[:], srcb[:], ident[:])
                    pb.copy(dstb[:, r0 - rr0:r0 - rr0 + 128], ps[:])
                pb.dma(dst[c0:c0 + 128, rr0:rr0 + BW], dstb[:])


def run_stage(pb, pref, build_fn, io):
    with pb.stage(pref, io):
        build_fn(pb)


def build_attn_a_qkv_f(pb):
    build_attn_a_qkv(pb)
    rkc = pb.out("rkcol", [128, RPC // 128])
    rk = pb.inp("rk", [1, RPC])
    one1 = pb.sb("one1", [1, 1], F32)
    pb.memset(one1[:], 1.0)
    rrow = pb.sb("rrow", [1, RPC], F32)
    pb.dma(rrow[:], rk)
    ps = pb.ps("ps_rk", [128, RPC // 128], F32)
    stg = pb.sb("stg_rk", [128, RPC // 128], F32)
    for kt in range(RPC // 128):
        pb.mm(ps[:, kt:kt + 1], rrow[0:1, kt * 128:(kt + 1) * 128], one1[:])
    pb.copy(stg[:], ps[:])
    pb.dma(rkc, stg[:])
    return pb


def build_attn_a_av_f(pb):
    NKT = S // 128
    xr = pb.inp("xr", [D, RPC])
    Qb = pb.inp("Qb", [D, RPC], BF16)
    Kg = pb.inp("Kg", [2, D * RPC], BF16)
    Vtg = pb.inp("Vtg", [2 * RPC, D], BF16)
    rqv = pb.inp("rq", [1, RPC])
    rkg = pb.inp("rkg", [2, 128 * (RPC // 128)])
    woT = pb.inp("woT", [D, D])
    bo = pb.inp("bo", [128, NCH])
    xo = pb.out("xo", [D, RPC])

    onc, onr = emit_ones(pb)
    oncb = pb.sb("oncb", [128, 1], BF16)
    pb.memset(oncb[:], 1.0)
    wo = load_w_fm(pb, woT, D, D, "wo")
    boc = load_vec_cols(pb, bo, D, "bo")

    qbuf = pb.sb("qbuf", [128, NCH * RPC], BF16)
    kfull = pb.sb("kfull", [128, NCH * S], BF16)
    vtile = pb.sb("vtile", [128, 512], BF16)
    rkcol = pb.sb("rkcol", [128, NKT], F32)
    rqrow = pb.sb("rqrow", [1, RPC], F32)
    rqb = pb.sb("rqb", [128, TT], F32)
    ebuf = pb.sb("ebuf", [128, NKT * TT], BF16)
    zbuf = pb.sb("zbuf", [128, TT], F32)
    sbuf2 = pb.sb("sbuf2", [128, TT], F32)
    att = pb.sb("att", [128, NCH * TT], BF16)
    dben = pb.sb("dben", [128, TT], F32)
    drow = pb.sb("drow", [1, TT], F32)
    obuf = pb.sb("obuf", [128, TT], F32)
    xbuf = pb.sb("xbuf", [128, TT], F32)
    ps_s = pb.ps("ps_s", [128, TT], F32)
    ps_d = pb.ps("ps_d", [1, TT], F32)
    ps_b = pb.ps("ps_b", [128, TT], F32)
    ps_av = pb.ps("ps_av", [128, 4 * TT], F32)
    ps_o = pb.ps("ps_o", [128, TT], F32)

    pb.dma(qbuf[:].rearrange("p (dc t) -> p dc t", dc=NCH),
           Qb.rearrange("(dc p) t -> p dc t", p=128))
    for h in range(2):
        pb.dma(kfull[:].rearrange("p (dc hf t) -> p dc hf t", dc=NCH, hf=2)[:, :, h, :],
               Kg[h:h + 1, :].rearrange("one (dc p t) -> p (one dc) t", p=128, t=RPC))
        pb.dma(rkcol[:, h * (RPC // 128):(h + 1) * (RPC // 128)],
               rkg[h:h + 1, :].rearrange("one (p c) -> p (one c)", p=128))
    pb.dma(rqrow[:], rqv)

    mpi = -math.pi / 2.0
    for t0 in range(0, RPC, TT):
        pb.mm(ps_b[:], onr[:], rqrow[0:1, t0:t0 + TT])
        pb.copy(rqb[:], ps_b[:])
        for kt in range(NKT):
            for dch in range(NCH):
                pb.mm(ps_s[:], kfull[:, dch * S + kt * 128: dch * S + (kt + 1) * 128],
                      qbuf[:, dch * RPC + t0: dch * RPC + t0 + TT],
                      start=(dch == 0), stop=(dch == NCH - 1))
            pb.ts(zbuf[:], ps_s[:], rkcol[:, kt:kt + 1], ALU.mult)
            pb.tt(zbuf[:], zbuf[:], rqb[:], ALU.mult)
            pb.act(zbuf[:], zbuf[:], AF.Abs)
            pb.act(sbuf2[:], zbuf[:], AF.Sin, bias=pb.constcol(mpi), scale=math.pi)
            pb.tt(zbuf[:], sbuf2[:], ps_s[:], ALU.mult)
            pb.act(ebuf[:, kt * TT:(kt + 1) * TT], zbuf[:], AF.Exp, scale=-1.0 / 32.0)
            pb.mm(ps_d[0:1, :], oncb[:], ebuf[:, kt * TT:(kt + 1) * TT],
                  start=(kt == 0), stop=(kt == NKT - 1))
        pb.recip(drow[0:1, :], ps_d[0:1, :])
        pb.mm(ps_b[:], onr[:], drow[0:1, :])
        pb.copy(dben[:], ps_b[:])
        for rnd in range(2):
            for kt in range(NKT):
                pb.dma(vtile[:], Vtg[kt * 128:(kt + 1) * 128,
                                     rnd * 512:(rnd + 1) * 512])
                for dv in range(4):
                    pb.mm(ps_av[:, dv * TT:(dv + 1) * TT],
                          vtile[:, dv * 128:(dv + 1) * 128],
                          ebuf[:, kt * TT:(kt + 1) * TT],
                          start=(kt == 0), stop=(kt == NKT - 1))
            for dv in range(4):
                pb.tt(att[:, (rnd * 4 + dv) * TT:(rnd * 4 + dv + 1) * TT],
                      ps_av[:, dv * TT:(dv + 1) * TT], dben[:], ALU.mult)
        for och in range(NCH):
            for dv in range(NCH):
                pb.mm(ps_o[:], wo(dv, och * 128, (och + 1) * 128),
                      att[:, dv * TT:(dv + 1) * TT],
                      start=(dv == 0), stop=(dv == NCH - 1))
            pb.ts(obuf[:], ps_o[:], boc[:, och:och + 1], ALU.add)
            pb.dma(xbuf[:], xr[och * 128:(och + 1) * 128, t0:t0 + TT])
            pb.tt(obuf[:], obuf[:], xbuf[:], ALU.add)
            pb.dma(xo[och * 128:(och + 1) * 128, t0:t0 + TT], obuf[:])
    return pb


def build_p20_local(pb):
    NG = 4
    T = RPC
    y32 = pb.inp("y32", [D, RPC])
    dl = pb.inp("dl", [128, NG])
    th = pb.inp("th", [128, NG])
    rolec = pb.inp("rolec", [128, 1])
    zml = pb.inp("zml", [1, 128 * 2 * NG])
    zmg = pb.inp("zmg", [2, 128 * 2 * NG])
    sfm = pb.out("sfm", [D, RPC])

    urs = pb.sb("urs", [128, NG * T], F32)
    uis = pb.sb("uis", [128, NG * T], F32)
    cs = pb.sb("cs", [128, NG * T], F32)
    sn = pb.sb("sn", [128, NG * T], F32)
    t1 = pb.sb("t1", [128, T], F32)
    t2 = pb.sb("t2", [128, T], F32)
    lam = pb.sb("lam", [128, NG], F32)
    lamt = pb.sb("lamt", [128, T], F32)
    ti = pb.sb("ti", [128, NG], mybir.dt.int32)
    twc = pb.sb("twc", [128, 1], F32)
    tws = pb.sb("tws", [128, 1], F32)
    tta = pb.sb("tta", [128, 1], F32)
    ttb = pb.sb("ttb", [128, 1], F32)
    rc = pb.sb("rc", [128, 1], F32)
    zst = pb.sb("zst", [128, 2 * NG], F32)
    ych = pb.sb("ych", [128, 2 * TT], F32)
    Ee = pb.sb("Ee", [128, 64], F32)
    Eo = pb.sb("Eo", [128, 64], F32)
    Fr = pb.sb("Fr", [128, 128], F32)
    Fi = pb.sb("Fi", [128, 128], F32)
    ps_a = pb.ps("ps_a", [128, TT], F32)

    pb.dma(rc[:], rolec)
    pb.dma(lam[:], dl)
    pb.act(lam[:], lam[:], AF.Exp)
    pb.act(lam[:], lam[:], AF.Exp, scale=-1.0)
    pb.dma(t1[:, 0:NG], th)
    pb.memset(Ee[:], 1.0)
    pb.affine_select(Ee[:], Ee[:], [[-2, 64]], ALU.is_equal, 0.0, channel_multiplier=1)
    pb.memset(Eo[:], 1.0)
    pb.affine_select(Eo[:], Eo[:], [[-2, 64]], ALU.is_equal, 0.0, channel_multiplier=1,
                     base=-1)
    for b0 in (0, 64):
        pb.memset(Fr[b0:b0 + 64, :], 1.0)
        pb.affine_select(Fr[b0:b0 + 64, :], Fr[b0:b0 + 64, :], [[1, 128]],
                         ALU.is_equal, 0.0, channel_multiplier=-2)
        pb.memset(Fi[b0:b0 + 64, :], 1.0)
        pb.affine_select(Fi[b0:b0 + 64, :], Fi[b0:b0 + 64, :], [[1, 128]],
                         ALU.is_equal, 0.0, channel_multiplier=-2, base=-1)
    pb.ts(t1[:, 0:NG], t1[:, 0:NG], 1.0 / (2 * math.pi), ALU.mult)
    pb.copy(ti[:], t1[:, 0:NG])
    pb.copy(t2[:, 0:NG], ti[:])
    pb.tt(t1[:, 0:NG], t1[:, 0:NG], t2[:, 0:NG], ALU.subtract)
    for g in range(NG):
        base = g * T
        pb.memset(cs[:, base:base + 1], 1.0, eng="vector")
        pb.memset(sn[:, base:base + 1], 0.0, eng="vector")
        pb.act(sn[:, base + 1:base + 2], t1[:, g:g + 1], AF.Sin, scale=2 * math.pi)
        pb.act(t2[:, 0:1], t1[:, g:g + 1], AF.Sin, scale=math.pi)
        pb.act(t2[:, 1:2], t2[:, 0:1], AF.Square)
        pb.ts(t2[:, 1:2], t2[:, 1:2], -2.0, ALU.mult, 1.0, ALU.add)
        pb.copy(cs[:, base + 1:base + 2], t2[:, 1:2])
    n = 1
    while n < T:
        m = min(n, T - n)
        for g in range(NG):
            base = g * T
            pb.tt(tta[:], cs[:, base + n - 1:base + n], cs[:, base + 1:base + 2], ALU.mult)
            pb.tt(ttb[:], sn[:, base + n - 1:base + n], sn[:, base + 1:base + 2], ALU.mult)
            pb.tt(twc[:], tta[:], ttb[:], ALU.subtract)
            pb.tt(tta[:], sn[:, base + n - 1:base + n], cs[:, base + 1:base + 2], ALU.mult)
            pb.tt(ttb[:], cs[:, base + n - 1:base + n], sn[:, base + 1:base + 2], ALU.mult)
            pb.tt(tws[:], tta[:], ttb[:], ALU.add)
            pb.ts(t1[:, 0:m], cs[:, base:base + m], twc[:, 0:1], ALU.mult)
            pb.ts(t2[:, 0:m], sn[:, base:base + m], tws[:, 0:1], ALU.mult)
            pb.tt(t1[:, 0:m], t1[:, 0:m], t2[:, 0:m], ALU.subtract)
            pb.ts(t2[:, 0:m], cs[:, base:base + m], tws[:, 0:1], ALU.mult)
            pb.ts(cs[:, base + n:base + n + m], sn[:, base:base + m], twc[:, 0:1], ALU.mult)
            pb.tt(sn[:, base + n:base + n + m], t2[:, 0:m], cs[:, base + n:base + n + m],
                  ALU.add)
            pb.copy(cs[:, base + n:base + n + m], t1[:, 0:m])
        n *= 2
    for g in range(NG):
        base = g * T
        pb.tt(tta[:], cs[:, base + T - 1:base + T], cs[:, base + 1:base + 2], ALU.mult)
        pb.tt(ttb[:], sn[:, base + T - 1:base + T], sn[:, base + 1:base + 2], ALU.mult)
        pb.tt(twc[:], tta[:], ttb[:], ALU.subtract)
        pb.tt(tta[:], sn[:, base + T - 1:base + T], cs[:, base + 1:base + 2], ALU.mult)
        pb.tt(ttb[:], cs[:, base + T - 1:base + T], sn[:, base + 1:base + 2], ALU.mult)
        pb.tt(tws[:], tta[:], ttb[:], ALU.add)
        pb.ts(twc[:], twc[:], -1.0, ALU.mult, 1.0, ALU.add)
        pb.tt(twc[:], rc[:, 0:1], twc[:], ALU.mult)
        pb.ts(twc[:], twc[:], -1.0, ALU.mult, 1.0, ALU.add)
        pb.tt(tws[:], rc[:, 0:1], tws[:], ALU.mult)
        pb.ts(t1[:, :], cs[:, base:base + T], twc[:, 0:1], ALU.mult)
        pb.ts(t2[:, :], sn[:, base:base + T], tws[:, 0:1], ALU.mult)
        pb.tt(t1[:, :], t1[:, :], t2[:, :], ALU.subtract)
        pb.ts(t2[:, :], cs[:, base:base + T], tws[:, 0:1], ALU.mult)
        pb.ts(cs[:, base:base + T], sn[:, base:base + T], twc[:, 0:1], ALU.mult)
        pb.tt(sn[:, base:base + T], t2[:, :], cs[:, base:base + T], ALU.add)
        pb.copy(cs[:, base:base + T], t1[:, :])
    for g in range(NG):
        for c0 in range(0, T, TT):
            pb.dma(ych[:, 0:TT], y32[(2 * g) * 128:(2 * g + 1) * 128, c0:c0 + TT])
            pb.dma(ych[:, TT:2 * TT], y32[(2 * g + 1) * 128:(2 * g + 2) * 128, c0:c0 + TT])
            for (E_, dst) in ((Ee, urs), (Eo, uis)):
                pb.mm(ps_a[0:64, :], E_[:], ych[:, 0:TT])
                pb.copy(dst[0:64, g * T + c0: g * T + c0 + TT], ps_a[0:64, :])
                pb.mm(ps_a[0:64, :], E_[:], ych[:, TT:2 * TT])
                pb.copy(dst[64:128, g * T + c0: g * T + c0 + TT], ps_a[0:64, :])
    for g in range(NG):
        b0 = g * T
        U, V, Cc, Sn = urs[:, b0:b0 + T], uis[:, b0:b0 + T], cs[:, b0:b0 + T], sn[:, b0:b0 + T]
        pb.tt(t1[:, :], V, Sn, ALU.mult)
        pb.tt(t2[:, :], V, Cc, ALU.mult)
        pb.tt(V, U, Sn, ALU.mult)
        pb.tt(V, t2[:, :], V, ALU.subtract)
        pb.tt(U, U, Cc, ALU.mult)
        pb.tt(U, U, t1[:, :], ALU.add)
    for g in range(NG):
        b0 = g * T
        pb.memset(lamt[:], 1.0)
        pb.ts(lamt[:], lamt[:], lam[:, g:g + 1], ALU.mult)
        pb.scan(t1[:, :], lamt[:], urs[:, b0:b0 + T], 0.0)
        pb.copy(zst[:, 2 * g:2 * g + 1], t1[:, T - 1:T])
        pb.scan(t1[:, :], lamt[:], uis[:, b0:b0 + T], 0.0)
        pb.copy(zst[:, 2 * g + 1:2 * g + 2], t1[:, T - 1:T])
    pb.dma(zml.rearrange("one (p c) -> (one p) c", p=128), zst[:])
    emit_ag(pb, zmg, zml)
    pb.dma(zst[:], zmg[0:1, :].rearrange("one (p c) -> p (one c)", p=128))
    pb.ts(zst[:], zst[:], rc[:, 0:1], ALU.mult)
    for g in range(NG):
        b0 = g * T
        pb.memset(lamt[:], 1.0)
        pb.ts(lamt[:], lamt[:], lam[:, g:g + 1], ALU.mult)
        pb.scan(urs[:, b0:b0 + T], lamt[:], urs[:, b0:b0 + T], zst[:, 2 * g:2 * g + 1])
        pb.scan(uis[:, b0:b0 + T], lamt[:], uis[:, b0:b0 + T], zst[:, 2 * g + 1:2 * g + 2])
    for g in range(NG):
        b0 = g * T
        Zr, Zi, Cc, Sn = urs[:, b0:b0 + T], uis[:, b0:b0 + T], cs[:, b0:b0 + T], sn[:, b0:b0 + T]
        pb.tt(t1[:, :], Zr, Cc, ALU.mult)
        pb.tt(t2[:, :], Zi, Sn, ALU.mult)
        pb.tt(t1[:, :], t1[:, :], t2[:, :], ALU.subtract)
        pb.tt(t2[:, :], Zi, Cc, ALU.mult)
        pb.tt(Zi, Zr, Sn, ALU.mult)
        pb.tt(Zi, t2[:, :], Zi, ALU.add)
        pb.copy(Zr, t1[:, :])
    for och in range(NCH):
        g = och // 2
        b0 = 64 * (och % 2)
        for c0 in range(0, T, TT):
            pb.mm(ps_a[:], Fr[b0:b0 + 64, :],
                  urs[b0:b0 + 64, g * T + c0: g * T + c0 + TT],
                  start=True, stop=False)
            pb.mm(ps_a[:], Fi[b0:b0 + 64, :],
                  uis[b0:b0 + 64, g * T + c0: g * T + c0 + TT],
                  start=False, stop=True)
            pb.copy(ych[:, 0:TT], ps_a[:])
            pb.dma(sfm[och * 128:(och + 1) * 128, c0:c0 + TT], ych[:, 0:TT])
    return pb


def build_delta_scan_local(pb, gated):
    NH, C = 16, 64
    T = RPC
    NC2 = T // C
    F = NH * 64
    NHP = NH // 2
    qfm = pb.inp("qfm", [D, T], BF16)
    kfm = pb.inp("kfm", [D, T], BF16)
    ktm = pb.inp("ktm", [T, F], BF16)
    vtm = pb.inp("vtm", [T, F], BF16)
    rolec = pb.inp("rolec", [128, 1])
    sml = pb.inp("sml", [1, 64 * F])
    smg = pb.inp("smg", [2, 64 * F])
    spY1 = pb.inp("spY1", [NC2 * 64, F])
    spP = pb.inp("spP", [NC2 * 64, F])
    spC = pb.inp("spC", [NC2 * 64, F] if gated else [NC2 * 128, NHP * 64], BF16)
    spR = pb.inp("spR", [NC2 * 64, F], BF16)
    ot = pb.out("ot", [T, F])
    if gated:
        gtb = pb.inp("gtb", [T, 48])
        mnt = pb.inp("mnt", [1, F])
        speb = pb.inp("speb", [NC2 * 64, 16])
    else:
        abr_in = pb.inp("ab_raw", [1, 32])

    qsb = pb.sb("qsb", [128, NHP * T], BF16)
    ksb = pb.sb("ksb", [128, NHP * T], BF16)
    kst = pb.sb("kst", [64, F], BF16)
    vst = pb.sb("vst", [64, F], BF16)
    Mb = pb.sb("Mb", [64, F], BF16)
    MTb = pb.sb("MTb", [64, F], BF16)
    M1T = pb.sb("M1T", [64, F], BF16)
    POW = [pb.sb(f"POW{l}", [64, F], BF16) for l in range(4)]
    POWT = [pb.sb(f"POWT{l}", [64, F], BF16) for l in range(4)]
    P32 = pb.sb("P32", [64, F], BF16)
    Xf = pb.sb("Xf", [64, F], F32)
    Xb = pb.sb("Xb", [64, F], BF16)
    Rall = pb.sb("Rall", [64, NH * 128], BF16)
    Ub = pb.sb("Ub", [64, NH * 128], BF16)
    tmpF = pb.sb("tmpF", [64, F], F32)
    t2F = pb.sb("t2F", [64, F], F32)
    Sf = pb.sb("Sf", [64, F], F32)
    Sb0 = pb.sb("Sb0", [64, F], BF16)
    Sbp = pb.sb("Sbp", [128, NHP * 64], BF16)
    sq = pb.sb("sq", [128, 512], F32)
    rowq = pb.sb("rowq", [1, 512], F32)
    red = pb.sb("red", [128, 16], F32)
    Imask = pb.sb("Imask", [64, F], F32)
    ones64f = pb.sb("ones64f", [64, 64], F32)
    onr1 = pb.sb("onr1", [1, 64], F32)
    rc = pb.sb("rc", [128, 1], F32)
    psA = pb.ps("psA", [64, F], F32)
    psB = pb.ps("psB", [64, F], F32)
    psU = pb.ps("psU", [64, NH * 128], F32)
    if gated:
        gts = pb.sb("gts", [128, (T // 128) * 48], F32)
        gst = pb.sb("gst", [64, 48], F32)
        bb = pb.sb("bb", [64, 16], F32)
        eb = pb.sb("ebx", [64, 16], F32)
        ed = pb.sb("edx", [64, 16], F32)
        ebC = pb.sb("ebC", [64, 16], F32)
        blnb = pb.sb("blnb", [64, 16], F32)
        bpre = pb.sb("bpre", [64, 16], F32)
        fdiag = pb.sb("fdiag", [64, F], F32)
        gmA = pb.sb("gmA", [64, F], F32)
        dk = pb.sb("dk", [64, F], BF16)
        mntb = pb.sb("mntb", [64, F], F32)
        mrow = pb.sb("mrow", [1, F], F32)
        triuI = pb.sb("triuI", [64, 64], F32)
        sLowS = pb.sb("sLowS", [64, 64], F32)
        mSLn = pb.sb("mSLn", [64, F], F32)
        mSUn = pb.sb("mSUn", [64, F], F32)
        mSUi = pb.sb("mSUi", [64, F], F32)
    else:
        abr = pb.sb("abr", [1, 32], F32)
        abbc = pb.sb("abbc", [64, 32], F32)
        maskA_SL = pb.sb("maskA_SL", [64, F], F32)
        maskA_SU = pb.sb("maskA_SU", [64, F], F32)
        maskSU = pb.sb("maskSU", [64, F], BF16)
        ghTb = pb.sb("ghTb", [128, NHP * 64], BF16)

    pb.dma(rc[:], rolec)
    for hp in range(NHP):
        pb.dma(qsb[:, hp * T:(hp + 1) * T], qfm[hp * 128:(hp + 1) * 128, :])
        pb.dma(ksb[:, hp * T:(hp + 1) * T], kfm[hp * 128:(hp + 1) * 128, :])
    pb.memset(Imask[:], 1.0)
    pb.affine_select(Imask[:], Imask[:], [[0, NH], [-1, C]], ALU.is_equal, 0.0,
                     channel_multiplier=1)
    pb.memset(ones64f[:], 1.0)
    pb.memset(onr1[:], 1.0)
    if gated:
        pb.dma(gts[:].rearrange("p (r f) -> p r f", f=48),
               gtb.rearrange("(r p) f -> p r f", p=128))
        pb.dma(mrow[:], mnt)
        pb.mm(psA[:, 0:512], onr1[:], mrow[0:1, 0:512])
        pb.mm(psA[:, 512:1024], onr1[:], mrow[0:1, 512:1024])
        pb.copy(mntb[:], psA[:])
        pb.memset(mSLn[:], 0.0)
        pb.affine_select(mSLn[:], mSLn[:], [[0, NH], [-1, C]], ALU.is_gt, -1e30,
                         channel_multiplier=1)
        pb.memset(mSUn[:], 0.0)
        pb.affine_select(mSUn[:], mSUn[:], [[0, NH], [1, C]], ALU.is_gt, -1e30,
                         channel_multiplier=-1)
        pb.memset(mSUi[:], 0.0)
        pb.affine_select(mSUi[:], mSUi[:], [[0, NH], [1, C]], ALU.is_ge, -1e30,
                         channel_multiplier=-1)
        pb.memset(triuI[:], 1.0)
        pb.affine_select(triuI[:], triuI[:], [[1, 64]], ALU.is_ge, 0.0,
                         channel_multiplier=-1)
        pb.memset(sLowS[:], 1.0)
        pb.affine_select(sLowS[:], sLowS[:], [[-1, 64]], ALU.is_gt, 0.0,
                         channel_multiplier=1)
    else:
        pb.dma(abr[:], abr_in)
        pb.act(abr[:], abr[:], AF.Sigmoid)
        pb.ts(abr[:], abr[:], 0.5, ALU.mult)
        pb.mm(psA[0:64, 0:32], onr1[:], abr[:])
        pb.copy(abbc[:], psA[0:64, 0:32])
        a3 = abbc[0:64, 0:16].unsqueeze(2).broadcast_to((64, 16, C))
        b3 = abbc[0:64, 16:32].unsqueeze(2).broadcast_to((64, 16, C))
        pb.memset(maskA_SL[:], 1.0)
        pb.affine_select(maskA_SL[:], maskA_SL[:], [[0, NH], [-1, C]], ALU.is_gt,
                         0.0, channel_multiplier=1)
        pb.memset(maskSU[:], 1.0)
        pb.affine_select(maskSU[:], maskSU[:], [[0, NH], [1, C]], ALU.is_gt, 0.0,
                         channel_multiplier=-1)
        pb.copy(maskA_SU[:], maskSU[:])
        pb.tt(maskA_SL[:].rearrange("p (h i) -> p h i", i=C),
              maskA_SL[:].rearrange("p (h i) -> p h i", i=C), a3, ALU.mult)
        pb.tt(maskA_SU[:].rearrange("p (h i) -> p h i", i=C),
              maskA_SU[:].rearrange("p (h i) -> p h i", i=C), a3, ALU.mult)

    def qslice(h, t0, w=C):
        hp, par = divmod(h, 2)
        return qsb[64 * par:64 * par + 64, hp * T + t0: hp * T + t0 + w]

    def kslice(h, t0, w=C):
        hp, par = divmod(h, 2)
        return ksb[64 * par:64 * par + 64, hp * T + t0: hp * T + t0 + w]

    bufs = ((qsb, True), (ksb, True)) if gated else ((ksb, False),)
    for (buf, is_rsq) in bufs:
        for h in range(NH):
            hp, par = divmod(h, 2)
            for t0 in range(0, T, 512):
                sl = buf[64 * par:64 * par + 64, hp * T + t0: hp * T + t0 + 512]
                pb.act(sq[0:64, :], sl, AF.Square)
                pb.mm(psA[0:1, 0:512], ones64f[:, 0:1], sq[0:64, :])
                if is_rsq:
                    pb.act(rowq[0:1, :], psA[0:1, 0:512], AF.Sqrt,
                           bias=pb.constcol(1e-6, 1))
                else:
                    pb.act(rowq[0:1, :], psA[0:1, 0:512], AF.Sqrt)
                    pb.ts(rowq[0:1, :], rowq[0:1, :], 1e-12, ALU.max)
                pb.recip(rowq[0:1, :], rowq[0:1, :])
                pb.mm(psB[0:64, 0:512], onr1[:], rowq[0:1, :])
                pb.tt(sl, sl, psB[0:64, 0:512], ALU.mult)
    def load_kst_vst(c):
        pb.dma(kst[:], ktm[c * C:(c + 1) * C, :])
        pb.dma(vst[:], vtm[c * C:(c + 1) * C, :])
        pb.act(sq[0:64, 0:512], kst[:, 0:512], AF.Square)
        pb.reduce(red[0:64, 0:8],
                  sq[0:64, 0:512].rearrange("p (h d) -> p h d", d=64), ALU.add)
        pb.act(sq[0:64, 0:512], kst[:, 512:1024], AF.Square)
        pb.reduce(red[0:64, 8:16],
                  sq[0:64, 0:512].rearrange("p (h d) -> p h d", d=64), ALU.add)
        if gated:
            pb.act(red[0:64, :], red[0:64, :], AF.Sqrt, bias=pb.constcol(1e-6, 64))
        else:
            pb.act(red[0:64, :], red[0:64, :], AF.Sqrt)
            pb.ts(red[0:64, :], red[0:64, :], 1e-12, ALU.max)
        pb.recip(red[0:64, :], red[0:64, :])
        pb.tt(kst[:].rearrange("p (h d) -> p h d", d=64),
              kst[:].rearrange("p (h d) -> p h d", d=64),
              red[0:64, 0:16].unsqueeze(2).broadcast_to((64, 16, 64)), ALU.mult)

    pb.memset(Sf[:], 0.0, eng="vector")
    pb.copy(Sb0[:], Sf[:])
    b8 = None
    for c in range(NC2):
        t0 = c * C
        load_kst_vst(c)
        if gated:
            gr, gpar = divmod(t0, 128)
            pb.copy(gst[:], gts[64 * (gpar // 64):64 * (gpar // 64) + 64,
                                gr * 48:(gr + 1) * 48])
            b8 = bb[:, 0:16]
            pb.mm(psA[0:64, 0:16], triuI[:], gst[:, 0:16])
            pb.copy(bb[:], psA[0:64, 0:16])
            pb.mm(psA[0:64, 0:16], sLowS[:], gst[:, 0:16])
            pb.act(eb[:], bb[:], AF.Exp)
            pb.copy(blnb[:], psA[0:64, 0:16])
            pb.act(ed[:], blnb[:], AF.Exp)
            pb.tt(ebC[:], eb[:], ed[:], ALU.mult)
            pb.tt(blnb[:], bb[:], gst[:, 16:32], ALU.add)
            pb.tt(bpre[:], gst[:, 32:48], eb[:], ALU.mult)
        for h in range(NH):
            pb.mm(psA[:, h * 64:(h + 1) * 64], kslice(h, t0), kslice(h, t0))
        for h in range(NH):
            pb.mm(psB[:, h * 64:(h + 1) * 64], kslice(h, t0), qslice(h, t0))
        if gated:
            pb.tt(fdiag[:].rearrange("p (h i) -> p h i", i=C),
                  Imask[:].rearrange("p (h i) -> p h i", i=C),
                  bb[:].unsqueeze(2).broadcast_to((64, 16, C)), ALU.mult)
            pb.mm(psU[:, 0:512], ones64f[:], fdiag[:, 0:512])
            pb.mm(psU[:, 512:F], ones64f[:], fdiag[:, 512:F])
            pb.tt(fdiag[:].rearrange("p (h i) -> p h i", i=C),
                  Imask[:].rearrange("p (h i) -> p h i", i=C),
                  blnb[:].unsqueeze(2).broadcast_to((64, 16, C)), ALU.mult)
            pb.mm(psU[:, F:F + 512], ones64f[:], fdiag[:, 0:512])
            pb.mm(psU[:, F + 512:2 * F], ones64f[:], fdiag[:, 512:F])
            pb.tt(gmA[:].rearrange("p (h i) -> p h i", i=C),
                  blnb[:].unsqueeze(2).broadcast_to((64, 16, C)),
                  psU[:, 0:F].rearrange("p (h i) -> p h i", i=C), ALU.subtract)
            pb.tt(gmA[:], gmA[:], mSLn[:], ALU.add)
            pb.act(gmA[:], gmA[:], AF.Exp)
            pb.tt(Mb[:], gmA[:], psA[:], ALU.mult)
            pb.tt(gmA[:].rearrange("p (h i) -> p h i", i=C),
                  psU[:, F:2 * F].rearrange("p (h i) -> p h i", i=C),
                  bb[:].unsqueeze(2).broadcast_to((64, 16, C)), ALU.subtract)
            pb.tt(gmA[:], gmA[:], mSUn[:], ALU.add)
            pb.act(gmA[:], gmA[:], AF.Exp)
            pb.tt(MTb[:], gmA[:], psA[:], ALU.mult)
            pb.tt(gmA[:].rearrange("p (h i) -> p h i", i=C),
                  psU[:, 0:F].rearrange("p (h i) -> p h i", i=C),
                  bb[:].unsqueeze(2).broadcast_to((64, 16, C)), ALU.subtract)
            pb.tt(gmA[:], gmA[:], mSUi[:], ALU.add)
            pb.act(gmA[:], gmA[:], AF.Exp)
            pb.tt(M1T[:], gmA[:], psB[:], ALU.mult)
        else:
            pb.tt(Mb[:], psA[:], maskA_SL[:], ALU.mult)
            pb.tt(MTb[:], psA[:], maskA_SU[:], ALU.mult)
            pb.tt(M1T[:], psB[:], maskSU[:], ALU.mult)
        prev, prevT = Mb, MTb
        for l in range(4):
            for h in range(NH):
                sl = slice(h * 64, (h + 1) * 64)
                pb.mm(psA[:, sl], prevT[:, sl], prev[:, sl])
            pb.copy(POW[l][:], psA[:])
            for h in range(NH):
                sl = slice(h * 64, (h + 1) * 64)
                pb.mm(psA[:, sl], prev[:, sl], prevT[:, sl])
            pb.copy(POWT[l][:], psA[:])
            prev, prevT = POW[l], POWT[l]
        for h in range(NH):
            sl = slice(h * 64, (h + 1) * 64)
            pb.mm(psA[:, sl], POWT[3][:, sl], POW[3][:, sl])
        pb.copy(P32[:], psA[:])
        pb.tt(Xf[:], Imask[:], MTb[:], ALU.subtract)
        pb.copy(Xb[:], Xf[:])
        for pw in (POW[0], POW[1], POW[2], POW[3], P32):
            for h in range(NH):
                sl = slice(h * 64, (h + 1) * 64)
                pb.mm(psA[:, sl], pw[:, sl], Xb[:, sl])
            pb.tt(Xf[:], Xf[:], psA[:], ALU.add)
            pb.copy(Xb[:], Xf[:])
        if gated:
            pb.tt(Rall[:].rearrange("p (h z) -> p h z", z=128)[:, :, 0:64],
                  vst[:].rearrange("p (h z) -> p h z", z=64),
                  gst[:, 32:48].unsqueeze(2).broadcast_to((64, 16, C)), ALU.mult)
            pb.tt(Rall[:].rearrange("p (h z) -> p h z", z=128)[:, :, 64:128],
                  kst[:].rearrange("p (h z) -> p h z", z=64),
                  bpre[:].unsqueeze(2).broadcast_to((64, 16, C)), ALU.mult)
        else:
            pb.tt(Rall[:].rearrange("p (h z) -> p h z", z=128)[:, :, 0:64],
                  vst[:].rearrange("p (h z) -> p h z", z=64), b3, ALU.mult)
            pb.tt(Rall[:].rearrange("p (h z) -> p h z", z=128)[:, :, 64:128],
                  kst[:].rearrange("p (h z) -> p h z", z=64), a3, ALU.mult)
        for h in range(NH):
            pb.mm(psU[:, h * 128:(h + 1) * 128], Xb[:, h * 64:(h + 1) * 64],
                  Rall[:, h * 128:(h + 1) * 128])
        pb.copy(Ub[:], psU[:])
        U0 = lambda h: Ub[:, h * 128: h * 128 + 64]
        W0 = lambda h: Ub[:, h * 128 + 64: (h + 1) * 128]
        for h in range(NH):
            pb.mm(psA[:, h * 64:(h + 1) * 64], M1T[:, h * 64:(h + 1) * 64], U0(h))
        pb.copy(tmpF[:], psA[:])
        pb.dma(spY1[c * 64:(c + 1) * 64, :], tmpF[:])
        for h in range(NH):
            pb.mm(psB[:, h * 64:(h + 1) * 64], W0(h), M1T[:, h * 64:(h + 1) * 64])
        if gated:
            pb.ts(Mb[:], psB[:], -1.0, ALU.mult)
            pb.dma(spC[c * 64:(c + 1) * 64, :], Mb[:])
        else:
            for par in range(2):
                pb.tt(ghTb[64 * par:64 * par + 64, :].rearrange(
                          "p (hp z) -> p hp z", z=64),
                      qsb[64 * par:64 * par + 64, :].rearrange(
                          "p (hp s) -> p hp s", s=T)[:, :, t0:t0 + C],
                      psB[0:64, :].rearrange("p (h z) -> p h z", z=64)[:, par::2, :],
                      ALU.subtract)
            pb.dma(spC[c * 128:(c + 1) * 128, :], ghTb[:])
        if gated:
            pb.tt(dk[:].rearrange("p (h z) -> p h z", z=64),
                  kst[:].rearrange("p (h z) -> p h z", z=64),
                  ed[:].unsqueeze(2).broadcast_to((64, 16, C)), ALU.mult)
            kuse = dk
        else:
            kuse = kst
        for h in range(NH):
            pb.mm(psA[:, h * 64:(h + 1) * 64], kuse[:, h * 64:(h + 1) * 64], U0(h))
        pb.copy(tmpF[:], psA[:])
        pb.dma(spP[c * 64:(c + 1) * 64, :], tmpF[:])
        for h in range(NH):
            pb.mm(psB[:, h * 64:(h + 1) * 64], W0(h), kuse[:, h * 64:(h + 1) * 64])
        if gated:
            pb.tt(t2F[:].rearrange("p (h i) -> p h i", i=C),
                  Imask[:].rearrange("p (h i) -> p h i", i=C),
                  ebC[:].unsqueeze(2).broadcast_to((64, 16, C)), ALU.mult)
            pb.tt(Mb[:], t2F[:], psB[:], ALU.subtract)
            pb.dma(speb[c * 64:(c + 1) * 64, :], eb[:])
        else:
            pb.tt(Mb[:], Imask[:], psB[:], ALU.subtract)
        pb.dma(spR[c * 64:(c + 1) * 64, :], Mb[:])
        for h in range(NH):
            pb.mm(psB[:, h * 64:(h + 1) * 64], Mb[:, h * 64:(h + 1) * 64],
                  Sb0[:, h * 64:(h + 1) * 64])
        pb.tt(Sf[:], tmpF[:], psB[:], ALU.add)
        pb.copy(Sb0[:], Sf[:])
    pb.dma(sml.rearrange("one (p c) -> (one p) c", p=64), Sf[:])
    emit_ag(pb, smg, sml)

    pb.dma(Sf[:], smg[0:1, :].rearrange("one (p c) -> p (one c)", p=64))
    pb.ts(Sf[:], Sf[:], rc[0:64, 0:1], ALU.mult)
    pb.copy(Sb0[:], Sf[:])
    for h in range(NH):
        hp, par = divmod(h, 2)
        pb.copy(Sbp[64 * par:64 * par + 64, hp * 64:(hp + 1) * 64],
                Sf[:, h * 64:(h + 1) * 64])
    Y1sb = tmpF
    for c in range(NC2):
        t0 = c * C
        pb.dma(Y1sb[:], spY1[c * 64:(c + 1) * 64, :])
        if gated:
            pb.dma(Mb[:], spC[c * 64:(c + 1) * 64, :])
        else:
            pb.dma(ghTb[:], spC[c * 128:(c + 1) * 128, :])
        pb.dma(MTb[:], spR[c * 64:(c + 1) * 64, :])
        if gated:
            pb.dma(eb[:], speb[c * 64:(c + 1) * 64, :])
            for h in range(NH):
                pb.mm(psA[:, h * 64:(h + 1) * 64], Mb[:, h * 64:(h + 1) * 64],
                      Sb0[:, h * 64:(h + 1) * 64])
            for h in range(NH):
                hp, par = divmod(h, 2)
                pb.mm(psB[:, h * 64:(h + 1) * 64], qslice(h, t0),
                      Sbp[64 * par:64 * par + 64, hp * 64:(hp + 1) * 64])
            pb.tt(t2F[:], Y1sb[:], psA[:], ALU.add)
            for h in range(NH):
                sl = slice(h * 64, (h + 1) * 64)
                pb.ts(Xf[:, sl], psB[:, sl], eb[:, h:h + 1], ALU.mult)
            pb.tt(t2F[:], t2F[:], Xf[:], ALU.add)
            pb.ts(t2F[:], t2F[:], 0.125, ALU.mult)
            pb.act(sq[0:64, 0:512], t2F[:, 0:512], AF.Square)
            pb.reduce(red[0:64, 0:8],
                      sq[0:64, 0:512].rearrange("p (h d) -> p h d", d=64), ALU.add)
            pb.act(sq[0:64, 0:512], t2F[:, 512:1024], AF.Square)
            pb.reduce(red[0:64, 8:16],
                      sq[0:64, 0:512].rearrange("p (h d) -> p h d", d=64), ALU.add)
            pb.ts(red[0:64, :], red[0:64, :], 1.0 / 64.0, ALU.mult)
            pb.act(red[0:64, :], red[0:64, :], AF.Sqrt, bias=pb.constcol(1e-6, 64))
            pb.recip(red[0:64, :], red[0:64, :])
            pb.tt(t2F[:].rearrange("p (h d) -> p h d", d=64),
                  t2F[:].rearrange("p (h d) -> p h d", d=64),
                  red[0:64, 0:16].unsqueeze(2).broadcast_to((64, 16, 64)), ALU.mult)
            pb.tt(t2F[:], t2F[:], mntb[:], ALU.mult)
        else:
            for h in range(NH):
                hp, par = divmod(h, 2)
                pb.mm(psA[:, h * 64:(h + 1) * 64],
                      ghTb[64 * par:64 * par + 64, hp * 64:(hp + 1) * 64],
                      Sbp[64 * par:64 * par + 64, hp * 64:(hp + 1) * 64])
            pb.tt(t2F[:], Y1sb[:], psA[:], ALU.add)
        pb.dma(ot[t0:t0 + C, :], t2F[:])
        pb.dma(Y1sb[:], spP[c * 64:(c + 1) * 64, :])
        for h in range(NH):
            pb.mm(psB[:, h * 64:(h + 1) * 64], MTb[:, h * 64:(h + 1) * 64],
                  Sb0[:, h * 64:(h + 1) * 64])
        pb.tt(Sf[:], Y1sb[:], psB[:], ALU.add)
        pb.copy(Sb0[:], Sf[:])
        for h in range(NH):
            hp, par = divmod(h, 2)
            pb.copy(Sbp[64 * par:64 * par + 64, hp * 64:(hp + 1) * 64],
                    Sf[:, h * 64:(h + 1) * 64])
    return pb


def build_attn_b1_scan_local(pb):
    NH, C = 16, 64
    T = RPC
    NC2 = T // C
    F = NH * 64
    NHP = NH // 2
    qfm = pb.inp("qfm", [D, T], BF16)
    kfm = pb.inp("kfm", [D, T], BF16)
    ktm = pb.inp("ktm", [T, F], BF16)
    vtm = pb.inp("vtm", [T, F], BF16)
    rolec = pb.inp("rolec", [128, 1])
    sml = pb.inp("sml", [1, 64 * F])
    smg = pb.inp("smg", [2, 64 * F])
    ot = pb.out("ot", [T, F])

    qsb = pb.sb("qsb", [128, NHP * T], BF16)
    ksb = pb.sb("ksb", [128, NHP * T], BF16)
    kst = pb.sb("kst", [64, F], BF16)
    vst = pb.sb("vst", [64, F], BF16)
    Sf = pb.sb("Sf", [64, F], F32)
    Sb0 = pb.sb("Sb0", [64, F], BF16)
    Sbp = pb.sb("Sbp", [128, NHP * 64], BF16)
    ebuf = pb.sb("ebuf", [64, F], BF16)
    mask = pb.sb("maskt", [64, F], BF16)
    ostage = pb.sb("ostage", [64, F], F32)
    rc = pb.sb("rc", [128, 1], F32)
    obig = pb.sb("obig", [128, F], F32)
    ps_e = pb.ps("ps_e", [64, F], F32)
    ps_o = pb.ps("ps_o", [64, F], F32)
    ps_s = pb.ps("ps_s", [64, F], F32)
    ps_c = pb.ps("ps_c", [128, F], F32)

    pb.dma(rc[:], rolec)
    for hp in range(NHP):
        pb.dma(qsb[:, hp * T:(hp + 1) * T], qfm[hp * 128:(hp + 1) * 128, :])
        pb.dma(ksb[:, hp * T:(hp + 1) * T], kfm[hp * 128:(hp + 1) * 128, :])
    pb.memset(mask[:], 1.0)
    pb.affine_select(mask[:], mask[:], [[0, NH], [1, C]], ALU.is_ge, 0.0,
                     channel_multiplier=-1)
    pb.memset(Sf[:], 0.0, eng="vector")
    pb.copy(Sb0[:], Sf[:])
    for h in range(NH):
        hp, par = divmod(h, 2)
        pb.copy(Sbp[64 * par:64 * par + 64, hp * 64:(hp + 1) * 64],
                Sf[:, h * 64:(h + 1) * 64])

    def qslice(h, t0, w=C):
        hp, par = divmod(h, 2)
        return qsb[64 * par:64 * par + 64, hp * T + t0: hp * T + t0 + w]

    def kslice(h, t0, w=C):
        hp, par = divmod(h, 2)
        return ksb[64 * par:64 * par + 64, hp * T + t0: hp * T + t0 + w]

    for c in range(NC2):
        t0 = c * C
        pb.dma(kst[:], ktm[c * C:(c + 1) * C, :])
        pb.dma(vst[:], vtm[c * C:(c + 1) * C, :])
        for h in range(NH):
            pb.mm(ps_e[:, h * 64:(h + 1) * 64], kslice(h, t0), qslice(h, t0))
        pb.tt(ebuf[:], ps_e[:], mask[:], ALU.mult)
        for h in range(NH):
            hp, par = divmod(h, 2)
            pb.mm(ps_o[:, h * 64:(h + 1) * 64], ebuf[:, h * 64:(h + 1) * 64],
                  vst[0:64, h * 64:(h + 1) * 64], start=True, stop=False)
            pb.mm(ps_o[:, h * 64:(h + 1) * 64], qslice(h, t0),
                  Sbp[64 * par:64 * par + 64, hp * 64:(hp + 1) * 64],
                  start=False, stop=True)
        pb.copy(ostage[:], ps_o[:])
        pb.dma(ot[t0:t0 + C, :], ostage[:])
        for h in range(NH):
            pb.mm(ps_s[:, h * 64:(h + 1) * 64], kst[0:64, h * 64:(h + 1) * 64],
                  vst[0:64, h * 64:(h + 1) * 64])
        pb.tt(Sf[:], Sf[:], ps_s[:], ALU.add)
        pb.copy(Sb0[:], Sf[:])
        for h in range(NH):
            hp, par = divmod(h, 2)
            pb.copy(Sbp[64 * par:64 * par + 64, hp * 64:(hp + 1) * 64],
                    Sf[:, h * 64:(h + 1) * 64])
    pb.dma(sml.rearrange("one (p c) -> (one p) c", p=64), Sf[:])
    emit_ag(pb, smg, sml)
    pb.dma(Sf[:], smg[0:1, :].rearrange("one (p c) -> p (one c)", p=64))
    pb.ts(Sf[:], Sf[:], rc[0:64, 0:1], ALU.mult)
    for h in range(NH):
        hp, par = divmod(h, 2)
        pb.copy(Sbp[64 * par:64 * par + 64, hp * 64:(hp + 1) * 64],
                Sf[:, h * 64:(h + 1) * 64])
    for tr in range(T // 128):
        t0 = tr * 128
        for h in range(NH):
            hp, par = divmod(h, 2)
            pb.mm(ps_c[:, h * 64:(h + 1) * 64], qslice(h, t0, 128),
                  Sbp[64 * par:64 * par + 64, hp * 64:(hp + 1) * 64])
        pb.dma(obig[:], ot[t0:t0 + 128, :])
        pb.tt(obig[:], obig[:], ps_c[:], ALU.add)
        pb.dma(ot[t0:t0 + 128, :], obig[:])
    return pb


def emit_bnd_gather(pb, pref, x_ap, ncols):
    loc = pb.dram(pref + "_bl", [128, NCH * ncols])
    g = pb.dram(pref + "_bg", [2, 128 * NCH * ncols])
    with pb.stage(pref, {}):
        t = pb.sb("t", [128, NCH * ncols], F32)
        pb.dma(t[:].rearrange("p (dc c) -> p dc c", c=ncols),
               x_ap[:, RPC - ncols:RPC].rearrange("(dc p) c -> p dc c", p=128),
               allow_slow=True)
        pb.dma(loc[:], t[:])
        emit_ag(pb, g, loc.rearrange("p c -> (p c)").unsqueeze(0))
    return g


def emit_attn_a(pb, E, xin, xout):
    qb = pb.dram("aa_Qb", [D, RPC], BF16)
    kf = pb.dram("aa_kf", [1, D * RPC], BF16)
    vb = pb.dram("aa_vb", [D, RPC], BF16)
    rq = pb.dram("aa_rq", [1, RPC])
    rk = pb.dram("aa_rk", [1, RPC])
    rkc = pb.dram("aa_rkc", [1, 128 * (RPC // 128)])
    run_stage(pb, "aaq", build_attn_a_qkv_f, {
        "xr": xin, "wqT": E["aa_wqT"], "wkT": E["aa_wkT"], "wvT": E["aa_wvT"],
        "bq": E["aa_bq"], "bk": E["aa_bk"], "bv": E["aa_bv"],
        "Qb": qb, "Kb": kf.rearrange("one (d t) -> (one d) t", t=RPC),
        "Vb": vb, "rq": rq, "rk": rk,
        "rkcol": rkc.rearrange("one (p c) -> (one p) c", p=128),
    })
    vt = pb.dram("aa_vt", [1, RPC * D], BF16)
    emit_transpose_mm(pb, "aaT",
                      vt.rearrange("one (t d) -> (one t) d", d=D),
                      vb, D, RPC, BF16)
    kg = pb.dram("aa_kg", [2, D * RPC], BF16)
    vtg = pb.dram("aa_vtg", [2, RPC * D], BF16)
    rkg = pb.dram("aa_rkg", [2, 128 * (RPC // 128)])
    emit_ag(pb, kg, kf)
    emit_ag(pb, vtg, vt)
    emit_ag(pb, rkg, rkc)
    run_stage(pb, "aav", build_attn_a_av_f, {
        "xr": xin, "Qb": qb, "Kg": kg,
        "Vtg": vtg.rearrange("h (t d) -> (h t) d", d=D),
        "rq": rq, "rkg": rkg, "woT": E["aa_woT"], "bo": E["aa_bo"], "xo": xout,
    })


def emit_ffn0(pb, E, xin, xout, tag):
    g = emit_bnd_gather(pb, tag + "bg", xin, 1)
    run_stage(pb, tag, build_l0_ffn, {
        "xr": xin, "xprev": g,
        "rolec": E["rolec"],
        "wrT": E["f0_wrT"], "wkT": E["f0_wkT"], "wvT": E["f0_wvT"],
        "mur": E["f0_mur"], "muk": E["f0_muk"], "xo": xout,
    })


def emit_ffn1(pb, E, xin, xout, tag):
    run_stage(pb, tag, build_l1_ffn, {
        "xr": xin, "w1": E["f1_w1"], "b1": E["f1_b1"],
        "w2": E["f1_w2"], "b2": E["f1_b2"], "xo": xout,
    })


def emit_gtb(pb, pref, gtb, srcs):
    with pb.stage(pref, {}):
        I16 = pb.sb("I16", [16, 16], F32)
        pb.memset(I16[:], 1.0)
        pb.affine_select(I16[:], I16[:], [[-1, 16]], ALU.is_equal, 0.0,
                         channel_multiplier=1)
        tile = pb.sb("tile", [16, 128], F32)
        buf = pb.sb("buf", [128, 48], F32)
        ps = pb.ps("ps", [128, 16], F32)
        for tr in range(RPC // 128):
            t0 = tr * 128
            for si, src in enumerate(srcs):
                pb.dma(tile[:], src[:, t0:t0 + 128])
                pb.mm(ps[:], tile[:], I16[:])
                pb.copy(buf[:, si * 16:(si + 1) * 16], ps[:])
            pb.dma(gtb[t0:t0 + 128, :], buf[:])


def emit_gdn(pb, E, xin, xout):
    g3 = emit_bnd_gather(pb, "gdb", xin, 3)
    y32 = pb.dram("gd_y32", [D, RPC])
    qc = pb.dram("gd_qc", [D, RPC], BF16)
    kc = pb.dram("gd_kc", [D, RPC], BF16)
    vc = pb.dram("gd_vc", [D, RPC], BF16)
    g16 = pb.dram("gd_g16", [16, RPC])
    lnb16 = pb.dram("gd_lnb16", [16, RPC])
    beta16 = pb.dram("gd_beta16", [16, RPC])
    gsig = pb.dram("gd_gsig", [D, RPC])
    run_stage(pb, "gp", build_gdn_prep, {
        "xr": xin, "xp3": g3, "rolec": E["rolec"], "cw": E["gd_cw"],
        "cbm": E["gd_cbm"], "cbt": E["gd_cbt"], "convw": E["gd_convw"],
        "y32": y32, "qc": qc, "kc": kc, "vc": vc, "g16": g16, "lnb16": lnb16,
        "beta16": beta16, "gsig": gsig,
    })
    zml = pb.dram("p2_zml", [1, 128 * 8])
    zmg = pb.dram("p2_zmg", [2, 128 * 8])
    sfm = pb.dram("p2_sfm", [D, RPC])
    run_stage(pb, "p2", build_p20_local, {
        "y32": y32, "dl": E["p20_dl"], "th": E["p20_th"], "rolec": E["rolec"],
        "zml": zml, "zmg": zmg, "sfm": sfm,
    })
    qm = pb.dram("gm_qm", [D, RPC], BF16)
    km = pb.dram("gm_km", [D, RPC], BF16)
    vm = pb.dram("gm_vm", [D, RPC], BF16)
    np20 = pb.dram("gm_np20", [D, RPC], BF16)
    run_stage(pb, "gm", build_gdn_mix, {
        "sfm": sfm, "qcf": qc, "kcf": kc, "vcf": vc, "p20cW": E["gm_p20cW"],
        "vnc": E["gm_vnc"], "rlc": E["gm_rlc"],
        "qm": qm, "km": km, "vm": vm, "np20": np20,
    })
    ktm = pb.dram("gs_ktm", [RPC, D], BF16)
    vtm = pb.dram("gs_vtm", [RPC, D], BF16)
    emit_transpose_mm(pb, "gsT1", ktm, km, D, RPC, BF16)
    emit_transpose_mm(pb, "gsT2", vtm, vm, D, RPC, BF16)
    gtb = pb.dram("gs_gtb", [RPC, 48])
    emit_gtb(pb, "gsG", gtb, (g16, lnb16, beta16))
    NC2 = RPC // 64
    sp = {
        "spY1": pb.dram("gs_spY1", [NC2 * 64, 1024]),
        "spP": pb.dram("gs_spP", [NC2 * 64, 1024]),
        "spC": pb.dram("gs_spC", [NC2 * 64, 1024], BF16),
        "spR": pb.dram("gs_spR", [NC2 * 64, 1024], BF16),
        "speb": pb.dram("gs_speb", [NC2 * 64, 16]),
    }
    sml = pb.dram("gs_sml", [1, 64 * 1024])
    smg = pb.dram("gs_smg", [2, 64 * 1024])
    nrt = pb.dram("gs_nrt", [RPC, D])
    run_stage(pb, "gs", lambda p: build_delta_scan_local(p, True), {
        "qfm": qm, "kfm": km, "ktm": ktm, "vtm": vtm, "rolec": E["rolec"],
        "sml": sml, "smg": smg, "ot": nrt, "gtb": gtb, "mnt": E["gs_mnt"], **sp,
    })
    nrf = pb.dram("go_nrf", [D, RPC])
    emit_transpose_mm(pb, "goT", nrf, nrt, RPC, D, F32)
    run_stage(pb, "go", build_gdn_out, {
        "nrf": nrf, "npf": np20, "gsig": gsig, "outw": E["go_outw"],
        "outb": E["go_outb"], "xr": xin, "xo": xout,
    })


def emit_attn_a1(pb, E, xin, xout):
    qt = pb.dram("a1_qt", [RPC, D], BF16)
    kt = pb.dram("a1_kt", [RPC, D], BF16)
    vt = pb.dram("a1_vt", [RPC, D], BF16)
    run_stage(pb, "a1q", build_attn_a1_qkv, {
        "xr": xin, "wq": E["a1_wq"], "wk": E["a1_wk"], "wv": E["a1_wv"],
        "qt": qt, "kt": kt, "vt": vt,
    })
    qfm = pb.dram("a1_qfm", [D, RPC], BF16)
    kfm = pb.dram("a1_kfm", [D, RPC], BF16)
    emit_transpose_mm(pb, "a1T1", qfm, qt, RPC, D, BF16)
    emit_transpose_mm(pb, "a1T2", kfm, kt, RPC, D, BF16)
    NC2 = RPC // 64
    sp = {
        "spY1": pb.dram("a1_spY1", [NC2 * 64, 1024]),
        "spP": pb.dram("a1_spP", [NC2 * 64, 1024]),
        "spC": pb.dram("a1_spC", [NC2 * 128, 512], BF16),
        "spR": pb.dram("a1_spR", [NC2 * 64, 1024], BF16),
    }
    sml = pb.dram("a1_sml", [1, 64 * 1024])
    smg = pb.dram("a1_smg", [2, 64 * 1024])
    ott = pb.dram("a1_ot", [RPC, D])
    run_stage(pb, "a1s", lambda p: build_delta_scan_local(p, False), {
        "qfm": qfm, "kfm": kfm, "ktm": kt, "vtm": vt, "rolec": E["rolec"],
        "sml": sml, "smg": smg, "ot": ott, "ab_raw": E["a1_ab"], **sp,
    })
    ofm = pb.dram("a1_ofm", [D, RPC])
    emit_transpose_mm(pb, "a1T3", ofm, ott, RPC, D, F32)
    run_stage(pb, "a1o", build_attn_a1_out, {
        "xr": xin, "ofm": ofm, "wo": E["a1_wo"], "xo": xout,
    })


def emit_attn_b1(pb, E, xin, xout):
    qr = pb.dram("b1_qr", [RPC, D], BF16)
    kr = pb.dram("b1_kr", [RPC, D], BF16)
    vt = pb.dram("b1_vt", [RPC, D], BF16)
    gate = pb.dram("b1_gate", [D, RPC])
    run_stage(pb, "b1q", build_attn_b1_qkv, {
        "xr": xin, "wq": E["b1_wq"], "wk": E["b1_wk"], "wv": E["b1_wv"],
        "wg1": E["b1_wg1"], "wg2": E["b1_wg2"], "poscol": E["b1_poscol"],
        "qr": qr, "kr": kr, "vt": vt, "gate": gate,
    })
    qfm = pb.dram("b1_qfm", [D, RPC], BF16)
    kfm = pb.dram("b1_kfm", [D, RPC], BF16)
    emit_transpose_mm(pb, "b1T1", qfm, qr, RPC, D, BF16)
    emit_transpose_mm(pb, "b1T2", kfm, kr, RPC, D, BF16)
    sml = pb.dram("b1_sml", [1, 64 * 1024])
    smg = pb.dram("b1_smg", [2, 64 * 1024])
    ott = pb.dram("b1_ot", [RPC, D])
    run_stage(pb, "b1s", build_attn_b1_scan_local, {
        "qfm": qfm, "kfm": kfm, "ktm": kr, "vtm": vt, "rolec": E["rolec"],
        "sml": sml, "smg": smg, "ot": ott,
    })
    ofm = pb.dram("b1_ofm", [D, RPC])
    emit_transpose_mm(pb, "b1T3", ofm, ott, RPC, D, F32)
    run_stage(pb, "b1o", build_attn_b1_out, {
        "xr": xin, "ofm": ofm, "gfm": gate, "wo": E["b1_wo"],
        "gam": E["b1_gam"], "bet": E["b1_bet"], "xo": xout,
    })


SUBLAYERS = ["attn_a", "ffn0a", "gdn", "ffn0b", "attn_a1", "ffn1a", "attn_b1",
             "ffn1b"]
FUSE_K = 8


def fused_prefix(X, inputs, K):
    ii = {k: np.asarray(v) for k, v in inputs.items()}
    key = f"fused_{K}"
    if key not in _progs:
        pb = PB(key)
        E = {}
        def ei(name, shape, dt=F32):
            E[name] = pb.inp(name, shape, dt)
        ei("x0", [D, RPC])
        ei("rolec", [128, 1])
        if K > 0:
            for nm in ("wqT", "wkT", "wvT", "woT"):
                ei("aa_" + nm, [D, D])
            for nm in ("bq", "bk", "bv", "bo"):
                ei("aa_" + nm, [128, NCH])
        if K > 1:
            for nm in ("wrT", "wkT", "wvT"):
                ei("f0_" + nm, [D, D])
            for nm in ("mur", "muk"):
                ei("f0_" + nm, [128, NCH])
        if K > 2:
            ei("gd_cw", [D, CTRLW]); ei("gd_cbm", [128, 32]); ei("gd_cbt", [32, 1])
            ei("gd_convw", [128, NCH * 12])
            ei("p20_dl", [128, 4]); ei("p20_th", [128, 4])
            ei("gm_p20cW", [D, 3 * D]); ei("gm_vnc", [128, NCH]); ei("gm_rlc", [128, NCH])
            ei("gs_mnt", [1, 1024])
            ei("go_outw", [2 * D, D]); ei("go_outb", [128, NCH])
        if K > 4:
            for nm in ("wq", "wk", "wv", "wo"):
                ei("a1_" + nm, [D, D])
            ei("a1_ab", [1, 32])
        if K > 5:
            ei("f1_w1", [D, 4096]); ei("f1_b1", [128, 32])
            ei("f1_w2", [4096, D]); ei("f1_b2", [128, NCH])
        if K > 6:
            for nm in ("wq", "wk", "wv", "wo"):
                ei("b1_" + nm, [D, D])
            ei("b1_wg1", [D, 64]); ei("b1_wg2", [64, D])
            ei("b1_poscol", [128, RPC // 128])
            ei("b1_gam", [128, NCH]); ei("b1_bet", [128, NCH])
        xo_ext = pb.out("XO", [D, RPC])
        xs = [pb.dram(f"xi{i}", [D, RPC]) for i in range(K)]
        chain = [E["x0"]] + xs
        emitters = {
            "attn_a": lambda i: emit_attn_a(pb, E, chain[i], chain[i + 1]),
            "ffn0a": lambda i: emit_ffn0(pb, E, chain[i], chain[i + 1], "f0a"),
            "ffn0b": lambda i: emit_ffn0(pb, E, chain[i], chain[i + 1], "f0b"),
            "gdn": lambda i: emit_gdn(pb, E, chain[i], chain[i + 1]),
            "attn_a1": lambda i: emit_attn_a1(pb, E, chain[i], chain[i + 1]),
            "attn_b1": lambda i: emit_attn_b1(pb, E, chain[i], chain[i + 1]),
            "ffn1a": lambda i: emit_ffn1(pb, E, chain[i], chain[i + 1], "f1a"),
            "ffn1b": lambda i: emit_ffn1(pb, E, chain[i], chain[i + 1], "f1b"),
        }
        for i in range(K):
            emitters[SUBLAYERS[i]](i)
        pb.dma(xo_ext, chain[K][:])
        _progs[key] = pb.finish()

    wrT = np.ascontiguousarray(ii["L0_ffn_W_r"].T)
    wkT0 = np.ascontiguousarray(ii["L0_ffn_W_k"].T)
    wvT0 = np.ascontiguousarray(ii["L0_ffn_W_v"].T)
    in_maps = []
    for c in range(NCORES):
        b, h = c // 2, c % 2
        m = {
            "x0": np.ascontiguousarray(X[b, h * RPC:(h + 1) * RPC, :].T),
            "rolec": np.full((128, 1), float(h), np.float32),
        }
        if K > 0:
            m.update({
                "aa_wqT": np.ascontiguousarray(ii["L0_attn_a_Wq"].T),
                "aa_wkT": np.ascontiguousarray(ii["L0_attn_a_Wk"].T),
                "aa_wvT": np.ascontiguousarray(ii["L0_attn_a_Wv"].T),
                "aa_woT": np.ascontiguousarray(ii["L0_attn_a_Wo"].T),
                "aa_bq": vec_cols_host(ii["L0_attn_a_bq"]),
                "aa_bk": vec_cols_host(ii["L0_attn_a_bk"]),
                "aa_bv": vec_cols_host(ii["L0_attn_a_bv"]),
                "aa_bo": vec_cols_host(ii["L0_attn_a_bo"]),
            })
        if K > 1:
            m.update({"f0_wrT": wrT, "f0_wkT": wkT0, "f0_wvT": wvT0,
                      "f0_mur": vec_cols_host(ii["L0_ffn_mu_r"]),
                      "f0_muk": vec_cols_host(ii["L0_ffn_mu_k"])})
        if K > 2:
            cw = ii["L0_attn_b_ctrl_W"]; cb = ii["L0_attn_b_ctrl_b"]
            perm = np.concatenate([np.arange(0, 3 * D),
                                   np.arange(3 * D + 32, 3 * D + 32 + D),
                                   np.arange(3 * D, 3 * D + 32)])
            convw = np.zeros((128, NCH * 12), np.float32)
            for wi, wname in enumerate(["qconv_w", "kconv_w", "vconv_w"]):
                cwv = ii[f"L0_attn_b_{wname}"]
                for fch in range(NCH):
                    for tap in range(4):
                        convw[:, wi * NCH * 4 + fch * 4 + tap] =                             cwv[fch * 128:(fch + 1) * 128, tap]
            cbr = np.ascontiguousarray(cb[perm])
            m.update({
                "gd_cw": np.ascontiguousarray(cw[:, perm]),
                "gd_cbm": vec_cols_host(cbr[:4096]),
                "gd_cbt": np.ascontiguousarray(cbr[4096:].reshape(32, 1)),
                "gd_convw": convw,
                "p20_dl": np.ascontiguousarray(
                    ii["L0_attn_b_p20_decay_log"].reshape(4, 128).T),
                "p20_th": np.ascontiguousarray(
                    ii["L0_attn_b_p20_theta"].reshape(4, 128).T),
                "gm_p20cW": ii["L0_attn_b_p20c_W"],
                "gm_vnc": vec_cols_host(ii["L0_attn_b_vnorm_scale"]),
                "gm_rlc": vec_cols_host(ii["L0_attn_b_ramp_logit"]),
                "gs_mnt": np.ascontiguousarray(
                    np.tile(ii["L0_attn_b_mnorm_scale"], 16).reshape(1, 1024).astype(np.float32)),
                "go_outw": ii["L0_attn_b_out_W"],
                "go_outb": vec_cols_host(ii["L0_attn_b_out_b"]),
            })
        if K > 4:
            m.update({"a1_wq": ii["L1_attn_a_Wq"], "a1_wk": ii["L1_attn_a_Wk"],
                      "a1_wv": ii["L1_attn_a_Wv"], "a1_wo": ii["L1_attn_a_Wo"],
                      "a1_ab": np.concatenate(
                          [ii["L1_attn_a_alpha_raw"].reshape(16),
                           ii["L1_attn_a_beta_raw"].reshape(16)]).reshape(1, 32).astype(np.float32)})
        if K > 5:
            m.update({"f1_w1": ii["L1_ffn_w1"], "f1_b1": vec_cols_host(ii["L1_ffn_b1"]),
                      "f1_w2": ii["L1_ffn_w2"], "f1_b2": vec_cols_host(ii["L1_ffn_b2"])})
        if K > 6:
            pos = (h * RPC + np.arange(RPC, dtype=np.float32)).reshape(RPC // 128, 128).T
            m.update({"b1_wq": ii["L1_attn_b_Wq"], "b1_wk": ii["L1_attn_b_Wk"],
                      "b1_wv": ii["L1_attn_b_Wv"], "b1_wo": ii["L1_attn_b_Wo"],
                      "b1_wg1": ii["L1_attn_b_Wg1"], "b1_wg2": ii["L1_attn_b_Wg2"],
                      "b1_poscol": np.ascontiguousarray(pos),
                      "b1_gam": vec_cols_host(ii["L1_attn_b_gamma"]),
                      "b1_bet": vec_cols_host(ii["L1_attn_b_beta"])})
        in_maps.append(m)
    res = run_bass_kernel_spmd(_progs[key], in_maps, core_ids=list(range(NCORES)))
    Xn = np.empty_like(X)
    for c in range(NCORES):
        b, h = c // 2, c % 2
        Xn[b, h * RPC:(h + 1) * RPC, :] = res.results[c]["XO"].T
    return Xn


def kernel(**inputs):
    X = np.asarray(inputs["L0_attn_a_x"], dtype=np.float32)
    rest = [sub_l0_attn_a, sub_l0_ffn, sub_l0_attn_b, sub_l0_ffn,
            sub_l1_attn_a, sub_l1_ffn, sub_l1_attn_b, sub_l1_ffn]
    if FUSE_K > 0:
        X = fused_prefix(X, inputs, FUSE_K)
    for fn in rest[FUSE_K:]:
        X = fn(X, inputs)
    return X


# version 13
# speedup vs baseline: 1.2958x; 1.2958x over previous
import contextlib
import math
import numpy as np
import concourse.bass as bass
import concourse.mybir as mybir
from concourse.bass_utils import run_bass_kernel_spmd

F32 = mybir.dt.float32
BF16 = mybir.dt.bfloat16
AF = mybir.ActivationFunctionType
ALU = mybir.AluOpType

B, S, D = 4, 4096, 1024
NCORES = 8
RPC = S // 2
NCH = D // 128
TT = 512
NTT = RPC // TT

_progs = {}


class PB:

    def __init__(self, name):
        self.name = name
        self.nc = bass.Bass("TRN2", target_bir_lowering=False)
        self.ctx = contextlib.ExitStack()
        self.steps = []
        self._uid = 0
        self.io_map = None
        self._stage_ctx = None
        self._stage_pref = ""

    def uid(self):
        self._uid += 1
        return str(self._uid)

    def constcol(self, value, p=128):
        if not hasattr(self, "_consts") or self._consts is None:
            self._consts = {}
        if value not in self._consts:
            t = self.sb(f"const_{len(self._consts)}", [128, 1], F32)
            self.memset(t[:], float(value))
            self._consts[value] = t
        return self._consts[value][0:p, 0:1]

    def inp(self, name, shape, dt=F32):
        if self.io_map is not None:
            ap = self.io_map[name]
            assert tuple(ap.shape) == tuple(shape), (name, ap.shape, shape)
            return ap
        return self.nc.dram_tensor(name, list(shape), dt, kind="ExternalInput").ap()

    def out(self, name, shape, dt=F32):
        if self.io_map is not None:
            ap = self.io_map[name]
            assert tuple(ap.shape) == tuple(shape), (name, ap.shape, shape)
            return ap
        return self.nc.dram_tensor(name, list(shape), dt, kind="ExternalOutput").ap()

    @contextlib.contextmanager
    def stage(self, pref, io_map, stream=True):
        old_ctx, old_map, old_pref = self._stage_ctx, self.io_map, self._stage_pref
        old_stream = getattr(self, "_stream", True)
        self._stage_ctx = contextlib.ExitStack()
        self.io_map = io_map
        self._stage_pref = pref + "_"
        self._stream = stream
        self._consts = {}
        try:
            yield self
        finally:
            self._stage_ctx.close()
            self._stage_ctx, self.io_map, self._stage_pref = old_ctx, old_map, old_pref
            self._stream = old_stream
            self._consts = {}

    def dram(self, name, shape, dt=F32):
        return self.nc.dram_tensor(name, list(shape), dt).ap()

    def sb(self, name, shape, dt=F32):
        ctx = self._stage_ctx if self._stage_ctx is not None else self.ctx
        return ctx.enter_context(
            self.nc.sbuf_tensor(self._stage_pref + name, list(shape), dt))

    def ps(self, name, shape, dt=F32):
        ctx = self._stage_ctx if self._stage_ctx is not None else self.ctx
        return ctx.enter_context(
            self.nc.psum_tensor(self._stage_pref + name, list(shape), dt))

    def add(self, eng, fn, dma=False, chain=False, meta=None):
        self.steps.append((eng, fn, 16 if dma else 1, chain,
                           (meta, getattr(self, "_stream", True))))

    def dma(self, out, in_, eng="sync", allow_slow=False, chain=False):
        self.add(eng, lambda e, o=out, i=in_, asl=allow_slow:
                 e.dma_start(out=o, in_=i, allow_slow_non_contiguous=asl), dma=True,
                 chain=chain)

    def cast_dma(self, out, in_):
        self.add("gpsimd", lambda e, o=out, i=in_: e.dma_start(out=o, in_=i), dma=True)

    def mm(self, out, lhsT, rhs, start=True, stop=True, tile_position=None):
        self.add("tensor", lambda e, o=out, l=lhsT, r=rhs, s1=start, s2=stop,
                 tp=tile_position:
                 e.matmul(o, l, r, start=s1, stop=s2, tile_position=tp),
                 meta=str(lhsT.dtype))

    def act(self, out, in_, func, bias=None, scale=None, accum_out=None, eng="scalar"):
        kw = {}
        if bias is not None:
            kw["bias"] = bias
        if scale is not None:
            kw["scale"] = scale
        if accum_out is not None:
            kw["accum_out"] = accum_out
        self.add(eng, lambda e, o=out, i=in_, f=func, kw=kw:
                 e.activation(out=o, in_=i, func=f, **kw))

    def tt(self, out, a, b, op, eng="vector"):
        self.add(eng, lambda e, o=out, x=a, y=b, p=op:
                 e.tensor_tensor(out=o, in0=x, in1=y, op=p))

    def ts(self, out, in_, s1, op0, s2=None, op1=None, accum_out=None, eng="vector"):
        kw = {}
        if accum_out is not None:
            kw["accum_out"] = accum_out
            if op1 is None:
                s2, op1 = 0.0, ALU.add
        self.add(eng, lambda e, o=out, i=in_, a=s1, b=s2, p0=op0, p1=op1, kw=kw:
                 e.tensor_scalar(out=o, in0=i, scalar1=a, scalar2=b, op0=p0,
                                 **({"op1": p1} if p1 is not None else {}), **kw))

    def stt(self, out, in0, scalar, in1, op0, op1, eng="vector"):
        self.add(eng, lambda e, o=out, i0=in0, s=scalar, i1=in1, p0=op0, p1=op1:
                 e.scalar_tensor_tensor(out=o, in0=i0, scalar=s, in1=i1, op0=p0, op1=p1))

    def copy(self, out, in_, eng="vector"):
        self.add(eng, lambda e, o=out, i=in_: e.tensor_copy(out=o, in_=i))

    def scan(self, out, data0, data1, initial, op0=ALU.mult, op1=ALU.add):
        self.add("vector", lambda e, o=out, d0=data0, d1=data1, ini=initial, p0=op0, p1=op1:
                 e.tensor_tensor_scan(out=o, data0=d0, data1=d1, initial=ini, op0=p0, op1=p1))

    def memset(self, ap, v, eng="gpsimd"):
        self.add(eng, lambda e, a=ap, vv=v: e.memset(a, vv))

    def reduce(self, out, in_, op, eng="vector"):
        self.add(eng, lambda e, o=out, i=in_, p=op:
                 e.tensor_reduce(out=o, in_=i, axis=mybir.AxisListType.X, op=p))

    def recip(self, out, in_):
        self.add("vector", lambda e, o=out, i=in_: e.reciprocal(out=o, in_=i))

    def transpose(self, out_psum, in_, ident):
        self.add("tensor", lambda e, o=out_psum, i=in_, d=ident: e.transpose(o, i, d),
                 meta=str(in_.dtype))

    def affine_select(self, out, in_, pattern, compare_op, fill, channel_multiplier=1,
                      base=None):
        self.add("gpsimd", lambda e, o=out, i=in_, p=pattern, c=compare_op, f=fill,
                 m=channel_multiplier, b=base:
                 e.affine_select(out=o, in_=i, pattern=p, compare_op=c, fill=f,
                                 channel_multiplier=m, base=b))

    def iota(self, out, pattern, base=0, channel_multiplier=0):
        self.add("gpsimd", lambda e, o=out, p=pattern, b=base, m=channel_multiplier:
                 e.iota(out=o, pattern=p, base=b, channel_multiplier=m,
                        allow_small_or_imprecise_dtypes=True))

    def collective(self, kind, op, groups, ins, outs):
        self.steps.append(("gpsimd",
                           lambda e, k=kind, o=op, g=groups, i=ins, u=outs:
                           e.collective_compute(k, o, replica_groups=g, ins=i, outs=u),
                           1, True, (None, True)))

    COALESCE = ()
    OWN_SKIP = False
    ELIDE = True
    ENGS = ("sync", "scalar", "vector", "tensor", "gpsimd")
    SCHED = "new"

    def finish_old(self):
        nc = self.nc
        sem_c = self.ctx.enter_context(nc.semaphore("semc"))
        sem_d = self.ctx.enter_context(nc.semaphore("semd"))
        block = self.ctx.enter_context(nc.Block())
        steps = self.steps
        for eng_name in self.ENGS:
            def make(eng_name):
                def body(eng):
                    c_before = d_before = 0
                    for (e, fn, inc, chain, meta) in steps:
                        if e == eng_name:
                            if c_before:
                                eng.wait_ge(sem_c, c_before)
                            if d_before:
                                eng.wait_ge(sem_d, d_before)
                            fn(eng).then_inc(sem_d if inc == 16 else sem_c, inc)
                        if inc == 16:
                            d_before += 16
                        else:
                            c_before += 1
                return body
            getattr(block, eng_name)(make(eng_name))
        cfin = sum(1 for st in steps if st[2] == 1)
        dfin = sum(16 for st in steps if st[2] == 16)

        @block.vector
        def _(v):
            if cfin:
                v.wait_ge(sem_c, cfin)
            if dfin:
                v.wait_ge(sem_d, dfin)
        self.ctx.close()
        return nc

    def finish(self):
        if self.SCHED == "old":
            return self.finish_old()
        return self.finish_new()

    def finish_new(self):
        nc = self.nc
        if getattr(self, "SHARED_SEMS", False):
            _sr = self.ctx.enter_context(nc.semaphore("r_all"))
            _sd = self.ctx.enter_context(nc.semaphore("d_all"))
            sem_r = {e: _sr for e in self.ENGS}
            sem_dq = {e: _sd for e in ("sync", "gpsimd", "scalar", "vector")}
        else:
            sem_r = {e: self.ctx.enter_context(nc.semaphore("r_" + e))
                     for e in self.ENGS}
            sem_dq = {e: self.ctx.enter_context(nc.semaphore("d_" + e))
                      for e in ("sync", "gpsimd", "scalar", "vector")}
        block = self.ctx.enter_context(nc.Block())
        steps = self.steps
        n = len(steps)
        run_start = [True] * n
        for i in range(1, n):
            e, _, inc, chain, meta = steps[i]
            pe, _, pinc, _, pmeta = steps[i - 1]
            if (e in self.COALESCE and e == pe and inc == pinc and not chain
                    and meta == pmeta and meta[1]):
                run_start[i] = False
        run_end = [True] * n
        for i in range(n - 1):
            run_end[i] = run_start[i + 1]
        for eng_name in self.ENGS:
            def make(eng_name):
                def body(eng):
                    runs_done = {e: 0 for e in self.ENGS}
                    dma_done = {e: 0 for e in sem_dq}
                    lw_r = {e: 0 for e in self.ENGS}
                    lw_d = {e: 0 for e in sem_dq}
                    for i, (e, fn, inc, chain, meta) in enumerate(steps):
                        if e == eng_name:
                            if run_start[i]:
                                own_force = (not meta[1]) or (
                                    e == "tensor" and i > 0
                                    and steps[i - 1][0] == "tensor"
                                    and steps[i - 1][4] != meta)
                                for e2 in self.ENGS:
                                    if e2 == eng_name and e2 == "tensor"                                             and getattr(self, "OWN_SKIP", True):
                                        continue
                                    v = runs_done[e2]
                                    if getattr(self, "ELIDE", True):
                                        if v > lw_r[e2]:
                                            eng.wait_ge(sem_r[e2], v)
                                            lw_r[e2] = v
                                    elif v > 0:
                                        eng.wait_ge(sem_r[e2], v)
                                for q in sem_dq:
                                    v = dma_done[q]
                                    if getattr(self, "ELIDE", True):
                                        if v > lw_d[q]:
                                            eng.wait_ge(sem_dq[q], v)
                                            lw_d[q] = v
                                    elif v > 0:
                                        eng.wait_ge(sem_dq[q], v)
                            inst = fn(eng)
                            if inc == 16:
                                inst.then_inc(sem_dq[e], 16)
                            elif run_end[i]:
                                inst.then_inc(sem_r[e], 1)
                        if inc == 16:
                            dma_done[e] += 16
                        elif run_end[i]:
                            runs_done[e] += 1
                return body
            getattr(block, eng_name)(make(eng_name))
        rfin = {e: 0 for e in self.ENGS}
        dfin = {e: 0 for e in sem_dq}
        for i, (e, fn, inc, chain, meta) in enumerate(steps):
            if inc == 16:
                dfin[e] += 16
            elif run_end[i]:
                rfin[e] += 1

        @block.vector
        def _(v):
            for e in self.ENGS:
                if rfin[e] and e != "vector":
                    v.wait_ge(sem_r[e], rfin[e])
            for q in dfin:
                if dfin[q]:
                    v.wait_ge(sem_dq[q], dfin[q])
        self.ctx.close()
        return nc


def run_prog(key, build_fn, in_maps):
    if key not in _progs:
        pb = PB(key)
        build_fn(pb)
        _progs[key] = pb.finish()
    res = run_bass_kernel_spmd(_progs[key], in_maps, core_ids=list(range(NCORES)))
    return res.results


def emit_ones(pb):
    onc = pb.sb("ones_col", [128, 1], F32)
    onr = pb.sb("ones_row", [1, 128], F32)
    pb.memset(onc[:], 1.0)
    pb.memset(onr[:], 1.0)
    return onc, onr


def load_w_fm(pb, hbm_ap, kdim, ndim, name):
    kch = kdim // 128
    wsb = pb.sb(f"w_{name}", [128, kch * ndim], BF16)
    pb.cast_dma(wsb[:].rearrange("p (kb n) -> p kb n", kb=kch),
                hbm_ap.rearrange("(kb p) n -> p kb n", p=128))
    return lambda kb, j0, j1: wsb[:, kb * ndim + j0: kb * ndim + j1]


def load_vec_cols(pb, hbm_ap, n, name):
    nch = n // 128
    vsb = pb.sb(f"v_{name}", [128, nch], F32)
    pb.dma(vsb[:], hbm_ap)
    return vsb


def vec_cols_host(v):
    v = np.asarray(v, np.float32)
    return np.ascontiguousarray(v.reshape(-1, 128).T)


def make_ln_scratch(pb, pref):
    return {
        "ps_s": pb.ps(f"{pref}_lnps", [1, 2 * TT], F32),
        "ps_b": pb.ps(f"{pref}_lnpb", [128, 2 * TT], F32),
        "sq": pb.sb(f"{pref}_lnsq", [128, TT], F32),
        "srow": pb.sb(f"{pref}_lnrow", [1, 4 * TT], F32),
    }


def emit_ln_tile(pb, scr, x_slices, y_out_fn, onc, onr, w, eps=1e-5):
    ps_s, ps_b, sq, srow = scr["ps_s"], scr["ps_b"], scr["sq"], scr["srow"]
    M = srow[0:1, 0:w]
    RS = srow[0:1, TT:TT + w]
    T1 = srow[0:1, 2 * TT:2 * TT + w]
    for ch in range(NCH):
        pb.mm(ps_s[0:1, 0:w], onc[:], x_slices[ch],
              start=(ch == 0), stop=(ch == NCH - 1))
    for ch in range(NCH):
        pb.act(sq[:, 0:w], x_slices[ch], AF.Square)
        pb.mm(ps_s[0:1, TT:TT + w], onc[:], sq[:, 0:w],
              start=(ch == 0), stop=(ch == NCH - 1))
    pb.ts(M, ps_s[0:1, 0:w], 1.0 / D, ALU.mult)
    pb.ts(T1, ps_s[0:1, TT:TT + w], 1.0 / D, ALU.mult)
    pb.tt(RS, M, M, ALU.mult)
    pb.tt(T1, T1, RS, ALU.subtract)
    pb.act(T1, T1, AF.Sqrt, bias=pb.constcol(eps, 1))
    pb.recip(RS, T1)
    pb.mm(ps_b[:, 0:w], onr[:], M)
    pb.mm(ps_b[:, TT:TT + w], onr[:], RS)
    mb = ps_b[:, 0:w]
    rb = ps_b[:, TT:TT + w]
    for ch in range(NCH):
        pb.tt(sq[:, 0:w], x_slices[ch], mb, ALU.subtract)
        pb.tt(y_out_fn(ch), sq[:, 0:w], rb, ALU.mult)
    return mb, rb


def build_l1_ffn(pb):
    WID = 4096
    WCH = WID // 128
    xr = pb.inp("xr", [D, RPC])
    w1 = pb.inp("w1", [D, WID])
    b1 = pb.inp("b1", [128, WID // 128])
    w2 = pb.inp("w2", [WID, D])
    b2 = pb.inp("b2", [128, D // 128])
    xo = pb.out("xo", [D, RPC])

    onc, onr = emit_ones(pb)
    w1b = load_w_fm(pb, w1, D, WID, "w1")
    w2b = load_w_fm(pb, w2, WID, D, "w2")
    b1c = load_vec_cols(pb, b1, WID, "b1")
    b2c = load_vec_cols(pb, b2, D, "b2")

    xbuf = pb.sb("xbuf", [128, NCH * TT], F32)
    ybuf = pb.sb("ybuf", [128, NCH * TT], BF16)
    hbuf = pb.sb("hbuf", [128, WCH * TT], BF16)
    obuf = pb.sb("obuf", [128, TT], F32)
    ps_h = pb.ps("ps_h", [128, TT], F32)
    ps_o = pb.ps("ps_o", [128, 2 * TT], F32)

    lnscr = make_ln_scratch(pb, "f")
    for t0 in range(0, RPC, TT):
        for ch in range(NCH):
            pb.dma(xbuf[:, ch * TT:(ch + 1) * TT],
                   xr[ch * 128:(ch + 1) * 128, t0:t0 + TT])
        xs = [xbuf[:, ch * TT:(ch + 1) * TT] for ch in range(NCH)]
        emit_ln_tile(pb, lnscr, xs, lambda ch: ybuf[:, ch * TT:(ch + 1) * TT],
                     onc, onr, TT)
        for j in range(WCH):
            for dch in range(NCH):
                pb.mm(ps_h[:], w1b(dch, j * 128, (j + 1) * 128),
                      ybuf[:, dch * TT:(dch + 1) * TT],
                      start=(dch == 0), stop=(dch == NCH - 1))
            pb.act(hbuf[:, j * TT:(j + 1) * TT], ps_h[:], AF.Relu,
                   bias=b1c[:, j:j + 1])
        for half in range(4):
            for oc in range(2):
                och = half * 2 + oc
                for j in range(WCH):
                    pb.mm(ps_o[:, oc * TT:(oc + 1) * TT],
                          w2b(j, och * 128, (och + 1) * 128),
                          hbuf[:, j * TT:(j + 1) * TT],
                          start=(j == 0), stop=(j == WCH - 1))
            for oc in range(2):
                och = half * 2 + oc
                pb.ts(obuf[:], ps_o[:, oc * TT:(oc + 1) * TT], b2c[:, och:och + 1],
                      ALU.add)
                pb.tt(obuf[:], obuf[:], xbuf[:, och * TT:(och + 1) * TT], ALU.add)
                pb.dma(xo[och * 128:(och + 1) * 128, t0:t0 + TT], obuf[:])
    return pb


def sub_l1_ffn(X, inputs):
    in_maps = []
    for c in range(NCORES):
        b, h = c // 2, c % 2
        xr = np.ascontiguousarray(X[b, h * RPC:(h + 1) * RPC, :].T)
        in_maps.append({
            "xr": xr,
            "w1": inputs["L1_ffn_w1"], "b1": vec_cols_host(inputs["L1_ffn_b1"]),
            "w2": inputs["L1_ffn_w2"], "b2": vec_cols_host(inputs["L1_ffn_b2"]),
        })
    res = run_prog("l1_ffn", build_l1_ffn, in_maps)
    Xn = np.empty_like(X)
    for c in range(NCORES):
        b, h = c // 2, c % 2
        Xn[b, h * RPC:(h + 1) * RPC, :] = res[c]["xo"].T
    return Xn


def build_l0_ffn(pb):
    xr_in = pb.inp("xr", [D, RPC])
    xprev = pb.inp("xprev", [2, D])
    rolec = pb.inp("rolec", [128, 1])
    wrT = pb.inp("wrT", [D, D])
    wkT = pb.inp("wkT", [D, D])
    wvT = pb.inp("wvT", [D, D])
    mur = pb.inp("mur", [128, NCH])
    muk = pb.inp("muk", [128, NCH])
    xo = pb.out("xo", [D, RPC])

    onc, onr = emit_ones(pb)
    wrb = load_w_fm(pb, wrT, D, D, "wr")
    wkb = load_w_fm(pb, wkT, D, D, "wk")
    wvb = load_w_fm(pb, wvT, D, D, "wv")
    murc = load_vec_cols(pb, mur, D, "mur")
    mukc = load_vec_cols(pb, muk, D, "muk")

    YW = RPC + 1
    ybuf = pb.sb("ybuf", [128, NCH * YW], BF16)
    xbuf = pb.sb("xbuf", [128, NCH * TT], F32)
    xpbuf = pb.sb("xpbuf", [128, NCH], F32)
    rbuf = pb.sb("rbuf", [128, NCH * TT], F32)
    kbuf = pb.sb("kbuf", [128, NCH * TT], BF16)
    mixr = pb.sb("mixr", [128, NCH * TT], BF16)
    mixk = pb.sb("mixk", [128, NCH * TT], BF16)
    tmp = pb.sb("tmpf", [128, TT], F32)
    obuf = pb.sb("obuf", [128, TT], F32)
    ps_m = pb.ps("ps_m", [128, 2 * TT], F32)
    lnscr = make_ln_scratch(pb, "f0")

    def ycol(ch, a, b):
        return ybuf[:, ch * YW + a: ch * YW + b]

    rcol = pb.sb("rcol", [128, 1], F32)
    pb.dma(rcol[:], rolec)
    pb.dma(xpbuf[:], xprev[0:1, :].rearrange("one (p c) -> p (one c)", p=128))
    ytmp = pb.sb("ybnd", [128, NCH], F32)
    emit_ln_tile(pb, lnscr, [xpbuf[:, ch:ch + 1] for ch in range(NCH)],
                 lambda ch: ytmp[:, ch:ch + 1], onc, onr, 1)
    for ch in range(NCH):
        pb.ts(ytmp[:, ch:ch + 1], ytmp[:, ch:ch + 1], rcol[:, 0:1], ALU.mult)
        pb.copy(ycol(ch, 0, 1), ytmp[:, ch:ch + 1])
    for t0 in range(0, RPC, TT):
        for ch in range(NCH):
            pb.dma(xbuf[:, ch * TT:(ch + 1) * TT],
                   xr_in[ch * 128:(ch + 1) * 128, t0:t0 + TT])
        emit_ln_tile(pb, lnscr, [xbuf[:, ch * TT:(ch + 1) * TT] for ch in range(NCH)],
                     lambda ch: ycol(ch, 1 + t0, 1 + t0 + TT), onc, onr, TT)

    for t0 in range(0, RPC, TT):
        for ch in range(NCH):
            y_cur = ycol(ch, 1 + t0, 1 + t0 + TT)
            y_prv = ycol(ch, t0, t0 + TT)
            for (mc, dst) in ((murc, mixr), (mukc, mixk)):
                pb.ts(tmp[:], y_cur, mc[:, ch:ch + 1], ALU.mult)
                pb.ts(obuf[:], y_prv, mc[:, ch:ch + 1], ALU.mult)
                pb.tt(obuf[:], y_prv, obuf[:], ALU.subtract)
                pb.tt(dst[:, ch * TT:(ch + 1) * TT], tmp[:], obuf[:], ALU.add)
        for och in range(NCH):
            for dch in range(NCH):
                pb.mm(ps_m[:, 0:TT], wrb(dch, och * 128, (och + 1) * 128),
                      mixr[:, dch * TT:(dch + 1) * TT],
                      start=(dch == 0), stop=(dch == NCH - 1))
            pb.act(rbuf[:, och * TT:(och + 1) * TT], ps_m[:, 0:TT], AF.Sigmoid)
            for dch in range(NCH):
                pb.mm(ps_m[:, TT:2 * TT], wkb(dch, och * 128, (och + 1) * 128),
                      mixk[:, dch * TT:(dch + 1) * TT],
                      start=(dch == 0), stop=(dch == NCH - 1))
            pb.act(tmp[:], ps_m[:, TT:2 * TT], AF.Relu)
            pb.act(kbuf[:, och * TT:(och + 1) * TT], tmp[:], AF.Square)
        for och in range(NCH):
            for dch in range(NCH):
                pb.mm(ps_m[:, 0:TT], wvb(dch, och * 128, (och + 1) * 128),
                      kbuf[:, dch * TT:(dch + 1) * TT],
                      start=(dch == 0), stop=(dch == NCH - 1))
            pb.tt(obuf[:], ps_m[:, 0:TT], rbuf[:, och * TT:(och + 1) * TT], ALU.mult)
            pb.dma(xbuf[:, 0:TT], xr_in[och * 128:(och + 1) * 128, t0:t0 + TT])
            pb.tt(obuf[:], obuf[:], xbuf[:, 0:TT], ALU.add)
            pb.dma(xo[och * 128:(och + 1) * 128, t0:t0 + TT], obuf[:])
    return pb


def sub_l0_ffn(X, inputs):
    in_maps = []
    wrT = np.ascontiguousarray(np.asarray(inputs["L0_ffn_W_r"]).T)
    wkT = np.ascontiguousarray(np.asarray(inputs["L0_ffn_W_k"]).T)
    wvT = np.ascontiguousarray(np.asarray(inputs["L0_ffn_W_v"]).T)
    mur = vec_cols_host(inputs["L0_ffn_mu_r"])
    muk = vec_cols_host(inputs["L0_ffn_mu_k"])
    for c in range(NCORES):
        b, h = c // 2, c % 2
        xr = np.ascontiguousarray(X[b, h * RPC:(h + 1) * RPC, :].T)
        xp = np.zeros((2, D), np.float32)
        xp[0] = X[b, RPC - 1].reshape(NCH, 128).T.flatten()
        role = np.full((128, 1), float(h), np.float32)
        in_maps.append({"xr": xr, "xprev": xp, "rolec": role, "wrT": wrT,
                        "wkT": wkT, "wvT": wvT, "mur": mur, "muk": muk})
    res = run_prog("l0_ffn", build_l0_ffn, in_maps)
    Xn = np.empty_like(X)
    for c in range(NCORES):
        b, h = c // 2, c % 2
        Xn[b, h * RPC:(h + 1) * RPC, :] = res[c]["xo"].T
    return Xn


def build_attn_a_qkv(pb):
    xr = pb.inp("xr", [D, RPC])
    wqT = pb.inp("wqT", [D, D]); wkT = pb.inp("wkT", [D, D]); wvT = pb.inp("wvT", [D, D])
    bq = pb.inp("bq", [128, NCH]); bk = pb.inp("bk", [128, NCH]); bv = pb.inp("bv", [128, NCH])
    Qb = pb.out("Qb", [D, RPC], BF16)
    Kb = pb.out("Kb", [D, RPC], BF16)
    Vb = pb.out("Vb", [D, RPC], BF16)
    rq = pb.out("rq", [1, RPC])
    rk = pb.out("rk", [1, RPC])

    onc, onr = emit_ones(pb)
    wq = load_w_fm(pb, wqT, D, D, "wq")
    wk = load_w_fm(pb, wkT, D, D, "wk")
    wv = load_w_fm(pb, wvT, D, D, "wv")
    bqc = load_vec_cols(pb, bq, D, "bq")
    bkc = load_vec_cols(pb, bk, D, "bk")
    bvc = load_vec_cols(pb, bv, D, "bv")

    xbuf = pb.sb("xbuf", [128, NCH * TT], F32)
    ybuf = pb.sb("ybuf", [128, NCH * TT], BF16)
    qf = pb.sb("qf", [128, TT], F32)
    sq = pb.sb("sqn", [128, TT], F32)
    stage = pb.sb("stage", [128, TT], BF16)
    nrow = pb.sb("nrow", [1, TT], F32)
    ps_p = pb.ps("ps_p", [128, TT], F32)
    ps_n = pb.ps("ps_n", [1, TT], F32)
    lnscr = make_ln_scratch(pb, "aa")

    for t0 in range(0, RPC, TT):
        for ch in range(NCH):
            pb.dma(xbuf[:, ch * TT:(ch + 1) * TT],
                   xr[ch * 128:(ch + 1) * 128, t0:t0 + TT])
        emit_ln_tile(pb, lnscr, [xbuf[:, ch * TT:(ch + 1) * TT] for ch in range(NCH)],
                     lambda ch: ybuf[:, ch * TT:(ch + 1) * TT], onc, onr, TT)
        for (wb, bc, OUT, rr) in ((wq, bqc, Qb, rq), (wk, bkc, Kb, rk)):
            for och in range(NCH):
                for dch in range(NCH):
                    pb.mm(ps_p[:], wb(dch, och * 128, (och + 1) * 128),
                          ybuf[:, dch * TT:(dch + 1) * TT],
                          start=(dch == 0), stop=(dch == NCH - 1))
                pb.ts(qf[:], ps_p[:], bc[:, och:och + 1], ALU.add)
                pb.copy(stage[:], qf[:])
                pb.dma(OUT[och * 128:(och + 1) * 128, t0:t0 + TT], stage[:])
                pb.act(sq[:], qf[:], AF.Square)
                pb.mm(ps_n[0:1, :], onc[:], sq[:],
                      start=(och == 0), stop=(och == NCH - 1))
            pb.act(nrow[0:1, :], ps_n[0:1, :], AF.Sqrt)
            pb.ts(nrow[0:1, :], nrow[0:1, :], 1e-8, ALU.add)
            pb.recip(nrow[0:1, :], nrow[0:1, :])
            pb.dma(rr[0:1, t0:t0 + TT], nrow[0:1, :])
        for och in range(NCH):
            for dch in range(NCH):
                pb.mm(ps_p[:], wv(dch, och * 128, (och + 1) * 128),
                      ybuf[:, dch * TT:(dch + 1) * TT],
                      start=(dch == 0), stop=(dch == NCH - 1))
            pb.ts(qf[:], ps_p[:], bvc[:, och:och + 1], ALU.add)
            pb.copy(stage[:], qf[:])
            pb.dma(Vb[och * 128:(och + 1) * 128, t0:t0 + TT], stage[:])
    return pb


def build_attn_a_av(pb):
    NKT = S // 128
    xr = pb.inp("xr", [D, RPC])
    Qb = pb.inp("Qb", [D, RPC], BF16)
    Kt = pb.inp("Kt", [NKT, 128, NCH * 128], BF16)
    Vt = pb.inp("Vt", [S, D], BF16)
    rqv = pb.inp("rq", [1, RPC])
    rkc = pb.inp("rkc", [128, NKT])
    woT = pb.inp("woT", [D, D])
    bo = pb.inp("bo", [128, NCH])
    xo = pb.out("xo", [D, RPC])

    onc, onr = emit_ones(pb)
    oncb = pb.sb("oncb", [128, 1], BF16)
    pb.memset(oncb[:], 1.0)
    wo = load_w_fm(pb, woT, D, D, "wo")
    boc = load_vec_cols(pb, bo, D, "bo")

    qbuf = pb.sb("qbuf", [128, NCH * RPC], BF16)
    vbuf = pb.sb("vbuf", [128, NKT * D], BF16)
    kbuf = pb.sb("kbuf", [128, NCH * 128], BF16)
    rkcol = pb.sb("rkcol", [128, NKT], F32)
    rqb = pb.sb("rqb", [128, TT], F32)
    ebuf = pb.sb("ebuf", [128, NKT * TT], BF16)
    zbuf = pb.sb("zbuf", [128, TT], F32)
    sbuf2 = pb.sb("sbuf2", [128, TT], F32)
    att = pb.sb("att", [128, NCH * TT], BF16)
    dben = pb.sb("dben", [128, TT], F32)
    drow = pb.sb("drow", [1, TT], F32)
    obuf = pb.sb("obuf", [128, TT], F32)
    xbuf = pb.sb("xbuf", [128, TT], F32)
    ps_s = pb.ps("ps_s", [128, TT], F32)
    ps_d = pb.ps("ps_d", [1, TT], F32)
    ps_b = pb.ps("ps_b", [128, TT], F32)
    ps_av = pb.ps("ps_av", [128, TT], F32)
    ps_o = pb.ps("ps_o", [128, TT], F32)

    pb.dma(qbuf[:].rearrange("p (dc t) -> p dc t", dc=NCH),
           Qb.rearrange("(dc p) t -> p dc t", p=128))
    pb.dma(vbuf[:].rearrange("p (kt d) -> p kt d", kt=NKT),
           Vt.rearrange("(kt p) d -> p kt d", p=128))
    pb.dma(rkcol[:], rkc)
    rqrow = pb.sb("rqrow", [1, RPC], F32)
    pb.dma(rqrow[:], rqv)

    mpi = -math.pi / 2.0
    for t0 in range(0, RPC, TT):
        pb.mm(ps_b[:], onr[:], rqrow[0:1, t0:t0 + TT])
        pb.copy(rqb[:], ps_b[:])
        for kt in range(NKT):
            pb.dma(kbuf[:], Kt[kt])
            for dch in range(NCH):
                pb.mm(ps_s[:], kbuf[:, dch * 128:(dch + 1) * 128],
                      qbuf[:, dch * RPC + t0: dch * RPC + t0 + TT],
                      start=(dch == 0), stop=(dch == NCH - 1))
            pb.ts(zbuf[:], ps_s[:], rkcol[:, kt:kt + 1], ALU.mult)
            pb.tt(zbuf[:], zbuf[:], rqb[:], ALU.mult)
            pb.act(zbuf[:], zbuf[:], AF.Abs)
            pb.act(sbuf2[:], zbuf[:], AF.Sin, bias=pb.constcol(mpi), scale=math.pi)
            pb.tt(zbuf[:], sbuf2[:], ps_s[:], ALU.mult)
            pb.act(ebuf[:, kt * TT:(kt + 1) * TT], zbuf[:], AF.Exp,
                   scale=-1.0 / 32.0)
            pb.mm(ps_d[0:1, :], oncb[:], ebuf[:, kt * TT:(kt + 1) * TT],
                  start=(kt == 0), stop=(kt == NKT - 1))
        pb.recip(drow[0:1, :], ps_d[0:1, :])
        pb.mm(ps_b[:], onr[:], drow[0:1, :])
        pb.copy(dben[:], ps_b[:])
        for dv in range(NCH):
            for kt in range(NKT):
                pb.mm(ps_av[:], vbuf[:, kt * D + dv * 128: kt * D + (dv + 1) * 128],
                      ebuf[:, kt * TT:(kt + 1) * TT],
                      start=(kt == 0), stop=(kt == NKT - 1))
            pb.tt(att[:, dv * TT:(dv + 1) * TT], ps_av[:], dben[:], ALU.mult)
        for och in range(NCH):
            for dv in range(NCH):
                pb.mm(ps_o[:], wo(dv, och * 128, (och + 1) * 128),
                      att[:, dv * TT:(dv + 1) * TT],
                      start=(dv == 0), stop=(dv == NCH - 1))
            pb.ts(obuf[:], ps_o[:], boc[:, och:och + 1], ALU.add)
            pb.dma(xbuf[:], xr[och * 128:(och + 1) * 128, t0:t0 + TT])
            pb.tt(obuf[:], obuf[:], xbuf[:], ALU.add)
            pb.dma(xo[och * 128:(och + 1) * 128, t0:t0 + TT], obuf[:])
    return pb


def _tile_k_host(Kfm):
    NKT = Kfm.shape[1] // 128
    a = Kfm.reshape(NCH, 128, NKT, 128)
    a = np.transpose(a, (2, 1, 0, 3))
    return np.ascontiguousarray(a.reshape(NKT, 128, NCH * 128))


def sub_l0_attn_a(X, inputs):
    ii = {k: np.asarray(v) for k, v in inputs.items()}
    wqT = np.ascontiguousarray(ii["L0_attn_a_Wq"].T)
    wkT = np.ascontiguousarray(ii["L0_attn_a_Wk"].T)
    wvT = np.ascontiguousarray(ii["L0_attn_a_Wv"].T)
    woT = np.ascontiguousarray(ii["L0_attn_a_Wo"].T)
    in_maps = []
    for c in range(NCORES):
        b, h = c // 2, c % 2
        xr = np.ascontiguousarray(X[b, h * RPC:(h + 1) * RPC, :].T)
        in_maps.append({
            "xr": xr, "wqT": wqT, "wkT": wkT, "wvT": wvT,
            "bq": vec_cols_host(ii["L0_attn_a_bq"]),
            "bk": vec_cols_host(ii["L0_attn_a_bk"]),
            "bv": vec_cols_host(ii["L0_attn_a_bv"]),
        })
    r1 = run_prog("attn_a_qkv", build_attn_a_qkv, in_maps)

    in_maps2 = []
    for c in range(NCORES):
        b, h = c // 2, c % 2
        pair0, pair1 = 2 * b, 2 * b + 1
        Kfull = np.concatenate([r1[pair0]["Kb"], r1[pair1]["Kb"]], axis=1)
        Vfull = np.concatenate([r1[pair0]["Vb"], r1[pair1]["Vb"]], axis=1)
        rkfull = np.concatenate([r1[pair0]["rk"], r1[pair1]["rk"]], axis=1)
        xr = np.ascontiguousarray(X[b, h * RPC:(h + 1) * RPC, :].T)
        in_maps2.append({
            "xr": xr,
            "Qb": r1[c]["Qb"],
            "Kt": _tile_k_host(Kfull),
            "Vt": np.ascontiguousarray(Vfull.T),
            "rq": r1[c]["rq"],
            "rkc": np.ascontiguousarray(rkfull.reshape(S // 128, 128).T),
            "woT": woT,
            "bo": vec_cols_host(ii["L0_attn_a_bo"]),
        })
    r2 = run_prog("attn_a_av", build_attn_a_av, in_maps2)
    Xn = np.empty_like(X)
    for c in range(NCORES):
        b, h = c // 2, c % 2
        Xn[b, h * RPC:(h + 1) * RPC, :] = r2[c]["xo"].T
    return Xn


def build_attn_b1_qkv(pb):
    xr = pb.inp("xr", [D, RPC])
    wq = pb.inp("wq", [D, D]); wk = pb.inp("wk", [D, D]); wv = pb.inp("wv", [D, D])
    wg1 = pb.inp("wg1", [D, 64]); wg2 = pb.inp("wg2", [64, D])
    poscol = pb.inp("poscol", [128, RPC // 128])
    qr = pb.out("qr", [RPC, D], BF16)
    kr = pb.out("kr", [RPC, D], BF16)
    vt = pb.out("vt", [RPC, D], BF16)
    gate = pb.out("gate", [D, RPC])

    onc, onr = emit_ones(pb)
    wqb = load_w_fm(pb, wq, D, D, "wq")
    wkb = load_w_fm(pb, wk, D, D, "wk")
    wvb = load_w_fm(pb, wv, D, D, "wv")
    wg1b = load_w_fm(pb, wg1, D, 64, "wg1")
    wg2s = pb.sb("w_wg2", [64, D], BF16)
    pb.cast_dma(wg2s[:], wg2)

    xbuf = pb.sb("xbuf", [128, NCH * TT], F32)
    ybuf = pb.sb("ybuf", [128, NCH * RPC], BF16)
    pcol = pb.sb("pcol", [128, RPC // 128], F32)
    fbc = pb.sb("fbc", [128, 32], F32)
    cosb = pb.sb("cosb", [128, 32], F32)
    sinb = pb.sb("sinb", [128, 32], F32)
    angb = pb.sb("angb", [128, 32], F32)
    angm = pb.sb("angm", [128, 32], F32)
    qs = pb.sb("qs", [128, TT], F32)
    rt1 = pb.sb("rt1", [128, 256], F32)
    rt2 = pb.sb("rt2", [128, 256], F32)
    rop = pb.sb("rop", [128, TT], BF16)
    g1buf = pb.sb("g1buf", [64, TT], BF16)
    gf = pb.sb("gf", [128, TT], F32)
    ps_q = pb.ps("ps_q", [128, TT], F32)
    ps_g = pb.ps("ps_g", [64, TT], F32)
    ps_r = pb.ps("ps_r", [128, 32], F32)
    lnscr = make_ln_scratch(pb, "b1")

    pb.dma(pcol[:], poscol)
    pb.iota(fbc[0:1, :], [[1, 32]])
    pb.act(fbc[0:1, :], fbc[0:1, :], AF.Exp, scale=-math.log(10000.0) / 32.0)
    pb.ts(fbc[0:1, :], fbc[0:1, :], 1.0 / (2.0 * math.pi), ALU.mult)
    pb.mm(ps_r[:], onr[:], fbc[0:1, :])
    pb.copy(fbc[:], ps_r[:])
    angi = pb.sb("angi", [128, 32], mybir.dt.int32)

    for t0 in range(0, RPC, TT):
        for ch in range(NCH):
            pb.dma(xbuf[:, ch * TT:(ch + 1) * TT],
                   xr[ch * 128:(ch + 1) * 128, t0:t0 + TT])
        emit_ln_tile(pb, lnscr,
                     [xbuf[:, ch * TT:(ch + 1) * TT] for ch in range(NCH)],
                     lambda ch: ybuf[:, ch * RPC + t0: ch * RPC + t0 + TT],
                     onc, onr, TT)

    PI = math.pi
    for tr in range(RPC // 128):
        t0 = tr * 128
        pb.ts(angb[:], fbc[:], pcol[:, tr:tr + 1], ALU.mult)
        pb.copy(angi[:], angb[:])
        pb.copy(angm[:], angi[:])
        pb.tt(angm[:], angb[:], angm[:], ALU.subtract)
        pb.act(sinb[:], angm[:], AF.Sin, scale=2 * PI)
        pb.ts(angb[:], angb[:], 0.25, ALU.add)
        pb.copy(angi[:], angb[:])
        pb.copy(angm[:], angi[:])
        pb.tt(angm[:], angb[:], angm[:], ALU.subtract)
        pb.act(cosb[:], angm[:], AF.Sin, scale=2 * PI)
        for (wb, OUT, do_rope) in ((wqb, qr, True), (wkb, kr, True), (wvb, vt, False)):
            for js in range(2):
                for dch in range(NCH):
                    pb.mm(ps_q[:], ybuf[:, dch * RPC + t0: dch * RPC + t0 + 128],
                          wb(dch, js * TT, (js + 1) * TT),
                          start=(dch == 0), stop=(dch == NCH - 1))
                if not do_rope:
                    pb.copy(rop[:], ps_q[:])
                    pb.dma(OUT[t0:t0 + 128, js * TT:(js + 1) * TT], rop[:])
                    continue
                pb.act(qs[:], ps_q[:], AF.Silu)
                v3 = qs[:].rearrange("p (h two j) -> p h two j", two=2, j=32)
                o3 = rop[:].rearrange("p (h two j) -> p h two j", two=2, j=32)
                cb = cosb[:].unsqueeze(1).broadcast_to((128, 8, 32))
                sb_ = sinb[:].unsqueeze(1).broadcast_to((128, 8, 32))
                x1 = v3[:, :, 0, :]
                x2 = v3[:, :, 1, :]
                t1 = rt1[:].rearrange("p (h j) -> p h j", j=32)
                t2 = rt2[:].rearrange("p (h j) -> p h j", j=32)
                pb.tt(t1, x1, cb, ALU.mult)
                pb.tt(t2, x2, sb_, ALU.mult)
                pb.tt(o3[:, :, 0, :], t1, t2, ALU.subtract)
                pb.tt(t1, x1, sb_, ALU.mult)
                pb.tt(t2, x2, cb, ALU.mult)
                pb.tt(o3[:, :, 1, :], t1, t2, ALU.add)
                pb.dma(OUT[t0:t0 + 128, js * TT:(js + 1) * TT], rop[:])
    for t0 in range(0, RPC, TT):
        for dch in range(NCH):
            pb.mm(ps_g[:], wg1b(dch, 0, 64),
                  ybuf[:, dch * RPC + t0: dch * RPC + t0 + TT],
                  start=(dch == 0), stop=(dch == NCH - 1))
        pb.copy(g1buf[:], ps_g[:])
        for och in range(NCH):
            pb.mm(ps_q[:], wg2s[:, och * 128:(och + 1) * 128], g1buf[:])
            pb.act(gf[:], ps_q[:], AF.Sigmoid)
            pb.dma(gate[och * 128:(och + 1) * 128, t0:t0 + TT], gf[:])
    return pb


def build_attn_b1_scan(pb):
    NH = 8
    C = 64
    NC_ = S // C
    F = NH * 64
    qfm = pb.inp("qfm", [4, 128, S], BF16)
    kfm = pb.inp("kfm", [4, 128, S], BF16)
    ktm = pb.inp("ktm", [S, F], BF16)
    vtm = pb.inp("vtm", [S, F], BF16)
    ot = pb.out("ot", [S, F])

    qsb = pb.sb("qsb", [128, 4 * S], BF16)
    ksb = pb.sb("ksb", [128, 4 * S], BF16)
    ktsb = pb.sb("ktsb", [64, NC_ * F], BF16)
    vtsb = pb.sb("vtsb", [64, NC_ * F], BF16)
    Sf = pb.sb("Sf", [64, F], F32)
    Sb = pb.sb("Sb", [128, 4 * 64], BF16)
    ebuf = pb.sb("ebuf", [64, F], BF16)
    mask = pb.sb("maskt", [64, F], BF16)
    ostage = pb.sb("ostage", [64, F], F32)
    pstage = pb.sb("pstage", [64, F], F32)
    ps_e = pb.ps("ps_e", [64, F], F32)
    ps_o = pb.ps("ps_o", [64, F], F32)
    ps_s = pb.ps("ps_s", [64, F], F32)

    for hp in range(4):
        pb.dma(qsb[:, hp * S:(hp + 1) * S], qfm[hp])
        pb.dma(ksb[:, hp * S:(hp + 1) * S], kfm[hp])
    pb.dma(ktsb[:].rearrange("p (c f) -> p c f", f=F),
           ktm.rearrange("(c p) f -> p c f", p=C))
    pb.dma(vtsb[:].rearrange("p (c f) -> p c f", f=F),
           vtm.rearrange("(c p) f -> p c f", p=C))
    pb.memset(mask[:], 1.0)
    pb.affine_select(mask[:], mask[:], [[0, NH], [1, C]], ALU.is_ge, 0.0,
                     channel_multiplier=-1)
    pb.memset(Sf[:], 0.0, eng="vector")
    for h in range(NH):
        hp, par = divmod(h, 2)
        pb.copy(Sb[64 * par:64 * par + 64, hp * 64:(hp + 1) * 64],
                Sf[:, h * 64:(h + 1) * 64])

    for c in range(NC_):
        t0 = c * C
        for h in range(NH):
            hp, par = divmod(h, 2)
            pb.mm(ps_e[:, h * 64:(h + 1) * 64],
                  ksb[64 * par:64 * par + 64, hp * S + t0: hp * S + t0 + C],
                  qsb[64 * par:64 * par + 64, hp * S + t0: hp * S + t0 + C])
        pb.tt(ebuf[:], ps_e[:], mask[:], ALU.mult)
        for h in range(NH):
            hp, par = divmod(h, 2)
            pb.mm(ps_o[:, h * 64:(h + 1) * 64], ebuf[:, h * 64:(h + 1) * 64],
                  vtsb[0:64, c * F + h * 64: c * F + (h + 1) * 64],
                  start=True, stop=False)
            pb.mm(ps_o[:, h * 64:(h + 1) * 64],
                  qsb[64 * par:64 * par + 64, hp * S + t0: hp * S + t0 + C],
                  Sb[64 * par:64 * par + 64, hp * 64:(hp + 1) * 64],
                  start=False, stop=True)
        pb.copy(ostage[:], ps_o[:])
        pb.dma(ot[t0:t0 + C, :], ostage[:])
        for h in range(NH):
            pb.mm(ps_s[:, h * 64:(h + 1) * 64],
                  ktsb[0:64, c * F + h * 64: c * F + (h + 1) * 64],
                  vtsb[0:64, c * F + h * 64: c * F + (h + 1) * 64])
        pb.tt(Sf[:], Sf[:], ps_s[:], ALU.add)
        for h in range(NH):
            hp, par = divmod(h, 2)
            pb.copy(Sb[64 * par:64 * par + 64, hp * 64:(hp + 1) * 64],
                    Sf[:, h * 64:(h + 1) * 64])
    return pb


def build_attn_b1_out(pb):
    xr = pb.inp("xr", [D, RPC])
    ofm = pb.inp("ofm", [D, RPC])
    gfm = pb.inp("gfm", [D, RPC])
    wo = pb.inp("wo", [D, D])
    gam = pb.inp("gam", [128, NCH])
    bet = pb.inp("bet", [128, NCH])
    xo = pb.out("xo", [D, RPC])

    onc, onr = emit_ones(pb)
    wob = load_w_fm(pb, wo, D, D, "wo")
    gamc = load_vec_cols(pb, gam, D, "gam")
    betc = load_vec_cols(pb, bet, D, "bet")

    abuf = pb.sb("abuf", [128, NCH * TT], F32)
    ybuf = pb.sb("ybuf", [128, NCH * TT], BF16)
    tmp = pb.sb("tmpo", [128, TT], F32)
    obuf = pb.sb("obuf", [128, TT], F32)
    ps_p = pb.ps("ps_p", [128, TT], F32)
    lnscr = make_ln_scratch(pb, "b1o")

    for t0 in range(0, RPC, TT):
        for ch in range(NCH):
            pb.dma(abuf[:, ch * TT:(ch + 1) * TT],
                   ofm[ch * 128:(ch + 1) * 128, t0:t0 + TT])
            pb.dma(tmp[:], gfm[ch * 128:(ch + 1) * 128, t0:t0 + TT])
            pb.tt(abuf[:, ch * TT:(ch + 1) * TT],
                  abuf[:, ch * TT:(ch + 1) * TT], tmp[:], ALU.mult)
        xs = [abuf[:, ch * TT:(ch + 1) * TT] for ch in range(NCH)]

        def yout(ch):
            return ybuf[:, ch * TT:(ch + 1) * TT]
        emit_ln_tile(pb, lnscr, xs, yout, onc, onr, TT)
        for ch in range(NCH):
            pb.ts(ybuf[:, ch * TT:(ch + 1) * TT], ybuf[:, ch * TT:(ch + 1) * TT],
                  gamc[:, ch:ch + 1], ALU.mult, betc[:, ch:ch + 1], ALU.add)
        for och in range(NCH):
            for dch in range(NCH):
                pb.mm(ps_p[:], wob(dch, och * 128, (och + 1) * 128),
                      ybuf[:, dch * TT:(dch + 1) * TT],
                      start=(dch == 0), stop=(dch == NCH - 1))
            pb.dma(tmp[:], xr[och * 128:(och + 1) * 128, t0:t0 + TT])
            pb.tt(obuf[:], ps_p[:], tmp[:], ALU.add)
            pb.dma(xo[och * 128:(och + 1) * 128, t0:t0 + TT], obuf[:])
    return pb


def sub_l1_attn_b(X, inputs):
    ii = {k: np.asarray(v) for k, v in inputs.items()}
    in_maps = []
    for c in range(NCORES):
        b, h = c // 2, c % 2
        xr = np.ascontiguousarray(X[b, h * RPC:(h + 1) * RPC, :].T)
        qoff = h * RPC
        pos = (qoff + np.arange(RPC, dtype=np.float32)).reshape(RPC // 128, 128).T
        in_maps.append({
            "xr": xr, "wq": ii["L1_attn_b_Wq"], "wk": ii["L1_attn_b_Wk"],
            "wv": ii["L1_attn_b_Wv"], "wg1": ii["L1_attn_b_Wg1"],
            "wg2": ii["L1_attn_b_Wg2"],
            "poscol": np.ascontiguousarray(pos),
        })
    r1 = run_prog("attn_b1_qkv", build_attn_b1_qkv, in_maps)

    in_maps2 = []
    for c in range(NCORES):
        b, hh = c // 2, c % 2
        qr = np.concatenate([r1[2 * b]["qr"], r1[2 * b + 1]["qr"]], axis=0)
        kr = np.concatenate([r1[2 * b]["kr"], r1[2 * b + 1]["kr"]], axis=0)
        vt = np.concatenate([r1[2 * b]["vt"], r1[2 * b + 1]["vt"]], axis=0)
        def fmpack(a):
            af = np.ascontiguousarray(a.T)
            af = af.reshape(16, 64, S)[hh * 8:(hh + 1) * 8]
            return np.ascontiguousarray(af.reshape(4, 128, S))
        cols = slice(hh * 512, (hh + 1) * 512)
        in_maps2.append({
            "qfm": fmpack(qr), "kfm": fmpack(kr),
            "ktm": np.ascontiguousarray(kr[:, cols]),
            "vtm": np.ascontiguousarray(vt[:, cols]),
        })
    r2 = run_prog("attn_b1_scan", build_attn_b1_scan, in_maps2)

    in_maps3 = []
    for c in range(NCORES):
        b, h = c // 2, c % 2
        o_full = np.concatenate([r2[2 * b]["ot"], r2[2 * b + 1]["ot"]], axis=1)
        ofm = np.ascontiguousarray(o_full[h * RPC:(h + 1) * RPC, :].T)
        xr = np.ascontiguousarray(X[b, h * RPC:(h + 1) * RPC, :].T)
        in_maps3.append({
            "xr": xr, "ofm": ofm, "gfm": r1[c]["gate"],
            "wo": ii["L1_attn_b_Wo"],
            "gam": vec_cols_host(ii["L1_attn_b_gamma"]),
            "bet": vec_cols_host(ii["L1_attn_b_beta"]),
        })
    r3 = run_prog("attn_b1_out", build_attn_b1_out, in_maps3)
    Xn = np.empty_like(X)
    for c in range(NCORES):
        b, h = c // 2, c % 2
        Xn[b, h * RPC:(h + 1) * RPC, :] = r3[c]["xo"].T
    return Xn


def build_attn_a1_qkv(pb):
    xr = pb.inp("xr", [D, RPC])
    wq = pb.inp("wq", [D, D]); wk = pb.inp("wk", [D, D]); wv = pb.inp("wv", [D, D])
    qt = pb.out("qt", [RPC, D], BF16)
    kt = pb.out("kt", [RPC, D], BF16)
    vt = pb.out("vt", [RPC, D], BF16)

    onc, onr = emit_ones(pb)
    wqb = load_w_fm(pb, wq, D, D, "wq")
    wkb = load_w_fm(pb, wk, D, D, "wk")
    wvb = load_w_fm(pb, wv, D, D, "wv")

    xbuf = pb.sb("xbuf", [128, NCH * TT], F32)
    ybuf = pb.sb("ybuf", [128, NCH * RPC], BF16)
    kf = pb.sb("kf", [128, TT], F32)
    sq = pb.sb("sq2", [128, TT], F32)
    rn = pb.sb("rn", [128, 16], F32)
    rop = pb.sb("rop2", [128, TT], BF16)
    ps_q = pb.ps("ps_q", [128, TT], F32)
    lnscr = make_ln_scratch(pb, "a1")

    for t0 in range(0, RPC, TT):
        for ch in range(NCH):
            pb.dma(xbuf[:, ch * TT:(ch + 1) * TT],
                   xr[ch * 128:(ch + 1) * 128, t0:t0 + TT])
        emit_ln_tile(pb, lnscr,
                     [xbuf[:, ch * TT:(ch + 1) * TT] for ch in range(NCH)],
                     lambda ch: ybuf[:, ch * RPC + t0: ch * RPC + t0 + TT],
                     onc, onr, TT)
    for tr in range(RPC // 128):
        t0 = tr * 128
        for (wb, OUT, norm) in ((wqb, qt, False), (wkb, kt, True), (wvb, vt, False)):
            for js in range(2):
                pb_slice = slice(js * TT, (js + 1) * TT)
                for dch in range(NCH):
                    pb.mm(ps_q[:], ybuf[:, dch * RPC + t0: dch * RPC + t0 + 128],
                          wb(dch, js * TT, (js + 1) * TT),
                          start=(dch == 0), stop=(dch == NCH - 1))
                if not norm:
                    pb.copy(rop[:], ps_q[:])
                    pb.dma(OUT[t0:t0 + 128, js * TT:(js + 1) * TT], rop[:])
                    continue
                pb.copy(kf[:], ps_q[:])
                pb.act(sq[:], kf[:], AF.Square)
                pb.reduce(rn[:, 0:8], sq[:].rearrange("p (h d) -> p h d", d=64),
                          ALU.add)
                pb.act(rn[:, 0:8], rn[:, 0:8], AF.Sqrt)
                pb.ts(rn[:, 0:8], rn[:, 0:8], 1e-12, ALU.max)
                pb.recip(rn[:, 0:8], rn[:, 0:8])
                pb.tt(rop[:].rearrange("p (h d) -> p h d", d=64),
                      kf[:].rearrange("p (h d) -> p h d", d=64),
                      rn[:, 0:8].unsqueeze(2).broadcast_to((128, 8, 64)), ALU.mult)
                pb.dma(OUT[t0:t0 + 128, js * TT:(js + 1) * TT], rop[:])
    return pb


def build_attn_a1_scan(pb):
    NH, C = 8, 64
    NC_ = S // C
    F = NH * 64
    qfm = pb.inp("qfm", [4, 128, S], BF16)
    kfm = pb.inp("kfm", [4, 128, S], BF16)
    ktm = pb.inp("ktm", [S, F], BF16)
    vtm = pb.inp("vtm", [S, F], BF16)
    ab_raw = pb.inp("ab_raw", [1, 16])
    ot = pb.out("ot", [S, F])

    qsb = pb.sb("qsb", [128, 4 * S], BF16)
    ksb = pb.sb("ksb", [128, 4 * S], BF16)
    ktsb = pb.sb("ktsb", [128, (NC_ // 2) * F], BF16)
    vtsb = pb.sb("vtsb", [128, (NC_ // 2) * F], BF16)
    kst = pb.sb("kst", [64, F], BF16)
    vst = pb.sb("vst", [64, F], BF16)
    abr = pb.sb("abr", [1, 16], F32)
    maskA_SL = pb.sb("maskA_SL", [64, F], F32)
    maskA_SU = pb.sb("maskA_SU", [64, F], F32)
    maskSU = pb.sb("maskSU", [64, F], BF16)
    Ib = pb.sb("Ib", [64, F], F32)
    b3sb = pb.sb("b3sb", [1, 16], F32)
    Mb = pb.sb("Mb", [64, F], BF16)
    MTb = pb.sb("MTb", [64, F], BF16)
    POW = [pb.sb(f"POW{l}", [64, F], BF16) for l in range(4)]
    POWT = [pb.sb(f"POWT{l}", [64, F], BF16) for l in range(4)]
    P32 = pb.sb("P32", [64, F], BF16)
    gT = pb.sb("gT", [64, F], BF16)
    Xf = pb.sb("Xf", [64, F], F32)
    Xb = pb.sb("Xb", [64, F], BF16)
    Rall = pb.sb("Rall", [64, NH * 128], BF16)
    Ub = pb.sb("Ub", [64, NH * 128], BF16)
    ghT = pb.sb("ghT", [128, 4 * 64], BF16)
    Rtb = pb.sb("Rtb", [64, F], BF16)
    Sf = pb.sb("Sf", [64, F], F32)
    Sb0 = pb.sb("Sb0", [64, F], BF16)
    Sbp = pb.sb("Sbp", [128, 4 * 64], BF16)
    ostage = pb.sb("ostage", [64, F], F32)
    pstage = pb.sb("pstage", [64, F], F32)
    ps_a = pb.ps("ps_a", [64, F], F32)
    ps_b2 = pb.ps("ps_b2", [64, F], F32)
    ps_u = pb.ps("ps_u", [64, NH * 128], F32)
    ps_y = pb.ps("ps_y", [64, F], F32)
    ps_s = pb.ps("ps_s", [64, F], F32)

    for hp in range(4):
        pb.dma(qsb[:, hp * S:(hp + 1) * S], qfm[hp])
        pb.dma(ksb[:, hp * S:(hp + 1) * S], kfm[hp])
    pb.dma(ktsb[:].rearrange("p (c f) -> p c f", f=F),
           ktm.rearrange("(c p) f -> p c f", p=128))
    pb.dma(vtsb[:].rearrange("p (c f) -> p c f", f=F),
           vtm.rearrange("(c p) f -> p c f", p=128))
    pb.dma(abr[:], ab_raw)
    pb.act(abr[:], abr[:], AF.Sigmoid)
    pb.ts(abr[:], abr[:], 0.5, ALU.mult)
    onr64 = pb.sb("onr64", [1, 64], F32)
    pb.memset(onr64[:], 1.0)
    abbc = pb.sb("abbc", [64, 16], F32)
    ps_ab = pb.ps("ps_ab", [64, 16], F32)
    pb.mm(ps_ab[:], onr64[:], abr[:])
    pb.copy(abbc[:], ps_ab[:])
    pb.memset(maskA_SL[:], 1.0)
    pb.affine_select(maskA_SL[:], maskA_SL[:], [[0, NH], [-1, C]], ALU.is_gt, 0.0,
                     channel_multiplier=1)
    pb.memset(maskSU[:], 1.0)
    pb.affine_select(maskSU[:], maskSU[:], [[0, NH], [1, C]], ALU.is_gt, 0.0,
                     channel_multiplier=-1)
    pb.copy(maskA_SU[:], maskSU[:])
    a3 = abbc[0:64, 0:8].unsqueeze(2).broadcast_to((64, 8, C))
    b3 = abbc[0:64, 8:16].unsqueeze(2).broadcast_to((64, 8, C))
    pb.tt(maskA_SL[:].rearrange("p (h i) -> p h i", i=C),
          maskA_SL[:].rearrange("p (h i) -> p h i", i=C), a3, ALU.mult)
    pb.tt(maskA_SU[:].rearrange("p (h i) -> p h i", i=C),
          maskA_SU[:].rearrange("p (h i) -> p h i", i=C), a3, ALU.mult)
    pb.memset(Ib[:], 1.0)
    pb.affine_select(Ib[:], Ib[:], [[0, NH], [-1, C]], ALU.is_equal, 0.0,
                     channel_multiplier=1)
    pb.memset(Sf[:], 0.0, eng="vector")
    pb.copy(Sb0[:], Sf[:])
    for h in range(NH):
        hp, par = divmod(h, 2)
        pb.copy(Sbp[64 * par:64 * par + 64, hp * 64:(hp + 1) * 64],
                Sf[:, h * 64:(h + 1) * 64])

    def qslice(h, t0, w=C):
        hp, par = divmod(h, 2)
        return qsb[64 * par:64 * par + 64, hp * S + t0: hp * S + t0 + w]

    def kslice(h, t0, w=C):
        hp, par = divmod(h, 2)
        return ksb[64 * par:64 * par + 64, hp * S + t0: hp * S + t0 + w]

    for c in range(NC_):
        t0 = c * C
        cr, cpar = divmod(c, 2)
        pb.copy(kst[:], ktsb[64 * cpar:64 * cpar + 64, cr * F:(cr + 1) * F])
        pb.copy(vst[:], vtsb[64 * cpar:64 * cpar + 64, cr * F:(cr + 1) * F])
        ktc = lambda h: kst[0:64, h * 64:(h + 1) * 64]
        vtc = lambda h: vst[0:64, h * 64:(h + 1) * 64]
        for h in range(NH):
            pb.mm(ps_a[:, h * 64:(h + 1) * 64], kslice(h, t0), kslice(h, t0))
        for h in range(NH):
            pb.mm(ps_b2[:, h * 64:(h + 1) * 64], kslice(h, t0), qslice(h, t0))
        pb.tt(Mb[:], ps_a[:], maskA_SL[:], ALU.mult)
        pb.tt(MTb[:], ps_a[:], maskA_SU[:], ALU.mult)
        pb.tt(gT[:], ps_b2[:], maskSU[:], ALU.mult)
        prev, prevT = Mb, MTb
        for l in range(4):
            for h in range(NH):
                sl = slice(h * 64, (h + 1) * 64)
                pb.mm(ps_a[:, sl], prevT[:, sl], prev[:, sl])
            pb.copy(POW[l][:], ps_a[:])
            for h in range(NH):
                sl = slice(h * 64, (h + 1) * 64)
                pb.mm(ps_a[:, sl], prev[:, sl], prevT[:, sl])
            pb.copy(POWT[l][:], ps_a[:])
            prev, prevT = POW[l], POWT[l]
        for h in range(NH):
            sl = slice(h * 64, (h + 1) * 64)
            pb.mm(ps_a[:, sl], POWT[3][:, sl], POW[3][:, sl])
        pb.copy(P32[:], ps_a[:])
        pb.tt(Xf[:], Ib[:], MTb[:], ALU.subtract)
        pb.copy(Xb[:], Xf[:])
        for pw in (POW[0], POW[1], POW[2], POW[3], P32):
            for h in range(NH):
                sl = slice(h * 64, (h + 1) * 64)
                pb.mm(ps_a[:, sl], pw[:, sl], Xb[:, sl])
            pb.tt(Xf[:], Xf[:], ps_a[:], ALU.add)
            pb.copy(Xb[:], Xf[:])
        pb.tt(Rall[:].rearrange("p (h z) -> p h z", z=128)[:, :, 0:64],
              vst[:].rearrange("p (h z) -> p h z", z=64), b3, ALU.mult)
        pb.tt(Rall[:].rearrange("p (h z) -> p h z", z=128)[:, :, 64:128],
              kst[:].rearrange("p (h z) -> p h z", z=64), a3, ALU.mult)
        for h in range(NH):
            pb.mm(ps_u[:, h * 128:(h + 1) * 128], Xb[:, h * 64:(h + 1) * 64],
                  Rall[:, h * 128:(h + 1) * 128])
        pb.copy(Ub[:], ps_u[:])
        U0 = lambda h: Ub[:, h * 128: h * 128 + 64]
        W0 = lambda h: Ub[:, h * 128 + 64: (h + 1) * 128]
        for h in range(NH):
            pb.mm(ps_b2[:, h * 64:(h + 1) * 64], W0(h), gT[:, h * 64:(h + 1) * 64])
        for par in range(2):
            src = ps_b2[:].rearrange("p (h z) -> p h z", z=64)
            pb.tt(ghT[64 * par:64 * par + 64, :].rearrange("p (hp z) -> p hp z", z=64),
                  qsb[64 * par:64 * par + 64, :].rearrange("p (hp s) -> p hp s", s=S)[:, :, t0:t0 + C],
                  ps_b2[0:64, :].rearrange("p (h z) -> p h z", z=64)[:, par::2, :],
                  ALU.subtract)
        for h in range(NH):
            pb.mm(ps_s[:, h * 64:(h + 1) * 64], ktc(h), U0(h))
        for h in range(NH):
            pb.mm(ps_b2[:, h * 64:(h + 1) * 64], W0(h), ktc(h))
        pb.tt(Rtb[:], Ib[:], ps_b2[:], ALU.subtract)
        for h in range(NH):
            hp, par = divmod(h, 2)
            pb.mm(ps_y[:, h * 64:(h + 1) * 64], gT[:, h * 64:(h + 1) * 64], U0(h),
                  start=True, stop=False)
            pb.mm(ps_y[:, h * 64:(h + 1) * 64],
                  ghT[64 * par:64 * par + 64, hp * 64:(hp + 1) * 64],
                  Sbp[64 * par:64 * par + 64, hp * 64:(hp + 1) * 64],
                  start=False, stop=True)
        pb.copy(ostage[:], ps_y[:])
        pb.dma(ot[t0:t0 + C, :], ostage[:])
        pb.copy(pstage[:], ps_s[:])
        for h in range(NH):
            pb.mm(ps_y[:, h * 64:(h + 1) * 64], Rtb[:, h * 64:(h + 1) * 64],
                  Sb0[:, h * 64:(h + 1) * 64])
        pb.tt(Sf[:], pstage[:], ps_y[:], ALU.add)
        pb.copy(Sb0[:], Sf[:])
        for h in range(NH):
            hp, par = divmod(h, 2)
            pb.copy(Sbp[64 * par:64 * par + 64, hp * 64:(hp + 1) * 64],
                    Sf[:, h * 64:(h + 1) * 64])
    return pb


def build_attn_a1_out(pb):
    xr = pb.inp("xr", [D, RPC])
    ofm = pb.inp("ofm", [D, RPC])
    wo = pb.inp("wo", [D, D])
    xo = pb.out("xo", [D, RPC])
    wob = load_w_fm(pb, wo, D, D, "wo")
    abuf = pb.sb("abuf", [128, TT], F32)
    ybuf = pb.sb("ybuf", [128, NCH * TT], BF16)
    obuf = pb.sb("obuf", [128, TT], F32)
    ps_p = pb.ps("ps_p", [128, TT], F32)
    for t0 in range(0, RPC, TT):
        for ch in range(NCH):
            pb.dma(abuf[:], ofm[ch * 128:(ch + 1) * 128, t0:t0 + TT])
            pb.copy(ybuf[:, ch * TT:(ch + 1) * TT], abuf[:])
        for och in range(NCH):
            for dch in range(NCH):
                pb.mm(ps_p[:], wob(dch, och * 128, (och + 1) * 128),
                      ybuf[:, dch * TT:(dch + 1) * TT],
                      start=(dch == 0), stop=(dch == NCH - 1))
            pb.dma(abuf[:], xr[och * 128:(och + 1) * 128, t0:t0 + TT])
            pb.tt(obuf[:], ps_p[:], abuf[:], ALU.add)
            pb.dma(xo[och * 128:(och + 1) * 128, t0:t0 + TT], obuf[:])
    return pb


def sub_l1_attn_a(X, inputs):
    ii = {k: np.asarray(v) for k, v in inputs.items()}
    in_maps = []
    for c in range(NCORES):
        b, h = c // 2, c % 2
        xr = np.ascontiguousarray(X[b, h * RPC:(h + 1) * RPC, :].T)
        in_maps.append({"xr": xr, "wq": ii["L1_attn_a_Wq"],
                        "wk": ii["L1_attn_a_Wk"], "wv": ii["L1_attn_a_Wv"]})
    r1 = run_prog("attn_a1_qkv", build_attn_a1_qkv, in_maps)

    alpha = ii["L1_attn_a_alpha_raw"].reshape(16)
    beta = ii["L1_attn_a_beta_raw"].reshape(16)
    in_maps2 = []
    for c in range(NCORES):
        b, hh = c // 2, c % 2
        qr = np.concatenate([r1[2 * b]["qt"], r1[2 * b + 1]["qt"]], axis=0)
        kr = np.concatenate([r1[2 * b]["kt"], r1[2 * b + 1]["kt"]], axis=0)
        vt = np.concatenate([r1[2 * b]["vt"], r1[2 * b + 1]["vt"]], axis=0)
        def fmpack(a):
            af = np.ascontiguousarray(a.T)
            af = af.reshape(16, 64, S)[hh * 8:(hh + 1) * 8]
            return np.ascontiguousarray(af.reshape(4, 128, S))
        cols = slice(hh * 512, (hh + 1) * 512)
        abrow = np.concatenate([alpha[hh * 8:(hh + 1) * 8],
                                beta[hh * 8:(hh + 1) * 8]]).reshape(1, 16).astype(np.float32)
        in_maps2.append({
            "qfm": fmpack(qr), "kfm": fmpack(kr),
            "ktm": np.ascontiguousarray(kr[:, cols]),
            "vtm": np.ascontiguousarray(vt[:, cols]),
            "ab_raw": abrow,
        })
    r2 = run_prog("attn_a1_scan", build_attn_a1_scan, in_maps2)

    in_maps3 = []
    for c in range(NCORES):
        b, h = c // 2, c % 2
        o_full = np.concatenate([r2[2 * b]["ot"], r2[2 * b + 1]["ot"]], axis=1)
        ofm = np.ascontiguousarray(o_full[h * RPC:(h + 1) * RPC, :].T)
        xr = np.ascontiguousarray(X[b, h * RPC:(h + 1) * RPC, :].T)
        in_maps3.append({"xr": xr, "ofm": ofm, "wo": ii["L1_attn_a_Wo"]})
    r3 = run_prog("attn_a1_out", build_attn_a1_out, in_maps3)
    Xn = np.empty_like(X)
    for c in range(NCORES):
        b, h = c // 2, c % 2
        Xn[b, h * RPC:(h + 1) * RPC, :] = r3[c]["xo"].T
    return Xn


CTRLW = 4128

def build_gdn_prep(pb):
    xr = pb.inp("xr", [D, RPC])
    xp3 = pb.inp("xp3", [2, NCH * 3 * 128])
    rolec = pb.inp("rolec", [128, 1])
    cw = pb.inp("cw", [D, CTRLW])
    cbm = pb.inp("cbm", [128, 32])
    cbt = pb.inp("cbt", [32, 1])
    convw = pb.inp("convw", [128, NCH * 12])
    y32 = pb.out("y32", [D, RPC])
    qc = pb.out("qc", [D, RPC], BF16)
    kc = pb.out("kc", [D, RPC], BF16)
    vc = pb.out("vc", [D, RPC], BF16)
    g16 = pb.out("g16", [16, RPC])
    lnb16 = pb.out("lnb16", [16, RPC])
    beta16 = pb.out("beta16", [16, RPC])
    gsig = pb.out("gsig", [D, RPC])

    onc, onr = emit_ones(pb)
    cwb = load_w_fm(pb, cw, D, CTRLW, "cw")
    cbms = pb.sb("cbms", [128, 32], F32)
    pb.dma(cbms[:], cbm)
    cbts = pb.sb("cbts", [32, 1], F32)
    pb.dma(cbts[:], cbt)
    cvs = pb.sb("cvs", [128, NCH * 12], F32)
    pb.dma(cvs[:], convw)
    rc = pb.sb("rc", [128, 1], F32)
    pb.dma(rc[:], rolec)

    xbuf = pb.sb("xbuf", [128, NCH * TT], F32)
    ybuf = pb.sb("ybuf", [128, NCH * RPC], BF16)
    y32b = pb.sb("y32b", [128, NCH * TT], F32)
    xb3 = pb.sb("xb3", [128, NCH * 3], F32)
    yb3 = pb.sb("yb3", [128, NCH * 3], BF16)
    bct = pb.sb("bct", [128, 24 * 3], BF16)
    hist = pb.sb("hist", [128, 24 * 3], BF16)
    cbuf = pb.sb("cbuf", [128, TT], F32)
    csrc = pb.sb("csrc", [128, TT + 3], F32)
    cacc = pb.sb("cacc", [128, TT], F32)
    ctmp = pb.sb("ctmp", [128, TT], F32)
    stage = pb.sb("stageP", [128, TT], BF16)
    tl32 = pb.sb("tl32", [32, TT], F32)
    ps_p = pb.ps("ps_pp", [128, TT], F32)
    ps_b3 = pb.ps("ps_b3", [128, 3], F32)
    lnscr = make_ln_scratch(pb, "gp")

    for t0 in range(0, RPC, TT):
        for ch in range(NCH):
            pb.dma(xbuf[:, ch * TT:(ch + 1) * TT],
                   xr[ch * 128:(ch + 1) * 128, t0:t0 + TT])
        xs = [xbuf[:, ch * TT:(ch + 1) * TT] for ch in range(NCH)]
        mb, rb = emit_ln_tile(pb, lnscr, xs,
                              lambda ch: y32b[:, ch * TT:(ch + 1) * TT],
                              onc, onr, TT)
        for ch in range(NCH):
            pb.copy(ybuf[:, ch * RPC + t0: ch * RPC + t0 + TT],
                    y32b[:, ch * TT:(ch + 1) * TT])
            pb.dma(y32[ch * 128:(ch + 1) * 128, t0:t0 + TT],
                   y32b[:, ch * TT:(ch + 1) * TT])
    pb.dma(xb3[:], xp3[0:1, :].rearrange("one (p c) -> p (one c)", p=128))
    emit_ln_tile(pb, lnscr, [xb3[:, ch * 3:(ch + 1) * 3] for ch in range(NCH)],
                 lambda ch: yb3[:, ch * 3:(ch + 1) * 3], onc, onr, 3)
    for och in range(24):
        for dch in range(NCH):
            pb.mm(ps_b3[:], cwb(dch, och * 128, (och + 1) * 128),
                  yb3[:, dch * 3:(dch + 1) * 3],
                  start=(dch == 0), stop=(dch == NCH - 1))
        pb.ts(csrc[:, 0:3], ps_b3[:], cbms[:, och:och + 1], ALU.add)
        pb.tt(bct[:, och * 3:(och + 1) * 3], csrc[:, 0:3], rc[:, 0:1].broadcast_to((128, 3)), ALU.mult)
    pb.copy(hist[:], bct[:])

    for t0 in range(0, RPC, TT):
        for och in range(24):
            for dch in range(NCH):
                pb.mm(ps_p[:], cwb(dch, och * 128, (och + 1) * 128),
                      ybuf[:, dch * RPC + t0: dch * RPC + t0 + TT],
                      start=(dch == 0), stop=(dch == NCH - 1))
            pb.copy(csrc[:, 0:3], hist[:, och * 3:(och + 1) * 3])
            pb.ts(csrc[:, 3:3 + TT], ps_p[:], cbms[:, och:och + 1], ALU.add)
            pb.copy(hist[:, och * 3:(och + 1) * 3], csrc[:, TT:TT + 3])
            which = och // 8
            fch = och % 8
            wc = lambda i: cvs[:, which * NCH * 4 + fch * 4 + i: which * NCH * 4 + fch * 4 + i + 1]
            pb.ts(cacc[:], csrc[:, 0:TT], wc(0), ALU.mult)
            for i in range(1, 4):
                pb.ts(ctmp[:], csrc[:, i:i + TT], wc(i), ALU.mult)
                pb.tt(cacc[:], cacc[:], ctmp[:], ALU.add)
            pb.act(stage[:], cacc[:], AF.Silu)
            OUT = (qc, kc, vc)[which]
            pb.dma(OUT[fch * 128:(fch + 1) * 128, t0:t0 + TT], stage[:])
        for och in range(24, 32):
            for dch in range(NCH):
                pb.mm(ps_p[:], cwb(dch, och * 128, (och + 1) * 128),
                      ybuf[:, dch * RPC + t0: dch * RPC + t0 + TT],
                      start=(dch == 0), stop=(dch == NCH - 1))
            pb.ts(cbuf[:], ps_p[:], cbms[:, och:och + 1], ALU.add)
            pb.act(cbuf[:], cbuf[:], AF.Sigmoid)
            pb.dma(gsig[(och - 24) * 128:(och - 23) * 128, t0:t0 + TT], cbuf[:])
        for dch in range(NCH):
            pb.mm(ps_p[0:32, 0:TT], cwb(dch, 4096, 4128),
                  ybuf[:, dch * RPC + t0: dch * RPC + t0 + TT],
                  start=(dch == 0), stop=(dch == NCH - 1))
        pb.ts(tl32[:], ps_p[0:32, 0:TT], cbts[:, 0:1], ALU.add)
        pb.act(cbuf[0:32, :], tl32[:], AF.Abs)
        pb.act(cbuf[0:32, :], cbuf[0:32, :], AF.Exp, scale=-1.0)
        pb.act(cbuf[0:32, :], cbuf[0:32, :], AF.Ln, bias=pb.constcol(1.0, 32))
        pb.ts(ctmp[0:32, :], tl32[:], 0.0, ALU.min)
        pb.tt(cbuf[0:32, :], ctmp[0:32, :], cbuf[0:32, :], ALU.subtract)
        pb.dma(g16[0:16, t0:t0 + TT], cbuf[0:16, :])
        pb.dma(lnb16[0:16, t0:t0 + TT], cbuf[16:32, :])
        pb.act(cbuf[0:32, :], tl32[:], AF.Sigmoid)
        pb.dma(beta16[0:16, t0:t0 + TT], cbuf[16:32, :])
    return pb


def build_p20_scan(pb):
    NCHN = 256
    ur = pb.inp("ur", [NCHN, S])
    ui = pb.inp("ui", [NCHN, S])
    dl = pb.inp("dl", [128, 2])
    th = pb.inp("th", [128, 2])
    sr_o = pb.out("sr", [NCHN, S])
    si_o = pb.out("si", [NCHN, S])

    urs = pb.sb("urs", [128, 2 * S], F32)
    uis = pb.sb("uis", [128, 2 * S], F32)
    cs = pb.sb("cs", [128, 2 * S], F32)
    sn = pb.sb("sn", [128, 2 * S], F32)
    lam = pb.sb("lam", [128, 2], F32)
    dls = pb.sb("dls", [128, 2], F32)
    ths = pb.sb("ths", [128, 2], F32)
    t1 = pb.sb("t1", [128, S], F32)
    t2 = pb.sb("t2", [128, S], F32)
    ti = pb.sb("ti", [128, 2], mybir.dt.int32)

    pb.dma(urs[:].rearrange("p (g t) -> p g t", g=2),
           ur.rearrange("(g p) t -> p g t", p=128))
    pb.dma(uis[:].rearrange("p (g t) -> p g t", g=2),
           ui.rearrange("(g p) t -> p g t", p=128))
    pb.dma(dls[:], dl)
    pb.dma(ths[:], th)
    pb.act(lam[:], dls[:], AF.Exp)
    pb.act(lam[:], lam[:], AF.Exp, scale=-1.0)
    pb.ts(t1[:, 0:2], ths[:], 1.0 / (2 * math.pi), ALU.mult)
    pb.copy(ti[:], t1[:, 0:2])
    pb.copy(t2[:, 0:2], ti[:])
    pb.tt(t1[:, 0:2], t1[:, 0:2], t2[:, 0:2], ALU.subtract)
    for g in range(2):
        pb.memset(cs[:, g * S: g * S + 1], 1.0, eng="vector")
        pb.memset(sn[:, g * S: g * S + 1], 0.0, eng="vector")
        pb.act(cs[:, g * S + 1: g * S + 2], t1[:, g:g + 1], AF.Sin,
               scale=2 * math.pi, bias=pb.constcol(math.pi / 2.0))
        pb.act(sn[:, g * S + 1: g * S + 2], t1[:, g:g + 1], AF.Sin, scale=2 * math.pi)
    for g in range(2):
        pb.act(t2[:, 0:1], t1[:, g:g + 1], AF.Sin, scale=math.pi)
        pb.act(t2[:, 1:2], t2[:, 0:1], AF.Square)
        pb.ts(t2[:, 1:2], t2[:, 1:2], -2.0, ALU.mult, 1.0, ALU.add)
        pb.copy(cs[:, g * S + 1: g * S + 2], t2[:, 1:2])
    twc = pb.sb("twc", [128, 1], F32)
    tws = pb.sb("tws", [128, 1], F32)
    tta = pb.sb("tta", [128, 1], F32)
    ttb = pb.sb("ttb", [128, 1], F32)
    n = 1
    while n < S:
        m = min(n, S - n)
        for g in range(2):
            base = g * S
            pb.tt(tta[:], cs[:, base + n - 1:base + n], cs[:, base + 1:base + 2], ALU.mult)
            pb.tt(ttb[:], sn[:, base + n - 1:base + n], sn[:, base + 1:base + 2], ALU.mult)
            pb.tt(twc[:], tta[:], ttb[:], ALU.subtract)
            pb.tt(tta[:], sn[:, base + n - 1:base + n], cs[:, base + 1:base + 2], ALU.mult)
            pb.tt(ttb[:], cs[:, base + n - 1:base + n], sn[:, base + 1:base + 2], ALU.mult)
            pb.tt(tws[:], tta[:], ttb[:], ALU.add)
            pb.ts(t1[:, 0:m], cs[:, base:base + m], twc[:, 0:1], ALU.mult)
            pb.ts(t2[:, 0:m], sn[:, base:base + m], tws[:, 0:1], ALU.mult)
            pb.tt(t1[:, 0:m], t1[:, 0:m], t2[:, 0:m], ALU.subtract)
            pb.ts(t2[:, 0:m], cs[:, base:base + m], tws[:, 0:1], ALU.mult)
            pb.ts(cs[:, base + n:base + n + m], sn[:, base:base + m],
                  twc[:, 0:1], ALU.mult)
            pb.tt(sn[:, base + n:base + n + m], t2[:, 0:m],
                  cs[:, base + n:base + n + m], ALU.add)
            pb.copy(cs[:, base + n:base + n + m], t1[:, 0:m])
        n *= 2
    for g in range(2):
        b0 = g * S
        U, V, Cc, Sn = urs[:, b0:b0 + S], uis[:, b0:b0 + S], cs[:, b0:b0 + S], sn[:, b0:b0 + S]
        pb.tt(t1[:, :], V, Sn, ALU.mult)
        pb.tt(t2[:, :], V, Cc, ALU.mult)
        pb.tt(V, U, Sn, ALU.mult)
        pb.tt(V, t2[:, :], V, ALU.subtract)
        pb.tt(U, U, Cc, ALU.mult)
        pb.tt(U, U, t1[:, :], ALU.add)
    lamt = pb.sb("lamt", [128, S], F32)
    for g in range(2):
        b0 = g * S
        pb.memset(lamt[:], 1.0)
        pb.ts(lamt[:], lamt[:], lam[:, g:g + 1], ALU.mult)
        pb.scan(urs[:, b0:b0 + S], lamt[:], urs[:, b0:b0 + S], 0.0)
        pb.scan(uis[:, b0:b0 + S], lamt[:], uis[:, b0:b0 + S], 0.0)
    for g in range(2):
        b0 = g * S
        Zr, Zi, Cc, Sn = urs[:, b0:b0 + S], uis[:, b0:b0 + S], cs[:, b0:b0 + S], sn[:, b0:b0 + S]
        pb.tt(t1[:, :], Zr, Cc, ALU.mult)
        pb.tt(t2[:, :], Zi, Sn, ALU.mult)
        pb.tt(t1[:, :], t1[:, :], t2[:, :], ALU.subtract)
        pb.dma(sr_o[g * 128:(g + 1) * 128, :], t1[:, :])
        pb.tt(t1[:, :], Zi, Cc, ALU.mult)
        pb.tt(t2[:, :], Zr, Sn, ALU.mult)
        pb.tt(t1[:, :], t1[:, :], t2[:, :], ALU.add)
        pb.dma(si_o[g * 128:(g + 1) * 128, :], t1[:, :])
    return pb


def build_gdn_mix(pb):
    sfm = pb.inp("sfm", [D, RPC])
    qcf = pb.inp("qcf", [D, RPC], BF16)
    kcf = pb.inp("kcf", [D, RPC], BF16)
    vcf = pb.inp("vcf", [D, RPC], BF16)
    p20cW = pb.inp("p20cW", [D, 3 * D])
    vnc = pb.inp("vnc", [128, NCH])
    rlc = pb.inp("rlc", [128, NCH])
    qm = pb.out("qm", [D, RPC], BF16)
    km = pb.out("km", [D, RPC], BF16)
    vm = pb.out("vm", [D, RPC], BF16)
    np20 = pb.out("np20", [D, RPC], BF16)

    onc, onr = emit_ones(pb)
    pwb = load_w_fm(pb, p20cW, D, 3 * D, "p20c")
    vncs = load_vec_cols(pb, vnc, D, "vnc")
    ramp = pb.sb("ramp", [128, NCH], F32)
    pb.dma(ramp[:], rlc)
    pb.act(ramp[:], ramp[:], AF.Sigmoid)

    sbu = pb.sb("sbu", [128, NCH * TT], F32)
    npb = pb.sb("npb", [128, NCH * TT], BF16)
    cin = pb.sb("cin", [128, TT], BF16)
    sq = pb.sb("sqm", [128, TT], F32)
    row = pb.sb("rowm", [1, TT], F32)
    stage = pb.sb("stageM", [128, TT], BF16)
    tmp = pb.sb("tmpM", [128, TT], F32)
    ps_r = pb.ps("ps_rm", [1, TT], F32)
    ps_b = pb.ps("ps_bm", [128, TT], F32)
    ps_p = pb.ps("ps_pm", [128, TT], F32)

    for t0 in range(0, RPC, TT):
        for ch in range(NCH):
            pb.dma(sbu[:, ch * TT:(ch + 1) * TT],
                   sfm[ch * 128:(ch + 1) * 128, t0:t0 + TT])
        for ch in range(NCH):
            pb.act(sq[:], sbu[:, ch * TT:(ch + 1) * TT], AF.Square)
            pb.mm(ps_r[0:1, :], onc[:], sq[:],
                  start=(ch == 0), stop=(ch == NCH - 1))
        pb.ts(row[0:1, :], ps_r[0:1, :], 1.0 / D, ALU.mult)
        pb.act(row[0:1, :], row[0:1, :], AF.Sqrt, bias=pb.constcol(1e-6, 1))
        pb.recip(row[0:1, :], row[0:1, :])
        pb.mm(ps_b[:], onr[:], row[0:1, :])
        for ch in range(NCH):
            pb.tt(tmp[:], sbu[:, ch * TT:(ch + 1) * TT], ps_b[:], ALU.mult)
            pb.ts(npb[:, ch * TT:(ch + 1) * TT], tmp[:], vncs[:, ch:ch + 1],
                  ALU.mult)
            pb.copy(stage[:], npb[:, ch * TT:(ch + 1) * TT])
            pb.dma(np20[ch * 128:(ch + 1) * 128, t0:t0 + TT], stage[:])
        for which, (OUT, CIN) in enumerate(((qm, qcf), (km, kcf), (vm, vcf))):
            for fch in range(NCH):
                och = which * NCH + fch
                for dch in range(NCH):
                    pb.mm(ps_p[:], pwb(dch, och * 128, (och + 1) * 128),
                          npb[:, dch * TT:(dch + 1) * TT],
                          start=(dch == 0), stop=(dch == NCH - 1))
                pb.ts(tmp[:], ps_p[:], ramp[:, fch:fch + 1], ALU.mult)
                pb.dma(cin[:], CIN[fch * 128:(fch + 1) * 128, t0:t0 + TT])
                pb.tt(stage[:], tmp[:], cin[:], ALU.add)
                pb.dma(OUT[fch * 128:(fch + 1) * 128, t0:t0 + TT], stage[:])
    return pb


def build_gdn_scan(pb):
    NH, C = 8, 64
    NC_ = S // C
    F = NH * 64
    qfm = pb.inp("qfm", [4, 128, S], BF16)
    kfm = pb.inp("kfm", [4, 128, S], BF16)
    ktm = pb.inp("ktm", [S, F], BF16)
    vtm = pb.inp("vtm", [S, F], BF16)
    gtb = pb.inp("gtb", [S, 24])
    mnt = pb.inp("mnt", [1, F])
    nr = pb.out("nr", [S, F])

    qsb = pb.sb("qsb", [128, 4 * S], BF16)
    ksb = pb.sb("ksb", [128, 4 * S], BF16)
    ktsb = pb.sb("ktsb", [128, (NC_ // 2) * F], BF16)
    vtsb = pb.sb("vtsb", [128, (NC_ // 2) * F], BF16)
    gts = pb.sb("gts", [128, (S // 128) * 24], F32)
    kst = pb.sb("kst", [64, F], BF16)
    vst = pb.sb("vst", [64, F], BF16)
    gst = pb.sb("gst", [64, 24], F32)
    bb = pb.sb("bb", [64, 16], F32)
    eb = pb.sb("ebx", [64, 8], F32)
    ed = pb.sb("edx", [64, 8], F32)
    ebC = pb.sb("ebC", [64, 8], F32)
    blnb = pb.sb("blnb", [64, 8], F32)
    bpre = pb.sb("bpre", [64, 8], F32)
    fdiag = pb.sb("fdiag", [64, F], F32)
    gmA = pb.sb("gmA", [64, F], F32)
    Mb = pb.sb("Mb2", [64, F], BF16)
    MTb = pb.sb("MTb2", [64, F], BF16)
    M1T = pb.sb("M1T", [64, F], BF16)
    POW = [pb.sb(f"gPOW{l}", [64, F], BF16) for l in range(4)]
    POWT = [pb.sb(f"gPOWT{l}", [64, F], BF16) for l in range(4)]
    P32 = pb.sb("gP32", [64, F], BF16)
    Xf = pb.sb("gXf", [64, F], F32)
    Xb = pb.sb("gXb", [64, F], BF16)
    Rall = pb.sb("gRall", [64, NH * 128], BF16)
    Ub = pb.sb("gUb", [64, NH * 128], BF16)
    ngb = pb.sb("gngb", [64, F], BF16)
    dk = pb.sb("gdk", [64, F], BF16)
    Rtb = pb.sb("gRtb", [64, F], BF16)
    Sf = pb.sb("gSf", [64, F], F32)
    Sb0 = pb.sb("gSb0", [64, F], BF16)
    Sbp = pb.sb("gSbp", [128, 4 * 64], BF16)
    ostage = pb.sb("gost", [64, F], F32)
    pstage = pb.sb("gpst", [64, F], F32)
    tmpF = pb.sb("gtmpF", [64, F], F32)
    sq = pb.sb("gsq", [128, 512], F32)
    rowq = pb.sb("growq", [1, 512], F32)
    red8 = pb.sb("gred8", [128, 8], F32)
    mntb = pb.sb("gmntb", [64, F], F32)
    mSLn = pb.sb("mSLn", [64, F], F32)
    mSUn = pb.sb("mSUn", [64, F], F32)
    mSUi = pb.sb("mSUi", [64, F], F32)
    Imask = pb.sb("Imask2", [64, F], F32)
    ones64f = pb.sb("ones64f", [64, 64], F32)
    triuI = pb.sb("triuI", [64, 64], F32)
    sLowS = pb.sb("sLowS", [64, 64], F32)
    onr1 = pb.sb("onr1", [1, 64], F32)
    mrow = pb.sb("mrow", [1, F], F32)
    ps1 = pb.ps("gps1", [64, F], F32)
    ps2 = pb.ps("gps2", [64, F], F32)
    ps3 = pb.ps("gps3", [64, F], F32)
    ps4 = pb.ps("gps4", [64, F], F32)
    psU = pb.ps("gpsU", [64, NH * 128], F32)
    psY = pb.ps("gpsY", [64, F], F32)
    psY2 = pb.ps("gpsY2", [64, F], F32)

    for hp in range(4):
        pb.dma(qsb[:, hp * S:(hp + 1) * S], qfm[hp])
        pb.dma(ksb[:, hp * S:(hp + 1) * S], kfm[hp])
    pb.dma(ktsb[:].rearrange("p (c f) -> p c f", f=F),
           ktm.rearrange("(c p) f -> p c f", p=128))
    pb.dma(vtsb[:].rearrange("p (c f) -> p c f", f=F),
           vtm.rearrange("(c p) f -> p c f", p=128))
    pb.dma(gts[:].rearrange("p (r f) -> p r f", f=24),
           gtb.rearrange("(r p) f -> p r f", p=128))
    pb.dma(mrow[:], mnt)

    pb.memset(mSLn[:], 0.0)
    pb.affine_select(mSLn[:], mSLn[:], [[0, NH], [-1, C]], ALU.is_gt, -1e30,
                     channel_multiplier=1)
    pb.memset(mSUn[:], 0.0)
    pb.affine_select(mSUn[:], mSUn[:], [[0, NH], [1, C]], ALU.is_gt, -1e30,
                     channel_multiplier=-1)
    pb.memset(mSUi[:], 0.0)
    pb.affine_select(mSUi[:], mSUi[:], [[0, NH], [1, C]], ALU.is_ge, -1e30,
                     channel_multiplier=-1)
    pb.memset(Imask[:], 1.0)
    pb.affine_select(Imask[:], Imask[:], [[0, NH], [-1, C]], ALU.is_equal, 0.0,
                     channel_multiplier=1)
    pb.memset(ones64f[:], 1.0)
    pb.memset(onr1[:], 1.0)
    pb.memset(triuI[:], 1.0)
    pb.affine_select(triuI[:], triuI[:], [[1, 64]], ALU.is_ge, 0.0,
                     channel_multiplier=-1)
    pb.memset(sLowS[:], 1.0)
    pb.affine_select(sLowS[:], sLowS[:], [[-1, 64]], ALU.is_gt, 0.0,
                     channel_multiplier=1)
    pb.mm(ps1[:], onr1[:], mrow[:])
    pb.copy(mntb[:], ps1[:])
    pb.memset(Sf[:], 0.0, eng="vector")
    pb.copy(Sb0[:], Sf[:])
    for h in range(NH):
        hp, par = divmod(h, 2)
        pb.copy(Sbp[64 * par:64 * par + 64, hp * 64:(hp + 1) * 64],
                Sf[:, h * 64:(h + 1) * 64])

    def qslice(h, t0, w=C):
        hp, par = divmod(h, 2)
        return qsb[64 * par:64 * par + 64, hp * S + t0: hp * S + t0 + w]

    def kslice(h, t0, w=C):
        hp, par = divmod(h, 2)
        return ksb[64 * par:64 * par + 64, hp * S + t0: hp * S + t0 + w]

    for (buf, nm) in ((qsb, "q"), (ksb, "k")):
        for h in range(NH):
            hp, par = divmod(h, 2)
            for t0 in range(0, S, 512):
                sl = buf[64 * par:64 * par + 64, hp * S + t0: hp * S + t0 + 512]
                pb.act(sq[0:64, :], sl, AF.Square)
                pb.mm(ps1[0:1, 0:512], pb.constcol(1.0)[64 * par:64 * par + 64, 0:1]
                      if False else ones64f[:, 0:1], sq[0:64, :])
                pb.act(rowq[0:1, :], ps1[0:1, 0:512], AF.Sqrt,
                       bias=pb.constcol(1e-6, 1))
                pb.recip(rowq[0:1, :], rowq[0:1, :])
                pb.mm(ps2[0:64, 0:512], onr1[:], rowq[0:1, :])
                pb.tt(sl, sl, ps2[0:64, 0:512], ALU.mult)
    for r in range(S // 128):
        sl = ktsb[:, r * F:(r + 1) * F] if False else None
    for r in range(NC_ // 2):
        sl = ktsb[:, r * F:(r + 1) * F]
        pb.act(sq[:, :], sl, AF.Square)
        pb.reduce(red8[:, 0:8], sq[:].rearrange("p (h d) -> p h d", d=64), ALU.add)
        pb.act(red8[:, 0:8], red8[:, 0:8], AF.Sqrt, bias=pb.constcol(1e-6))
        pb.recip(red8[:, 0:8], red8[:, 0:8])
        pb.tt(sl.rearrange("p (h d) -> p h d", d=64),
              sl.rearrange("p (h d) -> p h d", d=64),
              red8[:, 0:8].unsqueeze(2).broadcast_to((128, 8, 64)), ALU.mult)

    b8 = bb[:, 0:8]
    brev8 = bb[:, 8:16]
    for c in range(NC_):
        t0 = c * C
        cr, cpar = divmod(c, 2)
        pb.copy(kst[:], ktsb[64 * cpar:64 * cpar + 64, cr * F:(cr + 1) * F])
        pb.copy(vst[:], vtsb[64 * cpar:64 * cpar + 64, cr * F:(cr + 1) * F])
        gr, gpar = divmod(t0, 128)
        pb.copy(gst[:], gts[64 * (gpar // 64):64 * (gpar // 64) + 64,
                            gr * 24:(gr + 1) * 24])
        pb.mm(ps1[0:64, 0:8], triuI[:], gst[:, 0:8])
        pb.mm(ps1[0:64, 8:16], sLowS[:], gst[:, 0:8])
        pb.copy(bb[:], ps1[0:64, 0:16])
        pb.act(eb[:], b8, AF.Exp)
        pb.act(ed[:], brev8, AF.Exp)
        pb.tt(ebC[:], eb[:], ed[:], ALU.mult)
        pb.tt(blnb[:], b8, gst[:, 8:16], ALU.add)
        pb.tt(bpre[:], gst[:, 16:24], eb[:], ALU.mult)
        for h in range(NH):
            pb.mm(ps1[:, h * 64:(h + 1) * 64], kslice(h, t0), kslice(h, t0))
        for h in range(NH):
            pb.mm(ps2[:, h * 64:(h + 1) * 64], kslice(h, t0), qslice(h, t0))
        pb.tt(fdiag[:].rearrange("p (h i) -> p h i", i=C), Imask[:].rearrange("p (h i) -> p h i", i=C),
              b8.unsqueeze(2).broadcast_to((64, 8, C)), ALU.mult)
        pb.mm(ps3[:], ones64f[:], fdiag[:])
        pb.tt(fdiag[:].rearrange("p (h i) -> p h i", i=C), Imask[:].rearrange("p (h i) -> p h i", i=C),
              blnb[:].unsqueeze(2).broadcast_to((64, 8, C)), ALU.mult)
        pb.mm(ps4[:], ones64f[:], fdiag[:])
        pb.tt(gmA[:].rearrange("p (h i) -> p h i", i=C),
              blnb[:].unsqueeze(2).broadcast_to((64, 8, C)),
              ps3[:].rearrange("p (h i) -> p h i", i=C), ALU.subtract)
        pb.tt(gmA[:], gmA[:], mSLn[:], ALU.add)
        pb.act(gmA[:], gmA[:], AF.Exp)
        pb.tt(Mb[:], gmA[:], ps1[:], ALU.mult)
        pb.tt(gmA[:].rearrange("p (h i) -> p h i", i=C),
              ps4[:].rearrange("p (h i) -> p h i", i=C),
              b8.unsqueeze(2).broadcast_to((64, 8, C)), ALU.subtract)
        pb.tt(gmA[:], gmA[:], mSUn[:], ALU.add)
        pb.act(gmA[:], gmA[:], AF.Exp)
        pb.tt(MTb[:], gmA[:], ps1[:], ALU.mult)
        pb.tt(gmA[:].rearrange("p (h i) -> p h i", i=C),
              ps3[:].rearrange("p (h i) -> p h i", i=C),
              b8.unsqueeze(2).broadcast_to((64, 8, C)), ALU.subtract)
        pb.tt(gmA[:], gmA[:], mSUi[:], ALU.add)
        pb.act(gmA[:], gmA[:], AF.Exp)
        pb.tt(M1T[:], gmA[:], ps2[:], ALU.mult)
        prev, prevT = Mb, MTb
        for l in range(4):
            for h in range(NH):
                sl = slice(h * 64, (h + 1) * 64)
                pb.mm(ps1[:, sl], prevT[:, sl], prev[:, sl])
            pb.copy(POW[l][:], ps1[:])
            for h in range(NH):
                sl = slice(h * 64, (h + 1) * 64)
                pb.mm(ps1[:, sl], prev[:, sl], prevT[:, sl])
            pb.copy(POWT[l][:], ps1[:])
            prev, prevT = POW[l], POWT[l]
        for h in range(NH):
            sl = slice(h * 64, (h + 1) * 64)
            pb.mm(ps1[:, sl], POWT[3][:, sl], POW[3][:, sl])
        pb.copy(P32[:], ps1[:])
        pb.tt(Xf[:], Imask[:], MTb[:], ALU.subtract)
        pb.copy(Xb[:], Xf[:])
        for pw in (POW[0], POW[1], POW[2], POW[3], P32):
            for h in range(NH):
                sl = slice(h * 64, (h + 1) * 64)
                pb.mm(ps1[:, sl], pw[:, sl], Xb[:, sl])
            pb.tt(Xf[:], Xf[:], ps1[:], ALU.add)
            pb.copy(Xb[:], Xf[:])
        pb.tt(Rall[:].rearrange("p (h z) -> p h z", z=128)[:, :, 0:64],
              vst[:].rearrange("p (h z) -> p h z", z=64),
              gst[:, 16:24].unsqueeze(2).broadcast_to((64, 8, C)), ALU.mult)
        pb.tt(Rall[:].rearrange("p (h z) -> p h z", z=128)[:, :, 64:128],
              kst[:].rearrange("p (h z) -> p h z", z=64),
              bpre[:].unsqueeze(2).broadcast_to((64, 8, C)), ALU.mult)
        for h in range(NH):
            pb.mm(psU[:, h * 128:(h + 1) * 128], Xb[:, h * 64:(h + 1) * 64],
                  Rall[:, h * 128:(h + 1) * 128])
        pb.copy(Ub[:], psU[:])
        U0 = lambda h: Ub[:, h * 128: h * 128 + 64]
        W0 = lambda h: Ub[:, h * 128 + 64: (h + 1) * 128]
        for h in range(NH):
            pb.mm(ps2[:, h * 64:(h + 1) * 64], W0(h), M1T[:, h * 64:(h + 1) * 64])
        pb.ts(ngb[:], ps2[:], -1.0, ALU.mult)
        pb.tt(dk[:].rearrange("p (h z) -> p h z", z=64),
              kst[:].rearrange("p (h z) -> p h z", z=64),
              ed[:].unsqueeze(2).broadcast_to((64, 8, C)), ALU.mult)
        for h in range(NH):
            pb.mm(ps3[:, h * 64:(h + 1) * 64], dk[:, h * 64:(h + 1) * 64], U0(h))
        pb.copy(pstage[:], ps3[:])
        for h in range(NH):
            pb.mm(ps4[:, h * 64:(h + 1) * 64], W0(h), dk[:, h * 64:(h + 1) * 64])
        pb.tt(tmpF[:].rearrange("p (h i) -> p h i", i=C),
              Imask[:].rearrange("p (h i) -> p h i", i=C),
              ebC[:].unsqueeze(2).broadcast_to((64, 8, C)), ALU.mult)
        pb.tt(Rtb[:], tmpF[:], ps4[:], ALU.subtract)
        for h in range(NH):
            pb.mm(psY[:, h * 64:(h + 1) * 64], M1T[:, h * 64:(h + 1) * 64], U0(h),
                  start=True, stop=False)
            pb.mm(psY[:, h * 64:(h + 1) * 64], ngb[:, h * 64:(h + 1) * 64],
                  Sb0[:, h * 64:(h + 1) * 64], start=False, stop=True)
        for h in range(NH):
            hp, par = divmod(h, 2)
            pb.mm(psY2[:, h * 64:(h + 1) * 64], qslice(h, t0),
                  Sbp[64 * par:64 * par + 64, hp * 64:(hp + 1) * 64])
        for h in range(NH):
            sl = slice(h * 64, (h + 1) * 64)
            pb.ts(tmpF[:, sl], psY2[:, sl], eb[:, h:h + 1], ALU.mult)
            pb.tt(ostage[:, sl], psY[:, sl], tmpF[:, sl], ALU.add)
        pb.ts(ostage[:], ostage[:], 0.125, ALU.mult)
        pb.act(sq[0:64, :], ostage[:], AF.Square)
        pb.reduce(red8[0:64, 0:8], sq[0:64, :].rearrange("p (h d) -> p h d", d=64),
                  ALU.add)
        pb.ts(red8[0:64, 0:8], red8[0:64, 0:8], 1.0 / 64.0, ALU.mult)
        pb.act(red8[0:64, 0:8], red8[0:64, 0:8], AF.Sqrt, bias=pb.constcol(1e-6, 64))
        pb.recip(red8[0:64, 0:8], red8[0:64, 0:8])
        pb.tt(ostage[:].rearrange("p (h d) -> p h d", d=64),
              ostage[:].rearrange("p (h d) -> p h d", d=64),
              red8[0:64, 0:8].unsqueeze(2).broadcast_to((64, 8, 64)), ALU.mult)
        pb.tt(ostage[:], ostage[:], mntb[:], ALU.mult)
        pb.dma(nr[t0:t0 + C, :], ostage[:])
        for h in range(NH):
            pb.mm(psY2[:, h * 64:(h + 1) * 64], Rtb[:, h * 64:(h + 1) * 64],
                  Sb0[:, h * 64:(h + 1) * 64])
        pb.tt(Sf[:], pstage[:], psY2[:], ALU.add)
        pb.copy(Sb0[:], Sf[:])
        for h in range(NH):
            hp, par = divmod(h, 2)
            pb.copy(Sbp[64 * par:64 * par + 64, hp * 64:(hp + 1) * 64],
                    Sf[:, h * 64:(h + 1) * 64])
    return pb


def build_gdn_out(pb):
    nrf = pb.inp("nrf", [D, RPC])
    npf = pb.inp("npf", [D, RPC], BF16)
    gsig = pb.inp("gsig", [D, RPC])
    outw = pb.inp("outw", [2 * D, D])
    outb = pb.inp("outb", [128, NCH])
    xr = pb.inp("xr", [D, RPC])
    xo = pb.out("xo", [D, RPC])

    owb = load_w_fm(pb, outw, 2 * D, D, "outw")
    obc = load_vec_cols(pb, outb, D, "outb")
    nrb = pb.sb("nrb", [128, NCH * TT], BF16)
    npb = pb.sb("npb2", [128, NCH * TT], BF16)
    gb = pb.sb("gb2", [128, TT], F32)
    obuf = pb.sb("obuf2", [128, TT], F32)
    ps_p = pb.ps("ps_go", [128, TT], F32)

    for t0 in range(0, RPC, TT):
        for ch in range(NCH):
            pb.cast_dma(nrb[:, ch * TT:(ch + 1) * TT],
                        nrf[ch * 128:(ch + 1) * 128, t0:t0 + TT])
            pb.dma(npb[:, ch * TT:(ch + 1) * TT],
                   npf[ch * 128:(ch + 1) * 128, t0:t0 + TT])
        for och in range(NCH):
            for dch in range(NCH):
                pb.mm(ps_p[:], owb(dch, och * 128, (och + 1) * 128),
                      nrb[:, dch * TT:(dch + 1) * TT],
                      start=(dch == 0), stop=False)
            for dch in range(NCH):
                pb.mm(ps_p[:], owb(NCH + dch, och * 128, (och + 1) * 128),
                      npb[:, dch * TT:(dch + 1) * TT],
                      start=False, stop=(dch == NCH - 1))
            pb.ts(obuf[:], ps_p[:], obc[:, och:och + 1], ALU.add)
            pb.dma(gb[:], gsig[och * 128:(och + 1) * 128, t0:t0 + TT])
            pb.tt(obuf[:], obuf[:], gb[:], ALU.mult)
            pb.dma(gb[:], xr[och * 128:(och + 1) * 128, t0:t0 + TT])
            pb.tt(obuf[:], obuf[:], gb[:], ALU.add)
            pb.dma(xo[och * 128:(och + 1) * 128, t0:t0 + TT], obuf[:])
    return pb


def sub_l0_attn_b(X, inputs):
    ii = {k: np.asarray(v) for k, v in inputs.items()}
    cw = ii["L0_attn_b_ctrl_W"]
    cb = ii["L0_attn_b_ctrl_b"]
    perm = np.concatenate([np.arange(0, 3 * D), np.arange(3 * D + 32, 3 * D + 32 + D),
                           np.arange(3 * D, 3 * D + 32)])
    cw_r = np.ascontiguousarray(cw[:, perm])
    cb_r = np.ascontiguousarray(cb[perm])
    cbm = vec_cols_host(cb_r[:4096])
    cbt = np.ascontiguousarray(cb_r[4096:].reshape(32, 1))
    convw = np.zeros((128, NCH * 12), np.float32)
    for wi, wname in enumerate(["qconv_w", "kconv_w", "vconv_w"]):
        cwv = ii[f"L0_attn_b_{wname}"]
        for fch in range(NCH):
            for tap in range(4):
                convw[:, wi * NCH * 4 + fch * 4 + tap] = cwv[fch * 128:(fch + 1) * 128, tap]
    in_maps = []
    for c in range(NCORES):
        b, h = c // 2, c % 2
        xr = np.ascontiguousarray(X[b, h * RPC:(h + 1) * RPC, :].T)
        xp3 = np.zeros((2, NCH * 3 * 128), np.float32)
        xp3[0] = np.ascontiguousarray(
            X[b, RPC - 3:RPC, :].T.reshape(NCH, 128, 3).transpose(1, 0, 2)).flatten()
        role = np.full((128, 1), float(h), np.float32)
        in_maps.append({"xr": xr, "xp3": xp3, "rolec": role, "cw": cw_r,
                        "cbm": cbm, "cbt": cbt, "convw": convw})
    rA = run_prog("gdn_prep", build_gdn_prep, in_maps)

    dl = ii["L0_attn_b_p20_decay_log"]
    th = ii["L0_attn_b_p20_theta"]
    in_maps2 = []
    for c in range(NCORES):
        b, j = c // 2, c % 2
        yfull = np.concatenate([rA[2 * b]["y32"], rA[2 * b + 1]["y32"]], axis=1)
        blk = yfull[512 * j: 512 * (j + 1), :]
        ur = np.ascontiguousarray(blk[0::2, :])
        ui = np.ascontiguousarray(blk[1::2, :])
        dls = dl[256 * j: 256 * (j + 1)].reshape(2, 128).T
        ths = th[256 * j: 256 * (j + 1)].reshape(2, 128).T
        in_maps2.append({"ur": ur, "ui": ui,
                         "dl": np.ascontiguousarray(dls),
                         "th": np.ascontiguousarray(ths)})
    rB = run_prog("p20_scan", build_p20_scan, in_maps2)

    in_maps3 = []
    for c in range(NCORES):
        b, h = c // 2, c % 2
        sfull = np.zeros((D, S), np.float32)
        for j in range(2):
            cc = 2 * b + j
            sfull[512 * j: 512 * (j + 1):, :][0::2] = rB[cc]["sr"]
            sfull[512 * j: 512 * (j + 1):, :][1::2] = rB[cc]["si"]
        sfm = np.ascontiguousarray(sfull[:, h * RPC:(h + 1) * RPC])
        in_maps3.append({
            "sfm": sfm, "qcf": rA[c]["qc"], "kcf": rA[c]["kc"], "vcf": rA[c]["vc"],
            "p20cW": ii["L0_attn_b_p20c_W"],
            "vnc": vec_cols_host(ii["L0_attn_b_vnorm_scale"]),
            "rlc": vec_cols_host(ii["L0_attn_b_ramp_logit"]),
        })
    rC = run_prog("gdn_mix", build_gdn_mix, in_maps3)

    in_maps4 = []
    for c in range(NCORES):
        b, hh = c // 2, c % 2
        qm = np.concatenate([rC[2 * b]["qm"], rC[2 * b + 1]["qm"]], axis=1)
        km = np.concatenate([rC[2 * b]["km"], rC[2 * b + 1]["km"]], axis=1)
        vm = np.concatenate([rC[2 * b]["vm"], rC[2 * b + 1]["vm"]], axis=1)
        def fmpack(afm):
            af = afm.reshape(16, 64, S)[hh * 8:(hh + 1) * 8]
            return np.ascontiguousarray(af.reshape(4, 128, S))
        rows = slice(hh * 512, (hh + 1) * 512)
        g16 = np.concatenate([rA[2 * b]["g16"], rA[2 * b + 1]["g16"]], axis=1)
        lnb16 = np.concatenate([rA[2 * b]["lnb16"], rA[2 * b + 1]["lnb16"]], axis=1)
        be16 = np.concatenate([rA[2 * b]["beta16"], rA[2 * b + 1]["beta16"]], axis=1)
        hs = slice(hh * 8, (hh + 1) * 8)
        gtb = np.concatenate([g16[hs].T, lnb16[hs].T, be16[hs].T], axis=1)
        in_maps4.append({
            "qfm": fmpack(qm), "kfm": fmpack(km),
            "ktm": np.ascontiguousarray(km[rows].T),
            "vtm": np.ascontiguousarray(vm[rows].T),
            "gtb": np.ascontiguousarray(gtb.astype(np.float32)),
            "mnt": np.ascontiguousarray(
                np.tile(ii["L0_attn_b_mnorm_scale"], 8).reshape(1, 512).astype(np.float32)),
        })
    rD = run_prog("gdn_scan", build_gdn_scan, in_maps4)

    in_maps5 = []
    for c in range(NCORES):
        b, h = c // 2, c % 2
        nrfull = np.concatenate([rD[2 * b]["nr"], rD[2 * b + 1]["nr"]], axis=1)
        nrf = np.ascontiguousarray(nrfull[h * RPC:(h + 1) * RPC, :].T)
        xr = np.ascontiguousarray(X[b, h * RPC:(h + 1) * RPC, :].T)
        in_maps5.append({
            "nrf": nrf.astype(np.float32), "npf": rC[c]["np20"],
            "gsig": rA[c]["gsig"], "outw": ii["L0_attn_b_out_W"],
            "outb": vec_cols_host(ii["L0_attn_b_out_b"]), "xr": xr,
        })
    r5 = run_prog("gdn_out", build_gdn_out, in_maps5)
    Xn = np.empty_like(X)
    for c in range(NCORES):
        b, h = c // 2, c % 2
        Xn[b, h * RPC:(h + 1) * RPC, :] = r5[c]["xo"].T
    return Xn


PAIRS = [[0, 1], [2, 3], [4, 5], [6, 7]]


def emit_ag(pb, dst, src):
    pb.collective("AllGather", ALU.bypass, PAIRS, [src.opt()], [dst.opt()])


def emit_transpose_mm(pb, pref, dst, src, rows, cols, dt):
    BW = min(rows, 2048)
    with pb.stage(pref, {}):
        ident = pb.sb("ident", [128, 128], dt)
        pb.memset(ident[:], 1.0)
        pb.affine_select(ident[:], ident[:], [[-1, 128]], ALU.is_equal, 0.0,
                         channel_multiplier=1)
        srcb = pb.sb("srcb", [128, 128], dt)
        dstb = pb.sb("dstb", [128, BW], dt)
        ps = pb.ps("ps", [128, 128], F32)
        for c0 in range(0, cols, 128):
            for rr0 in range(0, rows, BW):
                for r0 in range(rr0, rr0 + BW, 128):
                    pb.dma(srcb[:], src[r0:r0 + 128, c0:c0 + 128])
                    pb.mm(ps[:], srcb[:], ident[:])
                    pb.copy(dstb[:, r0 - rr0:r0 - rr0 + 128], ps[:])
                pb.dma(dst[c0:c0 + 128, rr0:rr0 + BW], dstb[:])


def run_stage(pb, pref, build_fn, io, stream=True):
    with pb.stage(pref, io, stream=stream):
        build_fn(pb)


def build_attn_a_qkv_f(pb):
    build_attn_a_qkv(pb)
    rkc = pb.out("rkcol", [128, RPC // 128])
    rk = pb.inp("rk", [1, RPC])
    one1 = pb.sb("one1", [1, 1], F32)
    pb.memset(one1[:], 1.0)
    rrow = pb.sb("rrow", [1, RPC], F32)
    pb.dma(rrow[:], rk)
    ps = pb.ps("ps_rk", [128, RPC // 128], F32)
    stg = pb.sb("stg_rk", [128, RPC // 128], F32)
    for kt in range(RPC // 128):
        pb.mm(ps[:, kt:kt + 1], rrow[0:1, kt * 128:(kt + 1) * 128], one1[:])
    pb.copy(stg[:], ps[:])
    pb.dma(rkc, stg[:])
    return pb


def build_attn_a_av_f(pb):
    NKT = S // 128
    xr = pb.inp("xr", [D, RPC])
    Qb = pb.inp("Qb", [D, RPC], BF16)
    Kg = pb.inp("Kg", [2, D * RPC], BF16)
    Vtg = pb.inp("Vtg", [2 * RPC, D], BF16)
    rqv = pb.inp("rq", [1, RPC])
    rkg = pb.inp("rkg", [2, 128 * (RPC // 128)])
    woT = pb.inp("woT", [D, D])
    bo = pb.inp("bo", [128, NCH])
    xo = pb.out("xo", [D, RPC])

    onc, onr = emit_ones(pb)
    oncb = pb.sb("oncb", [128, 1], BF16)
    pb.memset(oncb[:], 1.0)
    wo = load_w_fm(pb, woT, D, D, "wo")
    boc = load_vec_cols(pb, bo, D, "bo")

    qbuf = pb.sb("qbuf", [128, NCH * RPC], BF16)
    kfull = pb.sb("kfull", [128, NCH * S], BF16)
    vtile = pb.sb("vtile", [128, 512], BF16)
    rkcol = pb.sb("rkcol", [128, NKT], F32)
    rqrow = pb.sb("rqrow", [1, RPC], F32)
    rqb = pb.sb("rqb", [128, TT], F32)
    ebuf = pb.sb("ebuf", [128, NKT * TT], BF16)
    zbuf = pb.sb("zbuf", [128, TT], F32)
    sbuf2 = pb.sb("sbuf2", [128, TT], F32)
    att = pb.sb("att", [128, NCH * TT], BF16)
    dben = pb.sb("dben", [128, TT], F32)
    drow = pb.sb("drow", [1, TT], F32)
    obuf = pb.sb("obuf", [128, TT], F32)
    xbuf = pb.sb("xbuf", [128, TT], F32)
    ps_s = pb.ps("ps_s", [128, TT], F32)
    ps_d = pb.ps("ps_d", [1, TT], F32)
    ps_b = pb.ps("ps_b", [128, TT], F32)
    ps_av = pb.ps("ps_av", [128, 4 * TT], F32)
    ps_o = pb.ps("ps_o", [128, TT], F32)

    pb.dma(qbuf[:].rearrange("p (dc t) -> p dc t", dc=NCH),
           Qb.rearrange("(dc p) t -> p dc t", p=128))
    for h in range(2):
        pb.dma(kfull[:].rearrange("p (dc hf t) -> p dc hf t", dc=NCH, hf=2)[:, :, h, :],
               Kg[h:h + 1, :].rearrange("one (dc p t) -> p (one dc) t", p=128, t=RPC))
        pb.dma(rkcol[:, h * (RPC // 128):(h + 1) * (RPC // 128)],
               rkg[h:h + 1, :].rearrange("one (p c) -> p (one c)", p=128))
    pb.dma(rqrow[:], rqv)

    mpi = -math.pi / 2.0
    for t0 in range(0, RPC, TT):
        pb.mm(ps_b[:], onr[:], rqrow[0:1, t0:t0 + TT])
        pb.copy(rqb[:], ps_b[:])
        for kt in range(NKT):
            for dch in range(NCH):
                pb.mm(ps_s[:], kfull[:, dch * S + kt * 128: dch * S + (kt + 1) * 128],
                      qbuf[:, dch * RPC + t0: dch * RPC + t0 + TT],
                      start=(dch == 0), stop=(dch == NCH - 1))
            pb.ts(zbuf[:], ps_s[:], rkcol[:, kt:kt + 1], ALU.mult)
            pb.tt(zbuf[:], zbuf[:], rqb[:], ALU.mult)
            pb.act(zbuf[:], zbuf[:], AF.Abs)
            pb.act(sbuf2[:], zbuf[:], AF.Sin, bias=pb.constcol(mpi), scale=math.pi)
            pb.tt(zbuf[:], sbuf2[:], ps_s[:], ALU.mult)
            pb.act(ebuf[:, kt * TT:(kt + 1) * TT], zbuf[:], AF.Exp, scale=-1.0 / 32.0)
            pb.mm(ps_d[0:1, :], oncb[:], ebuf[:, kt * TT:(kt + 1) * TT],
                  start=(kt == 0), stop=(kt == NKT - 1))
        pb.recip(drow[0:1, :], ps_d[0:1, :])
        pb.mm(ps_b[:], onr[:], drow[0:1, :])
        pb.copy(dben[:], ps_b[:])
        for rnd in range(2):
            for kt in range(NKT):
                pb.dma(vtile[:], Vtg[kt * 128:(kt + 1) * 128,
                                     rnd * 512:(rnd + 1) * 512])
                for dv in range(4):
                    pb.mm(ps_av[:, dv * TT:(dv + 1) * TT],
                          vtile[:, dv * 128:(dv + 1) * 128],
                          ebuf[:, kt * TT:(kt + 1) * TT],
                          start=(kt == 0), stop=(kt == NKT - 1))
            for dv in range(4):
                pb.tt(att[:, (rnd * 4 + dv) * TT:(rnd * 4 + dv + 1) * TT],
                      ps_av[:, dv * TT:(dv + 1) * TT], dben[:], ALU.mult)
        for och in range(NCH):
            for dv in range(NCH):
                pb.mm(ps_o[:], wo(dv, och * 128, (och + 1) * 128),
                      att[:, dv * TT:(dv + 1) * TT],
                      start=(dv == 0), stop=(dv == NCH - 1))
            pb.ts(obuf[:], ps_o[:], boc[:, och:och + 1], ALU.add)
            pb.dma(xbuf[:], xr[och * 128:(och + 1) * 128, t0:t0 + TT])
            pb.tt(obuf[:], obuf[:], xbuf[:], ALU.add)
            pb.dma(xo[och * 128:(och + 1) * 128, t0:t0 + TT], obuf[:])
    return pb


def build_p20_local(pb):
    NG = 4
    T = RPC
    y32 = pb.inp("y32", [D, RPC])
    dl = pb.inp("dl", [128, NG])
    th = pb.inp("th", [128, NG])
    rolec = pb.inp("rolec", [128, 1])
    zml = pb.inp("zml", [1, 128 * 2 * NG])
    zmg = pb.inp("zmg", [2, 128 * 2 * NG])
    sfm = pb.out("sfm", [D, RPC])

    urs = pb.sb("urs", [128, NG * T], F32)
    uis = pb.sb("uis", [128, NG * T], F32)
    cs = pb.sb("cs", [128, NG * T], F32)
    sn = pb.sb("sn", [128, NG * T], F32)
    t1 = pb.sb("t1", [128, T], F32)
    t2 = pb.sb("t2", [128, T], F32)
    lam = pb.sb("lam", [128, NG], F32)
    lamt = pb.sb("lamt", [128, T], F32)
    ti = pb.sb("ti", [128, NG], mybir.dt.int32)
    twc = pb.sb("twc", [128, 1], F32)
    tws = pb.sb("tws", [128, 1], F32)
    tta = pb.sb("tta", [128, 1], F32)
    ttb = pb.sb("ttb", [128, 1], F32)
    rc = pb.sb("rc", [128, 1], F32)
    zst = pb.sb("zst", [128, 2 * NG], F32)
    ych = pb.sb("ych", [128, 2 * TT], F32)
    Ee = pb.sb("Ee", [128, 64], F32)
    Eo = pb.sb("Eo", [128, 64], F32)
    Fr = pb.sb("Fr", [128, 128], F32)
    Fi = pb.sb("Fi", [128, 128], F32)
    ps_a = pb.ps("ps_a", [128, TT], F32)

    pb.dma(rc[:], rolec)
    pb.dma(lam[:], dl)
    pb.act(lam[:], lam[:], AF.Exp)
    pb.act(lam[:], lam[:], AF.Exp, scale=-1.0)
    pb.dma(t1[:, 0:NG], th)
    pb.memset(Ee[:], 1.0)
    pb.affine_select(Ee[:], Ee[:], [[-2, 64]], ALU.is_equal, 0.0, channel_multiplier=1)
    pb.memset(Eo[:], 1.0)
    pb.affine_select(Eo[:], Eo[:], [[-2, 64]], ALU.is_equal, 0.0, channel_multiplier=1,
                     base=-1)
    for b0 in (0, 64):
        pb.memset(Fr[b0:b0 + 64, :], 1.0)
        pb.affine_select(Fr[b0:b0 + 64, :], Fr[b0:b0 + 64, :], [[1, 128]],
                         ALU.is_equal, 0.0, channel_multiplier=-2)
        pb.memset(Fi[b0:b0 + 64, :], 1.0)
        pb.affine_select(Fi[b0:b0 + 64, :], Fi[b0:b0 + 64, :], [[1, 128]],
                         ALU.is_equal, 0.0, channel_multiplier=-2, base=-1)
    pb.ts(t1[:, 0:NG], t1[:, 0:NG], 1.0 / (2 * math.pi), ALU.mult, -0.5, ALU.add)
    for g in range(NG):
        base = g * T
        pb.memset(cs[:, base:base + 1], 1.0, eng="vector")
        pb.memset(sn[:, base:base + 1], 0.0, eng="vector")
        pb.act(sn[:, base + 1:base + 2], t1[:, g:g + 1], AF.Sin, scale=2 * math.pi)
        pb.ts(sn[:, base + 1:base + 2], sn[:, base + 1:base + 2], -1.0, ALU.mult)
        pb.act(t2[:, 0:1], t1[:, g:g + 1], AF.Sin, scale=math.pi)
        pb.act(t2[:, 1:2], t2[:, 0:1], AF.Square)
        pb.ts(t2[:, 1:2], t2[:, 1:2], 2.0, ALU.mult, -1.0, ALU.add)
        pb.copy(cs[:, base + 1:base + 2], t2[:, 1:2])
    n = 1
    while n < T:
        m = min(n, T - n)
        for g in range(NG):
            base = g * T
            pb.tt(tta[:], cs[:, base + n - 1:base + n], cs[:, base + 1:base + 2], ALU.mult)
            pb.tt(ttb[:], sn[:, base + n - 1:base + n], sn[:, base + 1:base + 2], ALU.mult)
            pb.tt(twc[:], tta[:], ttb[:], ALU.subtract)
            pb.tt(tta[:], sn[:, base + n - 1:base + n], cs[:, base + 1:base + 2], ALU.mult)
            pb.tt(ttb[:], cs[:, base + n - 1:base + n], sn[:, base + 1:base + 2], ALU.mult)
            pb.tt(tws[:], tta[:], ttb[:], ALU.add)
            pb.ts(t1[:, 0:m], cs[:, base:base + m], twc[:, 0:1], ALU.mult)
            pb.ts(t2[:, 0:m], sn[:, base:base + m], tws[:, 0:1], ALU.mult)
            pb.tt(t1[:, 0:m], t1[:, 0:m], t2[:, 0:m], ALU.subtract)
            pb.ts(t2[:, 0:m], cs[:, base:base + m], tws[:, 0:1], ALU.mult)
            pb.ts(cs[:, base + n:base + n + m], sn[:, base:base + m], twc[:, 0:1], ALU.mult)
            pb.tt(sn[:, base + n:base + n + m], t2[:, 0:m], cs[:, base + n:base + n + m],
                  ALU.add)
            pb.copy(cs[:, base + n:base + n + m], t1[:, 0:m])
        n *= 2
    for g in range(NG):
        base = g * T
        pb.tt(tta[:], cs[:, base + T - 1:base + T], cs[:, base + 1:base + 2], ALU.mult)
        pb.tt(ttb[:], sn[:, base + T - 1:base + T], sn[:, base + 1:base + 2], ALU.mult)
        pb.tt(twc[:], tta[:], ttb[:], ALU.subtract)
        pb.tt(tta[:], sn[:, base + T - 1:base + T], cs[:, base + 1:base + 2], ALU.mult)
        pb.tt(ttb[:], cs[:, base + T - 1:base + T], sn[:, base + 1:base + 2], ALU.mult)
        pb.tt(tws[:], tta[:], ttb[:], ALU.add)
        pb.ts(twc[:], twc[:], -1.0, ALU.mult, 1.0, ALU.add)
        pb.tt(twc[:], rc[:, 0:1], twc[:], ALU.mult)
        pb.ts(twc[:], twc[:], -1.0, ALU.mult, 1.0, ALU.add)
        pb.tt(tws[:], rc[:, 0:1], tws[:], ALU.mult)
        pb.ts(t1[:, :], cs[:, base:base + T], twc[:, 0:1], ALU.mult)
        pb.ts(t2[:, :], sn[:, base:base + T], tws[:, 0:1], ALU.mult)
        pb.tt(t1[:, :], t1[:, :], t2[:, :], ALU.subtract)
        pb.ts(t2[:, :], cs[:, base:base + T], tws[:, 0:1], ALU.mult)
        pb.ts(cs[:, base:base + T], sn[:, base:base + T], twc[:, 0:1], ALU.mult)
        pb.tt(sn[:, base:base + T], t2[:, :], cs[:, base:base + T], ALU.add)
        pb.copy(cs[:, base:base + T], t1[:, :])
    for g in range(NG):
        for c0 in range(0, T, TT):
            pb.dma(ych[:, 0:TT], y32[(2 * g) * 128:(2 * g + 1) * 128, c0:c0 + TT])
            pb.dma(ych[:, TT:2 * TT], y32[(2 * g + 1) * 128:(2 * g + 2) * 128, c0:c0 + TT])
            for (E_, dst) in ((Ee, urs), (Eo, uis)):
                pb.mm(ps_a[0:64, :], E_[:], ych[:, 0:TT])
                pb.copy(dst[0:64, g * T + c0: g * T + c0 + TT], ps_a[0:64, :])
                pb.mm(ps_a[0:64, :], E_[:], ych[:, TT:2 * TT])
                pb.copy(dst[64:128, g * T + c0: g * T + c0 + TT], ps_a[0:64, :])
    for g in range(NG):
        b0 = g * T
        U, V, Cc, Sn = urs[:, b0:b0 + T], uis[:, b0:b0 + T], cs[:, b0:b0 + T], sn[:, b0:b0 + T]
        pb.tt(t1[:, :], V, Sn, ALU.mult)
        pb.tt(t2[:, :], V, Cc, ALU.mult)
        pb.tt(V, U, Sn, ALU.mult)
        pb.tt(V, t2[:, :], V, ALU.subtract)
        pb.tt(U, U, Cc, ALU.mult)
        pb.tt(U, U, t1[:, :], ALU.add)
    for g in range(NG):
        b0 = g * T
        pb.memset(lamt[:], 1.0)
        pb.ts(lamt[:], lamt[:], lam[:, g:g + 1], ALU.mult)
        pb.scan(t1[:, :], lamt[:], urs[:, b0:b0 + T], 0.0)
        pb.copy(zst[:, 2 * g:2 * g + 1], t1[:, T - 1:T])
        pb.scan(t1[:, :], lamt[:], uis[:, b0:b0 + T], 0.0)
        pb.copy(zst[:, 2 * g + 1:2 * g + 2], t1[:, T - 1:T])
    pb.dma(zml.rearrange("one (p c) -> (one p) c", p=128), zst[:])
    emit_ag(pb, zmg, zml)
    pb.dma(zst[:], zmg[0:1, :].rearrange("one (p c) -> p (one c)", p=128))
    pb.ts(zst[:], zst[:], rc[:, 0:1], ALU.mult)
    for g in range(NG):
        b0 = g * T
        pb.memset(lamt[:], 1.0)
        pb.ts(lamt[:], lamt[:], lam[:, g:g + 1], ALU.mult)
        pb.scan(urs[:, b0:b0 + T], lamt[:], urs[:, b0:b0 + T], zst[:, 2 * g:2 * g + 1])
        pb.scan(uis[:, b0:b0 + T], lamt[:], uis[:, b0:b0 + T], zst[:, 2 * g + 1:2 * g + 2])
    for g in range(NG):
        b0 = g * T
        Zr, Zi, Cc, Sn = urs[:, b0:b0 + T], uis[:, b0:b0 + T], cs[:, b0:b0 + T], sn[:, b0:b0 + T]
        pb.tt(t1[:, :], Zr, Cc, ALU.mult)
        pb.tt(t2[:, :], Zi, Sn, ALU.mult)
        pb.tt(t1[:, :], t1[:, :], t2[:, :], ALU.subtract)
        pb.tt(t2[:, :], Zi, Cc, ALU.mult)
        pb.tt(Zi, Zr, Sn, ALU.mult)
        pb.tt(Zi, t2[:, :], Zi, ALU.add)
        pb.copy(Zr, t1[:, :])
    for och in range(NCH):
        g = och // 2
        b0 = 64 * (och % 2)
        for c0 in range(0, T, TT):
            pb.mm(ps_a[:], Fr[b0:b0 + 64, :],
                  urs[b0:b0 + 64, g * T + c0: g * T + c0 + TT],
                  start=True, stop=False)
            pb.mm(ps_a[:], Fi[b0:b0 + 64, :],
                  uis[b0:b0 + 64, g * T + c0: g * T + c0 + TT],
                  start=False, stop=True)
            pb.copy(ych[:, 0:TT], ps_a[:])
            pb.dma(sfm[och * 128:(och + 1) * 128, c0:c0 + TT], ych[:, 0:TT])
    return pb


SCAN_LIMIT = 10 ** 9
SCAN_PASS2 = True
SCAN_PRE = True


def build_delta_scan_local(pb, gated):
    NH, C = 16, 64
    T = RPC
    NC2 = T // C
    F = NH * 64
    NHP = NH // 2
    qfm = pb.inp("qfm", [D, T], BF16)
    kfm = pb.inp("kfm", [D, T], BF16)
    ktm = pb.inp("ktm", [T, F], BF16)
    vtm = pb.inp("vtm", [T, F], BF16)
    rolec = pb.inp("rolec", [128, 1])
    sml = pb.inp("sml", [1, 64 * F])
    smg = pb.inp("smg", [2, 64 * F])
    spY1 = pb.inp("spY1", [NC2 * 64, F])
    spP = pb.inp("spP", [NC2 * 64, F])
    spC = pb.inp("spC", [NC2 * 64, F] if gated else [NC2 * 128, NHP * 64], BF16)
    spR = pb.inp("spR", [NC2 * 64, F], BF16)
    ot = pb.out("ot", [T, F])
    if gated:
        gtb = pb.inp("gtb", [T, 48])
        mnt = pb.inp("mnt", [1, F])
        speb = pb.inp("speb", [NC2 * 64, 16])
    else:
        abr_in = pb.inp("ab_raw", [1, 32])

    qsb = pb.sb("qsb", [128, NHP * T], BF16)
    ksb = pb.sb("ksb", [128, NHP * T], BF16)
    kst = pb.sb("kst", [64, F], BF16)
    vst = pb.sb("vst", [64, F], BF16)
    Mb = pb.sb("Mb", [64, F], BF16)
    MTb = pb.sb("MTb", [64, F], BF16)
    M1T = pb.sb("M1T", [64, F], BF16)
    POW = [pb.sb(f"POW{l}", [64, F], BF16) for l in range(4)]
    POWT = [pb.sb(f"POWT{l}", [64, F], BF16) for l in range(4)]
    P32 = pb.sb("P32", [64, F], BF16)
    Xf = pb.sb("Xf", [64, F], F32)
    Xb = pb.sb("Xb", [64, F], BF16)
    Rall = pb.sb("Rall", [64, NH * 128], BF16)
    Ub = pb.sb("Ub", [64, NH * 128], BF16)
    tmpF = pb.sb("tmpF", [64, F], F32)
    t2F = pb.sb("t2F", [64, F], F32)
    Sf = pb.sb("Sf", [64, F], F32)
    Sb0 = pb.sb("Sb0", [64, F], BF16)
    Sbp = pb.sb("Sbp", [128, NHP * 64], BF16)
    sq = pb.sb("sq", [128, 512], F32)
    rowq = pb.sb("rowq", [1, 512], F32)
    red = pb.sb("red", [128, 16], F32)
    Imask = pb.sb("Imask", [64, F], F32)
    ones64f = pb.sb("ones64f", [64, 64], F32)
    onr1 = pb.sb("onr1", [1, 64], F32)
    rc = pb.sb("rc", [128, 1], F32)
    psA = pb.ps("psA", [64, F], F32)
    psB = pb.ps("psB", [64, F], F32)
    psU = pb.ps("psU", [64, NH * 128], F32)
    if gated:
        gts = pb.sb("gts", [128, (T // 128) * 48], F32)
        gst = pb.sb("gst", [64, 48], F32)
        bb = pb.sb("bb", [64, 16], F32)
        eb = pb.sb("ebx", [64, 16], F32)
        ed = pb.sb("edx", [64, 16], F32)
        ebC = pb.sb("ebC", [64, 16], F32)
        blnb = pb.sb("blnb", [64, 16], F32)
        bpre = pb.sb("bpre", [64, 16], F32)
        fdiag = pb.sb("fdiag", [64, F], F32)
        gmA = pb.sb("gmA", [64, F], F32)
        dk = pb.sb("dk", [64, F], BF16)
        mntb = pb.sb("mntb", [64, F], F32)
        mrow = pb.sb("mrow", [1, F], F32)
        triuI = pb.sb("triuI", [64, 64], F32)
        sLowS = pb.sb("sLowS", [64, 64], F32)
        mSLn = pb.sb("mSLn", [64, F], F32)
        mSUn = pb.sb("mSUn", [64, F], F32)
        mSUi = pb.sb("mSUi", [64, F], F32)
    else:
        abr = pb.sb("abr", [1, 32], F32)
        abbc = pb.sb("abbc", [64, 32], F32)
        maskA_SL = pb.sb("maskA_SL", [64, F], F32)
        maskA_SU = pb.sb("maskA_SU", [64, F], F32)
        maskSU = pb.sb("maskSU", [64, F], BF16)
        ghTb = pb.sb("ghTb", [128, NHP * 64], BF16)

    pb.dma(rc[:], rolec)
    for hp in range(NHP):
        pb.dma(qsb[:, hp * T:(hp + 1) * T], qfm[hp * 128:(hp + 1) * 128, :])
        pb.dma(ksb[:, hp * T:(hp + 1) * T], kfm[hp * 128:(hp + 1) * 128, :])
    pb.memset(Imask[:], 1.0)
    pb.affine_select(Imask[:], Imask[:], [[0, NH], [-1, C]], ALU.is_equal, 0.0,
                     channel_multiplier=1)
    pb.memset(ones64f[:], 1.0)
    pb.memset(onr1[:], 1.0)
    if gated:
        pb.dma(gts[:].rearrange("p (r f) -> p r f", f=48),
               gtb.rearrange("(r p) f -> p r f", p=128))
        pb.dma(mrow[:], mnt)
        pb.mm(psA[:, 0:512], onr1[:], mrow[0:1, 0:512])
        pb.mm(psA[:, 512:1024], onr1[:], mrow[0:1, 512:1024])
        pb.copy(mntb[:], psA[:])
        pb.memset(mSLn[:], 0.0)
        pb.affine_select(mSLn[:], mSLn[:], [[0, NH], [-1, C]], ALU.is_gt, -1e30,
                         channel_multiplier=1)
        pb.memset(mSUn[:], 0.0)
        pb.affine_select(mSUn[:], mSUn[:], [[0, NH], [1, C]], ALU.is_gt, -1e30,
                         channel_multiplier=-1)
        pb.memset(mSUi[:], 0.0)
        pb.affine_select(mSUi[:], mSUi[:], [[0, NH], [1, C]], ALU.is_ge, -1e30,
                         channel_multiplier=-1)
        pb.memset(triuI[:], 1.0)
        pb.affine_select(triuI[:], triuI[:], [[1, 64]], ALU.is_ge, 0.0,
                         channel_multiplier=-1)
        pb.memset(sLowS[:], 1.0)
        pb.affine_select(sLowS[:], sLowS[:], [[-1, 64]], ALU.is_gt, 0.0,
                         channel_multiplier=1)
    else:
        pb.dma(abr[:], abr_in)
        pb.act(abr[:], abr[:], AF.Sigmoid)
        pb.ts(abr[:], abr[:], 0.5, ALU.mult)
        pb.mm(psA[0:64, 0:32], onr1[:], abr[:])
        pb.copy(abbc[:], psA[0:64, 0:32])
        a3 = abbc[0:64, 0:16].unsqueeze(2).broadcast_to((64, 16, C))
        b3 = abbc[0:64, 16:32].unsqueeze(2).broadcast_to((64, 16, C))
        pb.memset(maskA_SL[:], 1.0)
        pb.affine_select(maskA_SL[:], maskA_SL[:], [[0, NH], [-1, C]], ALU.is_gt,
                         0.0, channel_multiplier=1)
        pb.memset(maskSU[:], 1.0)
        pb.affine_select(maskSU[:], maskSU[:], [[0, NH], [1, C]], ALU.is_gt, 0.0,
                         channel_multiplier=-1)
        pb.copy(maskA_SU[:], maskSU[:])
        pb.tt(maskA_SL[:].rearrange("p (h i) -> p h i", i=C),
              maskA_SL[:].rearrange("p (h i) -> p h i", i=C), a3, ALU.mult)
        pb.tt(maskA_SU[:].rearrange("p (h i) -> p h i", i=C),
              maskA_SU[:].rearrange("p (h i) -> p h i", i=C), a3, ALU.mult)

    def qslice(h, t0, w=C):
        hp, par = divmod(h, 2)
        return qsb[64 * par:64 * par + 64, hp * T + t0: hp * T + t0 + w]

    def kslice(h, t0, w=C):
        hp, par = divmod(h, 2)
        return ksb[64 * par:64 * par + 64, hp * T + t0: hp * T + t0 + w]

    bufs = ((qsb, True), (ksb, True)) if gated else ((ksb, False),)
    if not SCAN_PRE:
        bufs = ()
    for (buf, is_rsq) in bufs:
        for h in range(NH):
            hp, par = divmod(h, 2)
            for t0 in range(0, T, 512):
                sl = buf[64 * par:64 * par + 64, hp * T + t0: hp * T + t0 + 512]
                pb.act(sq[0:64, :], sl, AF.Square)
                pb.mm(psA[0:1, 0:512], ones64f[:, 0:1], sq[0:64, :])
                if is_rsq:
                    pb.act(rowq[0:1, :], psA[0:1, 0:512], AF.Sqrt,
                           bias=pb.constcol(1e-6, 1))
                else:
                    pb.act(rowq[0:1, :], psA[0:1, 0:512], AF.Sqrt)
                    pb.ts(rowq[0:1, :], rowq[0:1, :], 1e-12, ALU.max)
                pb.recip(rowq[0:1, :], rowq[0:1, :])
                pb.mm(psB[0:64, 0:512], onr1[:], rowq[0:1, :])
                pb.tt(sl, sl, psB[0:64, 0:512], ALU.mult)
    def load_kst_vst(c):
        pb.dma(kst[:], ktm[c * C:(c + 1) * C, :])
        pb.dma(vst[:], vtm[c * C:(c + 1) * C, :])
        pb.act(sq[0:64, 0:512], kst[:, 0:512], AF.Square)
        pb.reduce(red[0:64, 0:8],
                  sq[0:64, 0:512].rearrange("p (h d) -> p h d", d=64), ALU.add)
        pb.act(sq[0:64, 0:512], kst[:, 512:1024], AF.Square)
        pb.reduce(red[0:64, 8:16],
                  sq[0:64, 0:512].rearrange("p (h d) -> p h d", d=64), ALU.add)
        if gated:
            pb.act(red[0:64, :], red[0:64, :], AF.Sqrt, bias=pb.constcol(1e-6, 64))
        else:
            pb.act(red[0:64, :], red[0:64, :], AF.Sqrt)
            pb.ts(red[0:64, :], red[0:64, :], 1e-12, ALU.max)
        pb.recip(red[0:64, :], red[0:64, :])
        pb.tt(kst[:].rearrange("p (h d) -> p h d", d=64),
              kst[:].rearrange("p (h d) -> p h d", d=64),
              red[0:64, 0:16].unsqueeze(2).broadcast_to((64, 16, 64)), ALU.mult)

    pb.memset(Sf[:], 0.0, eng="vector")
    pb.copy(Sb0[:], Sf[:])
    b8 = None
    for c in range(min(NC2, SCAN_LIMIT)):
        t0 = c * C
        load_kst_vst(c)
        if gated:
            gr, gpar = divmod(t0, 128)
            pb.copy(gst[:], gts[64 * (gpar // 64):64 * (gpar // 64) + 64,
                                gr * 48:(gr + 1) * 48])
            b8 = bb[:, 0:16]
            pb.mm(psA[0:64, 0:16], triuI[:], gst[:, 0:16])
            pb.copy(bb[:], psA[0:64, 0:16])
            pb.mm(psA[0:64, 0:16], sLowS[:], gst[:, 0:16])
            pb.act(eb[:], bb[:], AF.Exp)
            pb.copy(blnb[:], psA[0:64, 0:16])
            pb.act(ed[:], blnb[:], AF.Exp)
            pb.tt(ebC[:], eb[:], ed[:], ALU.mult)
            pb.tt(blnb[:], bb[:], gst[:, 16:32], ALU.add)
            pb.tt(bpre[:], gst[:, 32:48], eb[:], ALU.mult)
        for h in range(NH):
            pb.mm(psA[:, h * 64:(h + 1) * 64], kslice(h, t0), kslice(h, t0))
        for h in range(NH):
            pb.mm(psB[:, h * 64:(h + 1) * 64], kslice(h, t0), qslice(h, t0))
        if gated:
            pb.tt(fdiag[:].rearrange("p (h i) -> p h i", i=C),
                  Imask[:].rearrange("p (h i) -> p h i", i=C),
                  bb[:].unsqueeze(2).broadcast_to((64, 16, C)), ALU.mult)
            pb.mm(psU[:, 0:512], ones64f[:], fdiag[:, 0:512])
            pb.mm(psU[:, 512:F], ones64f[:], fdiag[:, 512:F])
            pb.tt(fdiag[:].rearrange("p (h i) -> p h i", i=C),
                  Imask[:].rearrange("p (h i) -> p h i", i=C),
                  blnb[:].unsqueeze(2).broadcast_to((64, 16, C)), ALU.mult)
            pb.mm(psU[:, F:F + 512], ones64f[:], fdiag[:, 0:512])
            pb.mm(psU[:, F + 512:2 * F], ones64f[:], fdiag[:, 512:F])
            pb.tt(gmA[:].rearrange("p (h i) -> p h i", i=C),
                  blnb[:].unsqueeze(2).broadcast_to((64, 16, C)),
                  psU[:, 0:F].rearrange("p (h i) -> p h i", i=C), ALU.subtract)
            pb.tt(gmA[:], gmA[:], mSLn[:], ALU.add)
            pb.act(gmA[:], gmA[:], AF.Exp)
            pb.tt(Mb[:], gmA[:], psA[:], ALU.mult)
            pb.tt(gmA[:].rearrange("p (h i) -> p h i", i=C),
                  psU[:, F:2 * F].rearrange("p (h i) -> p h i", i=C),
                  bb[:].unsqueeze(2).broadcast_to((64, 16, C)), ALU.subtract)
            pb.tt(gmA[:], gmA[:], mSUn[:], ALU.add)
            pb.act(gmA[:], gmA[:], AF.Exp)
            pb.tt(MTb[:], gmA[:], psA[:], ALU.mult)
            pb.tt(gmA[:].rearrange("p (h i) -> p h i", i=C),
                  psU[:, 0:F].rearrange("p (h i) -> p h i", i=C),
                  bb[:].unsqueeze(2).broadcast_to((64, 16, C)), ALU.subtract)
            pb.tt(gmA[:], gmA[:], mSUi[:], ALU.add)
            pb.act(gmA[:], gmA[:], AF.Exp)
            pb.tt(M1T[:], gmA[:], psB[:], ALU.mult)
        else:
            pb.tt(Mb[:], psA[:], maskA_SL[:], ALU.mult)
            pb.tt(MTb[:], psA[:], maskA_SU[:], ALU.mult)
            pb.tt(M1T[:], psB[:], maskSU[:], ALU.mult)
        prev, prevT = Mb, MTb
        for l in range(4):
            for h in range(NH):
                sl = slice(h * 64, (h + 1) * 64)
                pb.mm(psA[:, sl], prevT[:, sl], prev[:, sl])
            pb.copy(POW[l][:], psA[:])
            for h in range(NH):
                sl = slice(h * 64, (h + 1) * 64)
                pb.mm(psA[:, sl], prev[:, sl], prevT[:, sl])
            pb.copy(POWT[l][:], psA[:])
            prev, prevT = POW[l], POWT[l]
        for h in range(NH):
            sl = slice(h * 64, (h + 1) * 64)
            pb.mm(psA[:, sl], POWT[3][:, sl], POW[3][:, sl])
        pb.copy(P32[:], psA[:])
        pb.tt(Xf[:], Imask[:], MTb[:], ALU.subtract)
        pb.copy(Xb[:], Xf[:])
        for pw in (POW[0], POW[1], POW[2], POW[3], P32):
            for h in range(NH):
                sl = slice(h * 64, (h + 1) * 64)
                pb.mm(psA[:, sl], pw[:, sl], Xb[:, sl])
            pb.tt(Xf[:], Xf[:], psA[:], ALU.add)
            pb.copy(Xb[:], Xf[:])
        if gated:
            pb.tt(Rall[:].rearrange("p (h z) -> p h z", z=128)[:, :, 0:64],
                  vst[:].rearrange("p (h z) -> p h z", z=64),
                  gst[:, 32:48].unsqueeze(2).broadcast_to((64, 16, C)), ALU.mult)
            pb.tt(Rall[:].rearrange("p (h z) -> p h z", z=128)[:, :, 64:128],
                  kst[:].rearrange("p (h z) -> p h z", z=64),
                  bpre[:].unsqueeze(2).broadcast_to((64, 16, C)), ALU.mult)
        else:
            pb.tt(Rall[:].rearrange("p (h z) -> p h z", z=128)[:, :, 0:64],
                  vst[:].rearrange("p (h z) -> p h z", z=64), b3, ALU.mult)
            pb.tt(Rall[:].rearrange("p (h z) -> p h z", z=128)[:, :, 64:128],
                  kst[:].rearrange("p (h z) -> p h z", z=64), a3, ALU.mult)
        for h in range(NH):
            pb.mm(psU[:, h * 128:(h + 1) * 128], Xb[:, h * 64:(h + 1) * 64],
                  Rall[:, h * 128:(h + 1) * 128])
        pb.copy(Ub[:], psU[:])
        U0 = lambda h: Ub[:, h * 128: h * 128 + 64]
        W0 = lambda h: Ub[:, h * 128 + 64: (h + 1) * 128]
        for h in range(NH):
            pb.mm(psA[:, h * 64:(h + 1) * 64], M1T[:, h * 64:(h + 1) * 64], U0(h))
        pb.copy(tmpF[:], psA[:])
        pb.dma(spY1[c * 64:(c + 1) * 64, :], tmpF[:])
        for h in range(NH):
            pb.mm(psB[:, h * 64:(h + 1) * 64], W0(h), M1T[:, h * 64:(h + 1) * 64])
        if gated:
            pb.ts(Mb[:], psB[:], -1.0, ALU.mult)
            pb.dma(spC[c * 64:(c + 1) * 64, :], Mb[:])
        else:
            for par in range(2):
                pb.tt(ghTb[64 * par:64 * par + 64, :].rearrange(
                          "p (hp z) -> p hp z", z=64),
                      qsb[64 * par:64 * par + 64, :].rearrange(
                          "p (hp s) -> p hp s", s=T)[:, :, t0:t0 + C],
                      psB[0:64, :].rearrange("p (h z) -> p h z", z=64)[:, par::2, :],
                      ALU.subtract)
            pb.dma(spC[c * 128:(c + 1) * 128, :], ghTb[:])
        if gated:
            pb.tt(dk[:].rearrange("p (h z) -> p h z", z=64),
                  kst[:].rearrange("p (h z) -> p h z", z=64),
                  ed[:].unsqueeze(2).broadcast_to((64, 16, C)), ALU.mult)
            kuse = dk
        else:
            kuse = kst
        for h in range(NH):
            pb.mm(psA[:, h * 64:(h + 1) * 64], kuse[:, h * 64:(h + 1) * 64], U0(h))
        pb.copy(tmpF[:], psA[:])
        pb.dma(spP[c * 64:(c + 1) * 64, :], tmpF[:])
        for h in range(NH):
            pb.mm(psB[:, h * 64:(h + 1) * 64], W0(h), kuse[:, h * 64:(h + 1) * 64])
        if gated:
            pb.tt(t2F[:].rearrange("p (h i) -> p h i", i=C),
                  Imask[:].rearrange("p (h i) -> p h i", i=C),
                  ebC[:].unsqueeze(2).broadcast_to((64, 16, C)), ALU.mult)
            pb.tt(Mb[:], t2F[:], psB[:], ALU.subtract)
            pb.dma(speb[c * 64:(c + 1) * 64, :], eb[:])
        else:
            pb.tt(Mb[:], Imask[:], psB[:], ALU.subtract)
        pb.dma(spR[c * 64:(c + 1) * 64, :], Mb[:])
        for h in range(NH):
            pb.mm(psB[:, h * 64:(h + 1) * 64], Mb[:, h * 64:(h + 1) * 64],
                  Sb0[:, h * 64:(h + 1) * 64])
        pb.tt(Sf[:], tmpF[:], psB[:], ALU.add)
        pb.copy(Sb0[:], Sf[:])
    pb.dma(sml.rearrange("one (p c) -> (one p) c", p=64), Sf[:])
    emit_ag(pb, smg, sml)

    pb.dma(Sf[:], smg[0:1, :].rearrange("one (p c) -> p (one c)", p=64))
    pb.ts(Sf[:], Sf[:], rc[0:64, 0:1], ALU.mult)
    pb.copy(Sb0[:], Sf[:])
    for h in range(NH):
        hp, par = divmod(h, 2)
        pb.copy(Sbp[64 * par:64 * par + 64, hp * 64:(hp + 1) * 64],
                Sf[:, h * 64:(h + 1) * 64])
    Y1sb = tmpF
    for c in range(min(NC2, SCAN_LIMIT) if SCAN_PASS2 else 0):
        t0 = c * C
        pb.dma(Y1sb[:], spY1[c * 64:(c + 1) * 64, :])
        if gated:
            pb.dma(Mb[:], spC[c * 64:(c + 1) * 64, :])
        else:
            pb.dma(ghTb[:], spC[c * 128:(c + 1) * 128, :])
        pb.dma(MTb[:], spR[c * 64:(c + 1) * 64, :])
        if gated:
            pb.dma(eb[:], speb[c * 64:(c + 1) * 64, :])
            for h in range(NH):
                pb.mm(psA[:, h * 64:(h + 1) * 64], Mb[:, h * 64:(h + 1) * 64],
                      Sb0[:, h * 64:(h + 1) * 64])
            for h in range(NH):
                hp, par = divmod(h, 2)
                pb.mm(psB[:, h * 64:(h + 1) * 64], qslice(h, t0),
                      Sbp[64 * par:64 * par + 64, hp * 64:(hp + 1) * 64])
            pb.tt(t2F[:], Y1sb[:], psA[:], ALU.add)
            for h in range(NH):
                sl = slice(h * 64, (h + 1) * 64)
                pb.ts(Xf[:, sl], psB[:, sl], eb[:, h:h + 1], ALU.mult)
            pb.tt(t2F[:], t2F[:], Xf[:], ALU.add)
            pb.ts(t2F[:], t2F[:], 0.125, ALU.mult)
            pb.act(sq[0:64, 0:512], t2F[:, 0:512], AF.Square)
            pb.reduce(red[0:64, 0:8],
                      sq[0:64, 0:512].rearrange("p (h d) -> p h d", d=64), ALU.add)
            pb.act(sq[0:64, 0:512], t2F[:, 512:1024], AF.Square)
            pb.reduce(red[0:64, 8:16],
                      sq[0:64, 0:512].rearrange("p (h d) -> p h d", d=64), ALU.add)
            pb.ts(red[0:64, :], red[0:64, :], 1.0 / 64.0, ALU.mult)
            pb.act(red[0:64, :], red[0:64, :], AF.Sqrt, bias=pb.constcol(1e-6, 64))
            pb.recip(red[0:64, :], red[0:64, :])
            pb.tt(t2F[:].rearrange("p (h d) -> p h d", d=64),
                  t2F[:].rearrange("p (h d) -> p h d", d=64),
                  red[0:64, 0:16].unsqueeze(2).broadcast_to((64, 16, 64)), ALU.mult)
            pb.tt(t2F[:], t2F[:], mntb[:], ALU.mult)
        else:
            for h in range(NH):
                hp, par = divmod(h, 2)
                pb.mm(psA[:, h * 64:(h + 1) * 64],
                      ghTb[64 * par:64 * par + 64, hp * 64:(hp + 1) * 64],
                      Sbp[64 * par:64 * par + 64, hp * 64:(hp + 1) * 64])
            pb.tt(t2F[:], Y1sb[:], psA[:], ALU.add)
        pb.dma(ot[t0:t0 + C, :], t2F[:])
        pb.dma(Y1sb[:], spP[c * 64:(c + 1) * 64, :])
        for h in range(NH):
            pb.mm(psB[:, h * 64:(h + 1) * 64], MTb[:, h * 64:(h + 1) * 64],
                  Sb0[:, h * 64:(h + 1) * 64])
        pb.tt(Sf[:], Y1sb[:], psB[:], ALU.add)
        pb.copy(Sb0[:], Sf[:])
        for h in range(NH):
            hp, par = divmod(h, 2)
            pb.copy(Sbp[64 * par:64 * par + 64, hp * 64:(hp + 1) * 64],
                    Sf[:, h * 64:(h + 1) * 64])
    return pb


def build_attn_b1_scan_local(pb):
    NH, C = 16, 64
    T = RPC
    NC2 = T // C
    F = NH * 64
    NHP = NH // 2
    qfm = pb.inp("qfm", [D, T], BF16)
    kfm = pb.inp("kfm", [D, T], BF16)
    ktm = pb.inp("ktm", [T, F], BF16)
    vtm = pb.inp("vtm", [T, F], BF16)
    rolec = pb.inp("rolec", [128, 1])
    sml = pb.inp("sml", [1, 64 * F])
    smg = pb.inp("smg", [2, 64 * F])
    ot = pb.out("ot", [T, F])

    qsb = pb.sb("qsb", [128, NHP * T], BF16)
    ksb = pb.sb("ksb", [128, NHP * T], BF16)
    kst = pb.sb("kst", [64, F], BF16)
    vst = pb.sb("vst", [64, F], BF16)
    Sf = pb.sb("Sf", [64, F], F32)
    Sb0 = pb.sb("Sb0", [64, F], BF16)
    Sbp = pb.sb("Sbp", [128, NHP * 64], BF16)
    ebuf = pb.sb("ebuf", [64, F], BF16)
    mask = pb.sb("maskt", [64, F], BF16)
    ostage = pb.sb("ostage", [64, F], F32)
    rc = pb.sb("rc", [128, 1], F32)
    obig = pb.sb("obig", [128, F], F32)
    ps_e = pb.ps("ps_e", [64, F], F32)
    ps_o = pb.ps("ps_o", [64, F], F32)
    ps_s = pb.ps("ps_s", [64, F], F32)
    ps_c = pb.ps("ps_c", [128, F], F32)

    pb.dma(rc[:], rolec)
    for hp in range(NHP):
        pb.dma(qsb[:, hp * T:(hp + 1) * T], qfm[hp * 128:(hp + 1) * 128, :])
        pb.dma(ksb[:, hp * T:(hp + 1) * T], kfm[hp * 128:(hp + 1) * 128, :])
    pb.memset(mask[:], 1.0)
    pb.affine_select(mask[:], mask[:], [[0, NH], [1, C]], ALU.is_ge, 0.0,
                     channel_multiplier=-1)
    pb.memset(Sf[:], 0.0, eng="vector")
    pb.copy(Sb0[:], Sf[:])
    for h in range(NH):
        hp, par = divmod(h, 2)
        pb.copy(Sbp[64 * par:64 * par + 64, hp * 64:(hp + 1) * 64],
                Sf[:, h * 64:(h + 1) * 64])

    def qslice(h, t0, w=C):
        hp, par = divmod(h, 2)
        return qsb[64 * par:64 * par + 64, hp * T + t0: hp * T + t0 + w]

    def kslice(h, t0, w=C):
        hp, par = divmod(h, 2)
        return ksb[64 * par:64 * par + 64, hp * T + t0: hp * T + t0 + w]

    for c in range(NC2):
        t0 = c * C
        pb.dma(kst[:], ktm[c * C:(c + 1) * C, :])
        pb.dma(vst[:], vtm[c * C:(c + 1) * C, :])
        for h in range(NH):
            pb.mm(ps_e[:, h * 64:(h + 1) * 64], kslice(h, t0), qslice(h, t0))
        pb.tt(ebuf[:], ps_e[:], mask[:], ALU.mult)
        for h in range(NH):
            hp, par = divmod(h, 2)
            pb.mm(ps_o[:, h * 64:(h + 1) * 64], ebuf[:, h * 64:(h + 1) * 64],
                  vst[0:64, h * 64:(h + 1) * 64], start=True, stop=False)
            pb.mm(ps_o[:, h * 64:(h + 1) * 64], qslice(h, t0),
                  Sbp[64 * par:64 * par + 64, hp * 64:(hp + 1) * 64],
                  start=False, stop=True)
        pb.copy(ostage[:], ps_o[:])
        pb.dma(ot[t0:t0 + C, :], ostage[:])
        for h in range(NH):
            pb.mm(ps_s[:, h * 64:(h + 1) * 64], kst[0:64, h * 64:(h + 1) * 64],
                  vst[0:64, h * 64:(h + 1) * 64])
        pb.tt(Sf[:], Sf[:], ps_s[:], ALU.add)
        pb.copy(Sb0[:], Sf[:])
        for h in range(NH):
            hp, par = divmod(h, 2)
            pb.copy(Sbp[64 * par:64 * par + 64, hp * 64:(hp + 1) * 64],
                    Sf[:, h * 64:(h + 1) * 64])
    pb.dma(sml.rearrange("one (p c) -> (one p) c", p=64), Sf[:])
    emit_ag(pb, smg, sml)
    pb.dma(Sf[:], smg[0:1, :].rearrange("one (p c) -> p (one c)", p=64))
    pb.ts(Sf[:], Sf[:], rc[0:64, 0:1], ALU.mult)
    for h in range(NH):
        hp, par = divmod(h, 2)
        pb.copy(Sbp[64 * par:64 * par + 64, hp * 64:(hp + 1) * 64],
                Sf[:, h * 64:(h + 1) * 64])
    for tr in range(T // 128):
        t0 = tr * 128
        for h in range(NH):
            hp, par = divmod(h, 2)
            pb.mm(ps_c[:, h * 64:(h + 1) * 64], qslice(h, t0, 128),
                  Sbp[64 * par:64 * par + 64, hp * 64:(hp + 1) * 64])
        pb.dma(obig[:], ot[t0:t0 + 128, :])
        pb.tt(obig[:], obig[:], ps_c[:], ALU.add)
        pb.dma(ot[t0:t0 + 128, :], obig[:])
    return pb


def emit_bnd_gather(pb, pref, x_ap, ncols):
    loc = pb.dram(pref + "_bl", [128, NCH * ncols])
    g = pb.dram(pref + "_bg", [2, 128 * NCH * ncols])
    with pb.stage(pref, {}):
        t = pb.sb("t", [128, NCH * ncols], F32)
        pb.dma(t[:].rearrange("p (dc c) -> p dc c", c=ncols),
               x_ap[:, RPC - ncols:RPC].rearrange("(dc p) c -> p dc c", p=128),
               allow_slow=True)
        pb.dma(loc[:], t[:], chain=True)
        emit_ag(pb, g, loc.rearrange("p c -> (p c)").unsqueeze(0))
    return g


def emit_attn_a(pb, E, xin, xout):
    qb = pb.dram("aa_Qb", [D, RPC], BF16)
    kf = pb.dram("aa_kf", [1, D * RPC], BF16)
    vb = pb.dram("aa_vb", [D, RPC], BF16)
    rq = pb.dram("aa_rq", [1, RPC])
    rk = pb.dram("aa_rk", [1, RPC])
    rkc = pb.dram("aa_rkc", [1, 128 * (RPC // 128)])
    run_stage(pb, "aaq", build_attn_a_qkv_f, {
        "xr": xin, "wqT": E["aa_wqT"], "wkT": E["aa_wkT"], "wvT": E["aa_wvT"],
        "bq": E["aa_bq"], "bk": E["aa_bk"], "bv": E["aa_bv"],
        "Qb": qb, "Kb": kf.rearrange("one (d t) -> (one d) t", t=RPC),
        "Vb": vb, "rq": rq, "rk": rk,
        "rkcol": rkc.rearrange("one (p c) -> (one p) c", p=128),
    })
    vt = pb.dram("aa_vt", [1, RPC * D], BF16)
    emit_transpose_mm(pb, "aaT",
                      vt.rearrange("one (t d) -> (one t) d", d=D),
                      vb, D, RPC, BF16)
    kg = pb.dram("aa_kg", [2, D * RPC], BF16)
    vtg = pb.dram("aa_vtg", [2, RPC * D], BF16)
    rkg = pb.dram("aa_rkg", [2, 128 * (RPC // 128)])
    emit_ag(pb, kg, kf)
    emit_ag(pb, vtg, vt)
    emit_ag(pb, rkg, rkc)
    run_stage(pb, "aav", build_attn_a_av_f, {
        "xr": xin, "Qb": qb, "Kg": kg,
        "Vtg": vtg.rearrange("h (t d) -> (h t) d", d=D),
        "rq": rq, "rkg": rkg, "woT": E["aa_woT"], "bo": E["aa_bo"], "xo": xout,
    })


def emit_ffn0(pb, E, xin, xout, tag):
    g = emit_bnd_gather(pb, tag + "bg", xin, 1)
    run_stage(pb, tag, build_l0_ffn, {
        "xr": xin, "xprev": g,
        "rolec": E["rolec"],
        "wrT": E["f0_wrT"], "wkT": E["f0_wkT"], "wvT": E["f0_wvT"],
        "mur": E["f0_mur"], "muk": E["f0_muk"], "xo": xout,
    })


def emit_ffn1(pb, E, xin, xout, tag):
    run_stage(pb, tag, build_l1_ffn, {
        "xr": xin, "w1": E["f1_w1"], "b1": E["f1_b1"],
        "w2": E["f1_w2"], "b2": E["f1_b2"], "xo": xout,
    })


def emit_gtb(pb, pref, gtb, srcs):
    with pb.stage(pref, {}):
        I16 = pb.sb("I16", [16, 16], F32)
        pb.memset(I16[:], 1.0)
        pb.affine_select(I16[:], I16[:], [[-1, 16]], ALU.is_equal, 0.0,
                         channel_multiplier=1)
        tile = pb.sb("tile", [16, 128], F32)
        buf = pb.sb("buf", [128, 48], F32)
        ps = pb.ps("ps", [128, 16], F32)
        for tr in range(RPC // 128):
            t0 = tr * 128
            for si, src in enumerate(srcs):
                pb.dma(tile[:], src[:, t0:t0 + 128])
                pb.mm(ps[:], tile[:], I16[:])
                pb.copy(buf[:, si * 16:(si + 1) * 16], ps[:])
            pb.dma(gtb[t0:t0 + 128, :], buf[:])


def emit_gdn(pb, E, xin, xout):
    g3 = emit_bnd_gather(pb, "gdb", xin, 3)
    y32 = pb.dram("gd_y32", [D, RPC])
    qc = pb.dram("gd_qc", [D, RPC], BF16)
    kc = pb.dram("gd_kc", [D, RPC], BF16)
    vc = pb.dram("gd_vc", [D, RPC], BF16)
    g16 = pb.dram("gd_g16", [16, RPC])
    lnb16 = pb.dram("gd_lnb16", [16, RPC])
    beta16 = pb.dram("gd_beta16", [16, RPC])
    gsig = pb.dram("gd_gsig", [D, RPC])
    run_stage(pb, "gp", build_gdn_prep, {
        "xr": xin, "xp3": g3, "rolec": E["rolec"], "cw": E["gd_cw"],
        "cbm": E["gd_cbm"], "cbt": E["gd_cbt"], "convw": E["gd_convw"],
        "y32": y32, "qc": qc, "kc": kc, "vc": vc, "g16": g16, "lnb16": lnb16,
        "beta16": beta16, "gsig": gsig,
    })
    zml = pb.dram("p2_zml", [1, 128 * 8])
    zmg = pb.dram("p2_zmg", [2, 128 * 8])
    sfm = pb.dram("p2_sfm", [D, RPC])
    run_stage(pb, "p2", build_p20_local, {
        "y32": y32, "dl": E["p20_dl"], "th": E["p20_th"], "rolec": E["rolec"],
        "zml": zml, "zmg": zmg, "sfm": sfm,
    })
    qm = pb.dram("gm_qm", [D, RPC], BF16)
    km = pb.dram("gm_km", [D, RPC], BF16)
    vm = pb.dram("gm_vm", [D, RPC], BF16)
    np20 = pb.dram("gm_np20", [D, RPC], BF16)
    run_stage(pb, "gm", build_gdn_mix, {
        "sfm": sfm, "qcf": qc, "kcf": kc, "vcf": vc, "p20cW": E["gm_p20cW"],
        "vnc": E["gm_vnc"], "rlc": E["gm_rlc"],
        "qm": qm, "km": km, "vm": vm, "np20": np20,
    })
    ktm = pb.dram("gs_ktm", [RPC, D], BF16)
    vtm = pb.dram("gs_vtm", [RPC, D], BF16)
    emit_transpose_mm(pb, "gsT1", ktm, km, D, RPC, BF16)
    emit_transpose_mm(pb, "gsT2", vtm, vm, D, RPC, BF16)
    gtb = pb.dram("gs_gtb", [RPC, 48])
    emit_gtb(pb, "gsG", gtb, (g16, lnb16, beta16))
    NC2 = RPC // 64
    sp = {
        "spY1": pb.dram("gs_spY1", [NC2 * 64, 1024]),
        "spP": pb.dram("gs_spP", [NC2 * 64, 1024]),
        "spC": pb.dram("gs_spC", [NC2 * 64, 1024], BF16),
        "spR": pb.dram("gs_spR", [NC2 * 64, 1024], BF16),
        "speb": pb.dram("gs_speb", [NC2 * 64, 16]),
    }
    sml = pb.dram("gs_sml", [1, 64 * 1024])
    smg = pb.dram("gs_smg", [2, 64 * 1024])
    nrt = pb.dram("gs_nrt", [RPC, D])
    run_stage(pb, "gs", lambda p: build_delta_scan_local(p, True), {
        "qfm": qm, "kfm": km, "ktm": ktm, "vtm": vtm, "rolec": E["rolec"],
        "sml": sml, "smg": smg, "ot": nrt, "gtb": gtb, "mnt": E["gs_mnt"], **sp,
    }, stream=False)
    nrf = pb.dram("go_nrf", [D, RPC])
    emit_transpose_mm(pb, "goT", nrf, nrt, RPC, D, F32)
    run_stage(pb, "go", build_gdn_out, {
        "nrf": nrf, "npf": np20, "gsig": gsig, "outw": E["go_outw"],
        "outb": E["go_outb"], "xr": xin, "xo": xout,
    })


def emit_attn_a1(pb, E, xin, xout):
    qt = pb.dram("a1_qt", [RPC, D], BF16)
    kt = pb.dram("a1_kt", [RPC, D], BF16)
    vt = pb.dram("a1_vt", [RPC, D], BF16)
    run_stage(pb, "a1q", build_attn_a1_qkv, {
        "xr": xin, "wq": E["a1_wq"], "wk": E["a1_wk"], "wv": E["a1_wv"],
        "qt": qt, "kt": kt, "vt": vt,
    })
    qfm = pb.dram("a1_qfm", [D, RPC], BF16)
    kfm = pb.dram("a1_kfm", [D, RPC], BF16)
    emit_transpose_mm(pb, "a1T1", qfm, qt, RPC, D, BF16)
    emit_transpose_mm(pb, "a1T2", kfm, kt, RPC, D, BF16)
    NC2 = RPC // 64
    sp = {
        "spY1": pb.dram("a1_spY1", [NC2 * 64, 1024]),
        "spP": pb.dram("a1_spP", [NC2 * 64, 1024]),
        "spC": pb.dram("a1_spC", [NC2 * 128, 512], BF16),
        "spR": pb.dram("a1_spR", [NC2 * 64, 1024], BF16),
    }
    sml = pb.dram("a1_sml", [1, 64 * 1024])
    smg = pb.dram("a1_smg", [2, 64 * 1024])
    ott = pb.dram("a1_ot", [RPC, D])
    run_stage(pb, "a1s", lambda p: build_delta_scan_local(p, False), {
        "qfm": qfm, "kfm": kfm, "ktm": kt, "vtm": vt, "rolec": E["rolec"],
        "sml": sml, "smg": smg, "ot": ott, "ab_raw": E["a1_ab"], **sp,
    }, stream=False)
    ofm = pb.dram("a1_ofm", [D, RPC])
    emit_transpose_mm(pb, "a1T3", ofm, ott, RPC, D, F32)
    run_stage(pb, "a1o", build_attn_a1_out, {
        "xr": xin, "ofm": ofm, "wo": E["a1_wo"], "xo": xout,
    })


def emit_attn_b1(pb, E, xin, xout):
    qr = pb.dram("b1_qr", [RPC, D], BF16)
    kr = pb.dram("b1_kr", [RPC, D], BF16)
    vt = pb.dram("b1_vt", [RPC, D], BF16)
    gate = pb.dram("b1_gate", [D, RPC])
    run_stage(pb, "b1q", build_attn_b1_qkv, {
        "xr": xin, "wq": E["b1_wq"], "wk": E["b1_wk"], "wv": E["b1_wv"],
        "wg1": E["b1_wg1"], "wg2": E["b1_wg2"], "poscol": E["b1_poscol"],
        "qr": qr, "kr": kr, "vt": vt, "gate": gate,
    })
    qfm = pb.dram("b1_qfm", [D, RPC], BF16)
    kfm = pb.dram("b1_kfm", [D, RPC], BF16)
    emit_transpose_mm(pb, "b1T1", qfm, qr, RPC, D, BF16)
    emit_transpose_mm(pb, "b1T2", kfm, kr, RPC, D, BF16)
    sml = pb.dram("b1_sml", [1, 64 * 1024])
    smg = pb.dram("b1_smg", [2, 64 * 1024])
    ott = pb.dram("b1_ot", [RPC, D])
    run_stage(pb, "b1s", build_attn_b1_scan_local, {
        "qfm": qfm, "kfm": kfm, "ktm": kr, "vtm": vt, "rolec": E["rolec"],
        "sml": sml, "smg": smg, "ot": ott,
    }, stream=False)
    ofm = pb.dram("b1_ofm", [D, RPC])
    emit_transpose_mm(pb, "b1T3", ofm, ott, RPC, D, F32)
    run_stage(pb, "b1o", build_attn_b1_out, {
        "xr": xin, "ofm": ofm, "gfm": gate, "wo": E["b1_wo"],
        "gam": E["b1_gam"], "bet": E["b1_bet"], "xo": xout,
    })


SUBLAYERS = ["attn_a", "ffn0a", "gdn", "ffn0b", "attn_a1", "ffn1a", "attn_b1",
             "ffn1b"]
FUSE_K = 2


def fused_prefix(X, inputs, K):
    ii = {k: np.asarray(v) for k, v in inputs.items()}
    key = f"fused_{K}"
    if key not in _progs:
        pb = PB(key)
        E = {}
        def ei(name, shape, dt=F32):
            E[name] = pb.inp(name, shape, dt)
        ei("x0", [D, RPC])
        ei("rolec", [128, 1])
        if K > 0:
            for nm in ("wqT", "wkT", "wvT", "woT"):
                ei("aa_" + nm, [D, D])
            for nm in ("bq", "bk", "bv", "bo"):
                ei("aa_" + nm, [128, NCH])
        if K > 1:
            for nm in ("wrT", "wkT", "wvT"):
                ei("f0_" + nm, [D, D])
            for nm in ("mur", "muk"):
                ei("f0_" + nm, [128, NCH])
        if K > 2:
            ei("gd_cw", [D, CTRLW]); ei("gd_cbm", [128, 32]); ei("gd_cbt", [32, 1])
            ei("gd_convw", [128, NCH * 12])
            ei("p20_dl", [128, 4]); ei("p20_th", [128, 4])
            ei("gm_p20cW", [D, 3 * D]); ei("gm_vnc", [128, NCH]); ei("gm_rlc", [128, NCH])
            ei("gs_mnt", [1, 1024])
            ei("go_outw", [2 * D, D]); ei("go_outb", [128, NCH])
        if K > 4:
            for nm in ("wq", "wk", "wv", "wo"):
                ei("a1_" + nm, [D, D])
            ei("a1_ab", [1, 32])
        if K > 5:
            ei("f1_w1", [D, 4096]); ei("f1_b1", [128, 32])
            ei("f1_w2", [4096, D]); ei("f1_b2", [128, NCH])
        if K > 6:
            for nm in ("wq", "wk", "wv", "wo"):
                ei("b1_" + nm, [D, D])
            ei("b1_wg1", [D, 64]); ei("b1_wg2", [64, D])
            ei("b1_poscol", [128, RPC // 128])
            ei("b1_gam", [128, NCH]); ei("b1_bet", [128, NCH])
        xo_ext = pb.out("XO", [D, RPC])
        xs = [pb.dram(f"xi{i}", [D, RPC]) for i in range(K)]
        chain = [E["x0"]] + xs
        emitters = {
            "attn_a": lambda i: emit_attn_a(pb, E, chain[i], chain[i + 1]),
            "ffn0a": lambda i: emit_ffn0(pb, E, chain[i], chain[i + 1], "f0a"),
            "ffn0b": lambda i: emit_ffn0(pb, E, chain[i], chain[i + 1], "f0b"),
            "gdn": lambda i: emit_gdn(pb, E, chain[i], chain[i + 1]),
            "attn_a1": lambda i: emit_attn_a1(pb, E, chain[i], chain[i + 1]),
            "attn_b1": lambda i: emit_attn_b1(pb, E, chain[i], chain[i + 1]),
            "ffn1a": lambda i: emit_ffn1(pb, E, chain[i], chain[i + 1], "f1a"),
            "ffn1b": lambda i: emit_ffn1(pb, E, chain[i], chain[i + 1], "f1b"),
        }
        for i in range(K):
            emitters[SUBLAYERS[i]](i)
        pb.dma(xo_ext, chain[K][:])
        _progs[key] = pb.finish()

    wrT = np.ascontiguousarray(ii["L0_ffn_W_r"].T)
    wkT0 = np.ascontiguousarray(ii["L0_ffn_W_k"].T)
    wvT0 = np.ascontiguousarray(ii["L0_ffn_W_v"].T)
    in_maps = []
    for c in range(NCORES):
        b, h = c // 2, c % 2
        m = {
            "x0": np.ascontiguousarray(X[b, h * RPC:(h + 1) * RPC, :].T),
            "rolec": np.full((128, 1), float(h), np.float32),
        }
        if K > 0:
            m.update({
                "aa_wqT": np.ascontiguousarray(ii["L0_attn_a_Wq"].T),
                "aa_wkT": np.ascontiguousarray(ii["L0_attn_a_Wk"].T),
                "aa_wvT": np.ascontiguousarray(ii["L0_attn_a_Wv"].T),
                "aa_woT": np.ascontiguousarray(ii["L0_attn_a_Wo"].T),
                "aa_bq": vec_cols_host(ii["L0_attn_a_bq"]),
                "aa_bk": vec_cols_host(ii["L0_attn_a_bk"]),
                "aa_bv": vec_cols_host(ii["L0_attn_a_bv"]),
                "aa_bo": vec_cols_host(ii["L0_attn_a_bo"]),
            })
        if K > 1:
            m.update({"f0_wrT": wrT, "f0_wkT": wkT0, "f0_wvT": wvT0,
                      "f0_mur": vec_cols_host(ii["L0_ffn_mu_r"]),
                      "f0_muk": vec_cols_host(ii["L0_ffn_mu_k"])})
        if K > 2:
            cw = ii["L0_attn_b_ctrl_W"]; cb = ii["L0_attn_b_ctrl_b"]
            perm = np.concatenate([np.arange(0, 3 * D),
                                   np.arange(3 * D + 32, 3 * D + 32 + D),
                                   np.arange(3 * D, 3 * D + 32)])
            convw = np.zeros((128, NCH * 12), np.float32)
            for wi, wname in enumerate(["qconv_w", "kconv_w", "vconv_w"]):
                cwv = ii[f"L0_attn_b_{wname}"]
                for fch in range(NCH):
                    for tap in range(4):
                        convw[:, wi * NCH * 4 + fch * 4 + tap] =                             cwv[fch * 128:(fch + 1) * 128, tap]
            cbr = np.ascontiguousarray(cb[perm])
            m.update({
                "gd_cw": np.ascontiguousarray(cw[:, perm]),
                "gd_cbm": vec_cols_host(cbr[:4096]),
                "gd_cbt": np.ascontiguousarray(cbr[4096:].reshape(32, 1)),
                "gd_convw": convw,
                "p20_dl": np.ascontiguousarray(
                    ii["L0_attn_b_p20_decay_log"].reshape(4, 128).T),
                "p20_th": np.ascontiguousarray(
                    ii["L0_attn_b_p20_theta"].reshape(4, 128).T),
                "gm_p20cW": ii["L0_attn_b_p20c_W"],
                "gm_vnc": vec_cols_host(ii["L0_attn_b_vnorm_scale"]),
                "gm_rlc": vec_cols_host(ii["L0_attn_b_ramp_logit"]),
                "gs_mnt": np.ascontiguousarray(
                    np.tile(ii["L0_attn_b_mnorm_scale"], 16).reshape(1, 1024).astype(np.float32)),
                "go_outw": ii["L0_attn_b_out_W"],
                "go_outb": vec_cols_host(ii["L0_attn_b_out_b"]),
            })
        if K > 4:
            m.update({"a1_wq": ii["L1_attn_a_Wq"], "a1_wk": ii["L1_attn_a_Wk"],
                      "a1_wv": ii["L1_attn_a_Wv"], "a1_wo": ii["L1_attn_a_Wo"],
                      "a1_ab": np.concatenate(
                          [ii["L1_attn_a_alpha_raw"].reshape(16),
                           ii["L1_attn_a_beta_raw"].reshape(16)]).reshape(1, 32).astype(np.float32)})
        if K > 5:
            m.update({"f1_w1": ii["L1_ffn_w1"], "f1_b1": vec_cols_host(ii["L1_ffn_b1"]),
                      "f1_w2": ii["L1_ffn_w2"], "f1_b2": vec_cols_host(ii["L1_ffn_b2"])})
        if K > 6:
            pos = (h * RPC + np.arange(RPC, dtype=np.float32)).reshape(RPC // 128, 128).T
            m.update({"b1_wq": ii["L1_attn_b_Wq"], "b1_wk": ii["L1_attn_b_Wk"],
                      "b1_wv": ii["L1_attn_b_Wv"], "b1_wo": ii["L1_attn_b_Wo"],
                      "b1_wg1": ii["L1_attn_b_Wg1"], "b1_wg2": ii["L1_attn_b_Wg2"],
                      "b1_poscol": np.ascontiguousarray(pos),
                      "b1_gam": vec_cols_host(ii["L1_attn_b_gamma"]),
                      "b1_bet": vec_cols_host(ii["L1_attn_b_beta"])})
        in_maps.append(m)
    res = run_bass_kernel_spmd(_progs[key], in_maps, core_ids=list(range(NCORES)))
    Xn = np.empty_like(X)
    for c in range(NCORES):
        b, h = c // 2, c % 2
        Xn[b, h * RPC:(h + 1) * RPC, :] = res.results[c]["XO"].T
    return Xn


def kernel(**inputs):
    X = np.asarray(inputs["L0_attn_a_x"], dtype=np.float32)
    rest = [sub_l0_attn_a, sub_l0_ffn, sub_l0_attn_b, sub_l0_ffn,
            sub_l1_attn_a, sub_l1_ffn, sub_l1_attn_b, sub_l1_ffn]
    if FUSE_K > 0:
        X = fused_prefix(X, inputs, FUSE_K)
    for fn in rest[FUSE_K:]:
        X = fn(X, inputs)
    return X
